# version 02
# baseline (speedup 1.0000x reference)
import numpy as np
import contextlib
import ml_dtypes
import concourse.bass as bass
import concourse.mybir as mybir
from concourse.bass_utils import run_bass_kernel_spmd

F32 = mybir.dt.float32
BF16 = mybir.dt.bfloat16
AF = mybir.ActivationFunctionType
ALU = mybir.AluOpType
AX = mybir.AxisListType

T = 4096
D = 1024
NT = T // 128
L0_SCALE = 1.0 / float(np.sqrt(np.float32(D * 2048), dtype=np.float32))
LN_EPS = 1e-5
BF = np.dtype(ml_dtypes.bfloat16)


class Sem:
    def __init__(self, name):
        self.name = name
        self.n = 0
        self.h = None


class Prog:
    ENGINES = ("sync", "tensor", "vector", "scalar", "gpsimd")

    def __init__(self, nc):
        self.nc = nc
        self.q = {e: [] for e in self.ENGINES}
        self.last = {e: None for e in self.ENGINES}
        self.sems = {}
        for nm in ("d_sync", "d_gp", "d_sc", "d_ve",
                   "x_ve", "x_sc", "x_gp", "x_te", "c_a", "c_b", "c_c", "c_d", "c_e", "c_f"):
            self.sems[nm] = Sem(nm)

    def sem(self, nm):
        return self.sems[nm]

    def emit(self, e, fn, inc=None, real=True):
        item = {"fn": fn, "inc": None}
        if inc is not None:
            sem, k = inc
            sem.n += k
            item["fn"] = (lambda eng, f=fn, s=sem, k=k: f(eng).then_inc(s.h, k))
            item["inc"] = (sem, sem.n)
        self.q[e].append(item)
        if real:
            self.last[e] = item
        return item

    def wait(self, e, sem, val=None):
        v = sem.n if val is None else val
        if v <= 0:
            return
        self.q[e].append({"fn": (lambda eng, s=sem, v=v: eng.wait_ge(s.h, v)), "inc": None})

    def dma(self, queue, dst, src, sem=None, **kw):
        s = sem if sem is not None else self.sems["d_" + ("gp" if queue == "gpsimd" else
                                                          "sc" if queue == "scalar" else
                                                          "ve" if queue == "vector" else "sync")]
        self.emit(queue, (lambda eng, d=dst, x=src, kw=kw: eng.dma_start(out=d, in_=x, **kw)),
                  inc=(s, 16))
        return s.n

    def fence(self, e):
        li = self.last[e]
        if li is None:
            return
        if li["inc"] is None:
            sname = {"vector": "x_ve", "scalar": "x_sc", "gpsimd": "x_gp", "tensor": "x_te"}.get(e)
            if sname is None:
                return
            sem = self.sems[sname]
            sem.n += 1
            f_old = li["fn"]
            li["fn"] = (lambda eng, f=f_old, s=sem: f(eng).then_inc(s.h, 1))
            li["inc"] = (sem, sem.n)
        sem, val = li["inc"]
        self.wait(e, sem, val)

    def mark(self, e):
        li = self.last[e]
        if li is None:
            return None
        if li["inc"] is None:
            sname = {"vector": "x_ve", "scalar": "x_sc", "gpsimd": "x_gp", "tensor": "x_te"}[e]
            sem = self.sems[sname]
            sem.n += 1
            f_old = li["fn"]
            li["fn"] = (lambda eng, f=f_old, s=sem: f(eng).then_inc(s.h, 1))
            li["inc"] = (sem, sem.n)
        return li["inc"]

    def barrier(self):
        marks = []
        for e in ("tensor", "vector", "scalar", "gpsimd"):
            m = self.mark(e)
            if m is not None:
                marks.append(m)
        for nm in ("d_sync", "d_gp", "d_sc", "d_ve"):
            s = self.sems[nm]
            if s.n > 0:
                marks.append((s, s.n))
        for e in self.ENGINES:
            for sem, val in marks:
                li = self.last[e]
                if li is not None and li["inc"] == (sem, val):
                    continue
                self.wait(e, sem, val)

    def run(self, block):
        for e in self.ENGINES:
            items = self.q[e]

            def mk(engine_items):
                def body(eng):
                    for it in engine_items:
                        it["fn"](eng)
                return body
            getattr(block, e)(mk(items))


def mmgroup(P, psum_ap, pairs, inc=None):
    n = len(pairs)
    for i, (lt, rh) in enumerate(pairs):
        fn = (lambda t, o=psum_ap, l=lt, r=rh, st=(i == 0), sp=(i == n - 1):
              t.matmul(o, l, r, start=st, stop=sp))
        P.emit("tensor", fn, inc=(inc if i == n - 1 else None))



def host_constants():
    HEAD_DIM = 64
    THETA = 10000.0
    inv = 1.0 / THETA ** (np.arange(0, HEAD_DIM, 2, dtype=np.float32) / HEAD_DIM)
    ang = np.arange(T, dtype=np.float32)[:, None] * inv[None, :]
    return np.cos(ang).astype(np.float32), np.sin(ang).astype(np.float32)



def alloc_common(P, es):
    nc = P.nc
    g = {}
    g["ident_b"] = es.enter_context(nc.sbuf_tensor("g_ident_b", [128, 128], BF16))[:]
    g["ident_f"] = es.enter_context(nc.sbuf_tensor("g_ident_f", [128, 128], F32))[:]
    g["ones1"] = es.enter_context(nc.sbuf_tensor("g_ones1", [1, 128], F32))[:]
    g["consts"] = es.enter_context(nc.sbuf_tensor("g_consts", [128, 8], F32))[:]
    def setup(gp, g=g):
        gp.memset(g["ident_b"][:], 0.0)
        gp.affine_select(out=g["ident_b"][:], in_=g["ident_b"][:], compare_op=ALU.not_equal,
                         fill=1.0, base=0, pattern=[[-1, 128]], channel_multiplier=1)
        gp.memset(g["ident_f"][:], 0.0)
        gp.affine_select(out=g["ident_f"][:], in_=g["ident_f"][:], compare_op=ALU.not_equal,
                         fill=1.0, base=0, pattern=[[-1, 128]], channel_multiplier=1)
        gp.memset(g["ones1"][:], 1.0)
        gp.memset(g["consts"][:, 0:1], LN_EPS)
        gp.memset(g["consts"][:, 1:2], 1e-12)
        gp.memset(g["consts"][:, 2:3], 1e-9)
        gp.memset(g["consts"][:, 3:4], 1.0)
        return gp.memset(g["consts"][:, 4:5], 1.0 / 1024.0)
    P.emit("gpsimd", setup)
    P.mark("gpsimd")
    P.barrier()
    return g


def emit_ln(P, xt, xn_out, tmp, stats, sq=None, out2=None, rho_out=None, g=None):
    nm = stats[:, 0:1]
    ssq = stats[:, 1:2]
    rstd = stats[:, 2:3]
    t3 = stats[:, 3:4]
    P.emit("vector", lambda v, o=nm, x=xt: v.tensor_reduce(o, x, AX.X, ALU.add, negate=True))
    P.fence("vector")
    P.emit("vector", lambda v, o=nm: v.tensor_scalar(out=o, in0=o, scalar1=1.0 / 1024.0,
                                                     scalar2=None, op0=ALU.mult))
    P.fence("vector")
    P.emit("vector", lambda v, o=tmp, x=xt, s=nm: v.tensor_scalar(out=o, in0=x, scalar1=s,
                                                                  scalar2=None, op0=ALU.add))
    P.fence("vector")
    mv = P.mark("vector")
    P.wait("scalar", *mv)
    P.emit("scalar", lambda s, o=ssq, x=tmp, d=sq: s.activation(d, x, AF.Square, accum_out=o))
    P.fence("scalar")
    P.emit("scalar", lambda s, o=rstd, x=ssq, c=g["consts"]: s.activation(o, x, AF.Sqrt,
                                                                          bias=c[:, 0:1], scale=1.0 / 1024.0))
    ms = P.mark("scalar")
    P.wait("vector", *ms)
    if rho_out is not None:
        P.emit("scalar", lambda s, o=t3, x=ssq: s.activation(o, x, AF.Sqrt))
        P.fence("scalar")
    ms2 = P.mark("scalar")
    P.wait("vector", *ms2)
    P.emit("vector", lambda v, o=rstd: v.reciprocal(o, o))
    P.fence("vector")
    if rho_out is not None:
        P.emit("vector", lambda v, o=t3, r=rstd: v.tensor_tensor(o, o, r, ALU.mult))
        P.fence("vector")
        P.emit("vector", lambda v, o=t3, c=g["consts"]: v.tensor_scalar(out=o, in0=o, scalar1=c[:, 1:2],
                                                                        scalar2=None, op0=ALU.max))
        P.fence("vector")
        P.emit("vector", lambda v, o=rho_out, x=t3: v.reciprocal(o, x))
        P.fence("vector")
        P.emit("vector", lambda v, o=rho_out: v.tensor_scalar(out=o, in0=o, scalar1=32.0,
                                                              scalar2=None, op0=ALU.mult))
        P.fence("vector")
    P.emit("vector", lambda v, o=xn_out, x=tmp, r=rstd: v.tensor_scalar(out=o, in0=x, scalar1=r,
                                                                        scalar2=None, op0=ALU.mult))
    if out2 is not None:
        P.emit("vector", lambda v, o=out2, x=tmp, r=rstd: v.tensor_scalar(out=o, in0=x, scalar1=r,
                                                                          scalar2=None, op0=ALU.mult))


def transpose_tile(P, psT, src_bf, dst, ident, w=128):
    P.emit("tensor", lambda t, o=psT, x=src_bf, i=ident: t.transpose(o, x, i), inc=(P.sem("c_a"), 1))
    P.wait("vector", P.sem("c_a"))
    P.emit("vector", lambda v, o=dst, x=psT: v.tensor_copy(o, x))


def emit_p1(P, g, io, es):
    nc = P.nc
    sb = lambda n, s, d=F32: es.enter_context(nc.sbuf_tensor(n, s, d))[:]
    pst = lambda n, s, d=F32: es.enter_context(nc.psum_tensor(n, s, d))[:]

    wnames = ["W1q", "W2q", "W1k", "W2k", "W1v", "W2v"]
    wb = {n: sb("p1_" + n, [128, 8, 1024], BF16) for n in wnames}
    wstage = sb("p1_wstage", [128, 8, 1024], F32)
    bia = sb("p1_bias", [128, 8, 8], F32)
    bvb = sb("p1_bvb", [128, 1024], F32)
    gxb = sb("p1_gxb", [128, 1024], F32)
    rowbuf = sb("p1_rowbuf", [1, 1024], F32)

    for i, n in enumerate(wnames):
        ld = P.dma("sync", wstage[:], io[n].rearrange("(k p) n -> p k n", p=128))
        P.wait("vector", P.sem("d_sync"), ld)
        if i > 0:
            P.fence("vector")
        P.emit("vector", lambda v, o=wb[n], x=wstage: v.tensor_copy(o[:], x[:]))
        mv = P.mark("vector")
        P.wait("sync", *mv)
    for i, bn in enumerate(["b1q", "b2q", "b1k", "b2k", "b1v", "b2v"]):
        for mc in range(8):
            P.dma("gpsimd", bia[:, i, mc:mc + 1], io[bn][mc * 128:(mc + 1) * 128].unsqueeze(1))
    P.wait("vector", P.sem("d_gp"))
    P.emit("vector", lambda v, o=bia: v.tensor_scalar(out=o[:, 1, :], in0=o[:, 1, :],
                                                      scalar1=L0_SCALE, scalar2=None, op0=ALU.mult))
    P.fence("vector")
    P.dma("gpsimd", rowbuf[0:1, :], io["b2v"][None, :])
    ps_b = pst("p1_psb", [128, 1024], F32)
    P.wait("tensor", P.sem("d_gp"))
    mmgroup(P, ps_b[:, 0:512], [(g["ones1"][:], rowbuf[0:1, 0:512])])
    mmgroup(P, ps_b[:, 512:1024], [(g["ones1"][:], rowbuf[0:1, 512:1024])], inc=(P.sem("c_a"), 1))
    P.wait("vector", P.sem("c_a"))
    P.emit("vector", lambda v, o=bvb, x=ps_b: v.tensor_copy(o[:], x[:]), inc=(P.sem("c_b"), 1))
    P.wait("gpsimd", P.sem("c_a"))
    ld = P.dma("gpsimd", rowbuf[0:1, :], io["gamma_x"][None, :])
    P.wait("tensor", P.sem("d_gp"), ld)
    P.wait("tensor", P.sem("c_b"))
    mmgroup(P, ps_b[:, 0:512], [(g["ones1"][:], rowbuf[0:1, 0:512])])
    mmgroup(P, ps_b[:, 512:1024], [(g["ones1"][:], rowbuf[0:1, 512:1024])], inc=(P.sem("c_a"), 1))
    P.wait("vector", P.sem("c_a"))
    P.emit("vector", lambda v, o=gxb, x=ps_b: v.tensor_copy(o[:], x[:]))

    xt = sb("p1_xt", [128, 2, 1024], F32)
    tmp = sb("p1_tmp", [128, 1024], F32)
    sqscr = sb("p1_sqscr", [128, 1024], F32)
    stats = sb("p1_stats", [128, 4], F32)
    rho = sb("p1_rho", [128, 1], F32)
    xn_bf = sb("p1_xnbf", [128, 1024], BF16)
    xnr_bf = sb("p1_xnrbf", [128, 1024], BF16)
    xnf = sb("p1_xnf", [128, 8, 512], BF16)
    xnrf = sb("p1_xnrf", [128, 8, 512], BF16)
    hq = sb("p1_hq", [128, 8, 512], BF16)
    qf = sb("p1_qf", [128, 8, 512], BF16)
    vtm = sb("p1_vtm", [128, 1024], BF16)
    psT = pst("p1_psT", [128, 2, 128], BF16)
    psH = pst("p1_psH", [128, 2, 512], F32)
    psV = pst("p1_psV", [128, 1024], F32)

    P.barrier()

    for ch in range(8):
        for tt4 in range(4):
            tt = ch * 4 + tt4
            buf = tt % 2
            if tt >= 2:
                P.fence("vector")
                mv = P.mark("vector")
                P.wait("sync", *mv)
            ld = P.dma("sync", xt[:, buf, :], io["x0"][tt * 128:(tt + 1) * 128, :])
            P.wait("vector", P.sem("d_sync"), ld)
            emit_ln(P, xt[:, buf, :], xn_bf, tmp, stats, sq=sqscr, rho_out=rho[:, 0:1], g=g)
            P.fence("vector")
            P.emit("vector", lambda v, o=xnr_bf, x=xn_bf, r=rho, w=gxb:
                   v.scalar_tensor_tensor(out=o, in0=x, scalar=r[:, 0:1], in1=w, op0=ALU.mult, op1=ALU.mult))
            P.fence("vector")
            mv = P.mark("vector")
            P.wait("tensor", *mv)
            P.wait("sync", *mv)
            P.dma("sync", io["xnr_tm"][tt * 128:(tt + 1) * 128, :], xnr_bf[:])
            P.wait("vector", P.sem("d_sync"))
            for k in range(8):
                pb = k % 2
                if k >= 2:
                    P.wait("tensor", P.sem("c_b"))
                P.emit("tensor", lambda t, o=psT[:, k % 2, :], x=xn_bf[:, k * 128:(k + 1) * 128],
                       i=g["ident_b"][:]: t.transpose(o, x, i), inc=(P.sem("c_a"), 1))
                P.wait("vector", P.sem("c_a"))
                P.emit("vector", lambda v, o=xnf[:, k, tt4 * 128:(tt4 + 1) * 128], x=psT[:, k % 2, :]:
                       v.tensor_copy(o, x), inc=(P.sem("c_b"), 1))
            for k in range(8):
                P.wait("tensor", P.sem("c_b"))
                P.emit("tensor", lambda t, o=psT[:, k % 2, :], x=xnr_bf[:, k * 128:(k + 1) * 128],
                       i=g["ident_b"][:]: t.transpose(o, x, i), inc=(P.sem("c_a"), 1))
                P.wait("vector", P.sem("c_a"))
                P.emit("vector", lambda v, o=xnrf[:, k, tt4 * 128:(tt4 + 1) * 128], x=psT[:, k % 2, :]:
                       v.tensor_copy(o, x), inc=(P.sem("c_b"), 1))
        mv = P.mark("vector")
        P.wait("sync", *mv)
        P.dma("sync", io["xnr_fm"].rearrange("(k p) n -> p k n", p=128)[:, :, ch * 512:(ch + 1) * 512],
              xnrf[:])
        i0 = ch * 512
        for proj, (w1n, w2n, b1c, b2c) in enumerate([("W1q", "W2q", 0, 1), ("W1k", "W2k", 2, 3),
                                                     ("W1v", "W2v", 4, 5)]):
            for mc in range(8):
                pb = mc % 2
                if mc >= 2:
                    P.wait("tensor", P.sem("c_c"))
                mmgroup(P, psH[:, pb, :],
                        [(wb[w1n][:, k, mc * 128:(mc + 1) * 128], xnf[:, k, :]) for k in range(8)],
                        inc=(P.sem("c_a"), 1))
                P.wait("scalar", P.sem("c_a"))
                P.emit("scalar", lambda s, o=hq[:, mc, :], x=psH[:, pb, :], b=bia[:, b1c, mc:mc + 1]:
                       s.activation(o, x, AF.Gelu, bias=b), inc=(P.sem("c_c"), 1))
            if proj < 2:
                scl = L0_SCALE if proj == 0 else 1.0
                for mc in range(8):
                    pb = mc % 2
                    if mc >= 2:
                        P.wait("tensor", P.sem("c_d"))
                    P.wait("tensor", P.sem("c_c"))
                    mmgroup(P, psH[:, pb, :],
                            [(wb[w2n][:, k, mc * 128:(mc + 1) * 128], hq[:, k, :]) for k in range(8)],
                            inc=(P.sem("c_a"), 1))
                    P.wait("scalar", P.sem("c_a"))
                    P.emit("scalar", lambda s, o=qf[:, mc, :], x=psH[:, pb, :], b=bia[:, b2c, mc:mc + 1], sc=scl:
                           s.activation(o, x, AF.Identity, bias=b, scale=sc), inc=(P.sem("c_d"), 1))
                ms = P.mark("scalar")
                P.wait("sync", *ms)
                dst = io["qf0"] if proj == 0 else io["kf0"]
                P.dma("sync", dst.rearrange("(k p) n -> p k n", p=128)[:, :, i0:i0 + 512], qf[:])
                P.wait("scalar", P.sem("d_sync"))
            else:
                for tt4 in range(4):
                    P.wait("tensor", P.sem("c_c"))
                    for half in range(2):
                        mmgroup(P, psV[:, half * 512:(half + 1) * 512],
                                [(hq[:, k, tt4 * 128:(tt4 + 1) * 128],
                                  wb[w2n][:, k, half * 512:(half + 1) * 512]) for k in range(8)],
                                inc=(P.sem("c_a"), 1) if half == 1 else None)
                    P.wait("vector", P.sem("c_a"))
                    P.emit("vector", lambda v, o=vtm, x=psV, b=bvb: v.tensor_add(o[:], x[:], b[:]),
                           inc=(P.sem("c_b"), 1))
                    mv = P.mark("vector")
                    P.wait("sync", *mv)
                    tt = ch * 4 + tt4
                    P.dma("sync", io["v0"][tt * 128:(tt + 1) * 128, :], vtm[:])
                    P.wait("tensor", P.sem("d_sync"))
                    P.wait("vector", P.sem("d_sync"))
    P.barrier()




def emit_p2(P, g, io, es):
    nc = P.nc
    sb = lambda n, s, d=F32: es.enter_context(nc.sbuf_tensor(n, s, d))[:]
    pst = lambda n, s, d=F32: es.enter_context(nc.psum_tensor(n, s, d))[:]

    k_res = sb("p2_k", [128, 8, T], BF16)
    v_res = sb("p2_v", [128, NT, D], BF16)
    qc = sb("p2_q", [128, 2, 8, 512], BF16)
    sc_sb = sb("p2_sc", [128, NT, 512], BF16)
    s0pre = sb("p2_s0pre", [128, 8, 512], BF16)
    psc = pst("p2_psc", [128, 2, 512], F32)
    pso = pst("p2_pso", [128, 2, 512], F32)

    P.dma("sync", k_res, io["kf0"].rearrange("(k p) n -> p k n", p=128))
    P.dma("sync", v_res, io["v0"].rearrange("(j p) d -> p j d", p=128))

    for ib in range(8):
        i0 = ib * 512
        njt = 4 * ib + 4
        buf = ib % 2
        if ib >= 2:
            P.fence("tensor")
            mt = P.mark("tensor")
            P.wait("sync", *mt)
        P.dma("sync", qc[:, buf, :, :], io["qf0"].rearrange("(k p) n -> p k n", p=128)[:, :, i0:i0 + 512])
        P.wait("tensor", P.sem("d_sync"))
        for jt in range(njt):
            pb = jt % 2
            if jt >= 2:
                P.wait("tensor", P.sem("c_c"))
            mmgroup(P, psc[:, pb, :],
                    [(k_res[:, k, jt * 128:(jt + 1) * 128], qc[:, buf, k, :]) for k in range(8)],
                    inc=(P.sem("c_a"), 1))
            P.wait("scalar", P.sem("c_a"))
            P.wait("scalar", P.sem("c_d"))
            P.emit("scalar", lambda s, o=sc_sb[:, jt, :], x=psc[:, pb, :]:
                   s.activation(o, x, AF.Identity), inc=(P.sem("c_c"), 1))
            if jt >= 4 * ib:
                P.wait("gpsimd", P.sem("c_c"))
                P.emit("gpsimd", lambda gp, o=sc_sb[:, jt, :], b=i0 - 128 * jt:
                       gp.affine_select(out=o, in_=o, compare_op=ALU.is_ge, fill=0.0,
                                        base=b, pattern=[[1, 512]], channel_multiplier=-1),
                       inc=(P.sem("c_d"), 1))
        for dv in range(8):
            pb = dv % 2
            if dv >= 2:
                P.wait("tensor", P.sem("c_b"))
            P.wait("tensor", P.sem("c_c"))
            P.wait("tensor", P.sem("c_d"))
            mmgroup(P, pso[:, pb, :],
                    [(v_res[:, 4 * ib0 + r, dv * 128:(dv + 1) * 128], sc_sb[:, 4 * ib0 + r, :])
                     for ib0 in range(ib + 1) for r in range(4)][:njt],
                    inc=(P.sem("c_a"), 1))
            P.wait("vector", P.sem("c_a"))
            P.emit("vector", lambda v, o=s0pre[:, dv, :], x=pso[:, pb, :]:
                   v.tensor_copy(o, x), inc=(P.sem("c_b"), 1))
        P.fence("vector")
        mv = P.mark("vector")
        P.wait("sync", *mv)
        P.dma("sync", io["s0pre_fm"].rearrange("(k p) n -> p k n", p=128)[:, :, i0:i0 + 512], s0pre)
        P.wait("vector", P.sem("d_sync"))
    P.barrier()


def emit_p2b(P, g, io, es):
    nc = P.nc
    sb = lambda n, s, d=F32: es.enter_context(nc.sbuf_tensor(n, s, d))[:]
    pst = lambda n, s, d=F32: es.enter_context(nc.psum_tensor(n, s, d))[:]
    sfm = sb("p2b_sfm", [128, 2, 8, 128], BF16)
    s0tm = sb("p2b_s0tm", [128, 1024], F32)
    tmp = sb("p2b_tmp", [128, 1024], F32)
    sq = sb("p2b_sq", [128, 1024], F32)
    stats = sb("p2b_stats", [128, 4], F32)
    s0bf = sb("p2b_s0bf", [128, 1024], BF16)
    lnw_b = sb("p2b_lnwb", [128, 1024], F32)
    lnb_b = sb("p2b_lnbb", [128, 1024], F32)
    rowbuf = sb("p2b_rowbuf", [1, 1024], F32)
    psT = pst("p2b_psT", [128, 2, 128], BF16)
    psB = pst("p2b_psB", [128, 1024], F32)

    for nm, dst in (("ln_w", lnw_b), ("ln_b", lnb_b)):
        P.wait("gpsimd", P.sem("c_a"))
        ld = P.dma("gpsimd", rowbuf[0:1, :], io[nm][None, :])
        P.wait("tensor", P.sem("d_gp"), ld)
        P.wait("tensor", P.sem("c_b"))
        mmgroup(P, psB[:, 0:512], [(g["ones1"][:], rowbuf[0:1, 0:512])])
        mmgroup(P, psB[:, 512:1024], [(g["ones1"][:], rowbuf[0:1, 512:1024])], inc=(P.sem("c_a"), 1))
        P.wait("vector", P.sem("c_a"))
        P.emit("vector", lambda v, o=dst, x=psB: v.tensor_copy(o[:], x[:]), inc=(P.sem("c_b"), 1))

    for tt in range(NT):
        buf = tt % 2
        if tt >= 2:
            P.fence("tensor")
            mt = P.mark("tensor")
            P.wait("sync", *mt)
        ld = P.dma("sync", sfm[:, buf, :, :],
                   io["s0pre_fm"].rearrange("(k p) n -> p k n", p=128)[:, :, tt * 128:(tt + 1) * 128])
        P.wait("tensor", P.sem("d_sync"), ld)
        for kk in range(8):
            P.wait("tensor", P.sem("c_b"))
            P.emit("tensor", lambda t, o=psT[:, kk % 2, :], x=sfm[:, buf, kk, :],
                   i=g["ident_b"]: t.transpose(o, x, i), inc=(P.sem("c_a"), 1))
            P.wait("vector", P.sem("c_a"))
            P.emit("vector", lambda v, o=s0tm[:, kk * 128:(kk + 1) * 128], x=psT[:, kk % 2, :]:
                   v.tensor_copy(o, x), inc=(P.sem("c_b"), 1))
        emit_ln(P, s0tm, tmp, tmp, stats, sq=sq, g=g)
        P.fence("vector")
        P.emit("vector", lambda v, o=tmp, w=lnw_b: v.tensor_tensor(o, o, w, ALU.mult))
        P.fence("vector")
        P.emit("vector", lambda v, o=s0bf, x=tmp, b=lnb_b: v.tensor_tensor(o, x, b, ALU.add))
        P.fence("vector")
        mv = P.mark("vector")
        P.wait("sync", *mv)
        P.dma("sync", io["s0"][tt * 128:(tt + 1) * 128, :], s0bf)
        P.wait("vector", P.sem("d_sync"))
    P.barrier()


def emit_p3(P, g, io, es):
    nc = P.nc
    sb = lambda n, s, d=F32: es.enter_context(nc.sbuf_tensor(n, s, d))[:]
    pst = lambda n, s, d=F32: es.enter_context(nc.psum_tensor(n, s, d))[:]
    set_ = sb("p3_se", [128, 2, 1024], F32)
    sqd = sb("p3_sqd", [128, 1024], F32)
    ssq = sb("p3_ssq", [128, 4], F32)
    senb = sb("p3_senb", [128, 1024], BF16)
    sef = sb("p3_sef", [128, 8, 128], BF16)
    gsb = sb("p3_gsb", [128, 1024], F32)
    rowbuf = sb("p3_rowbuf", [1, 1024], F32)
    c32 = sb("p3_c32", [1, 128], F32)
    psT = pst("p3_psT", [128, 2, 128], BF16)
    psB = pst("p3_psB", [128, 1024], F32)

    P.emit("gpsimd", lambda gp, o=c32: gp.memset(o, 32.0))
    P.wait("gpsimd", P.sem("c_a"))
    ld = P.dma("gpsimd", rowbuf[0:1, :], io["gamma_slot"][None, :])
    mg = P.mark("gpsimd")
    P.wait("tensor", *mg)
    P.wait("tensor", P.sem("d_gp"), ld)
    P.wait("tensor", P.sem("c_b"))
    mmgroup(P, psB[:, 0:512], [(c32, rowbuf[0:1, 0:512])])
    mmgroup(P, psB[:, 512:1024], [(c32, rowbuf[0:1, 512:1024])], inc=(P.sem("c_a"), 1))
    P.wait("vector", P.sem("c_a"))
    P.emit("vector", lambda v, o=gsb, x=psB: v.tensor_copy(o[:], x[:]), inc=(P.sem("c_b"), 1))

    for tt in range(NT):
        buf = tt % 2
        if tt >= 2:
            P.fence("vector")
            mv = P.mark("vector")
            P.wait("sync", *mv)
        ld = P.dma("sync", set_[:, buf, :], io["slot_embeds"][tt * 128:(tt + 1) * 128, :])
        P.wait("vector", P.sem("d_sync"), ld)
        P.emit("vector", lambda v, o=sqd, x=set_[:, buf, :]: v.tensor_mul(o, x, x))
        P.fence("vector")
        P.emit("vector", lambda v, o=ssq, x=sqd: v.tensor_reduce(o[:, 0:1], x, AX.X, ALU.add))
        P.fence("vector")
        mv = P.mark("vector")
        P.wait("scalar", *mv)
        P.emit("scalar", lambda s, o=ssq: s.activation(o[:, 1:2], o[:, 0:1], AF.Sqrt))
        ms = P.mark("scalar")
        P.wait("vector", *ms)
        P.emit("vector", lambda v, o=ssq, c=g["consts"]: v.tensor_scalar(out=o[:, 1:2], in0=o[:, 1:2],
                                                                         scalar1=c[:, 1:2], scalar2=None,
                                                                         op0=ALU.max))
        P.fence("vector")
        P.emit("vector", lambda v, o=ssq: v.reciprocal(o[:, 2:3], o[:, 1:2]))
        P.fence("vector")
        P.emit("vector", lambda v, o=senb, x=set_[:, buf, :], r=ssq, w=gsb:
               v.scalar_tensor_tensor(out=o, in0=x, scalar=r[:, 2:3], in1=w, op0=ALU.mult, op1=ALU.mult))
        P.fence("vector")
        mv = P.mark("vector")
        P.wait("tensor", *mv)
        for kk in range(8):
            P.wait("tensor", P.sem("c_b"))
            P.emit("tensor", lambda t, o=psT[:, kk % 2, :], x=senb[:, kk * 128:(kk + 1) * 128],
                   i=g["ident_b"]: t.transpose(o, x, i), inc=(P.sem("c_a"), 1))
            P.wait("vector", P.sem("c_a"))
            P.emit("vector", lambda v, o=sef[:, kk, :], x=psT[:, kk % 2, :]:
                   v.tensor_copy(o, x), inc=(P.sem("c_b"), 1))
        P.fence("vector")
        mv = P.mark("vector")
        P.wait("sync", *mv)
        P.dma("sync", io["se_fm"].rearrange("(k p) n -> p k n", p=128)[:, :, tt * 128:(tt + 1) * 128], sef)
        P.wait("vector", P.sem("d_sync"))
    P.barrier()




def emit_p5(P, g, io, es):
    nc = P.nc
    sb = lambda n, s, d=F32: es.enter_context(nc.sbuf_tensor(n, s, d))[:]
    pst = lambda n, s, d=F32: es.enter_context(nc.psum_tensor(n, s, d))[:]
    xnrf = sb("p5_xnrf", [128, 8, T], BF16)
    xnrt = sb("p5_xnrt", [128, NT, D], BF16)
    se_sl = sb("p5_sesl", [128, 2, 8, 128], BF16)
    lT_sb = sb("p5_lt", [128, T], F32)
    E = sb("p5_E", [128, T], BF16)
    disp = sb("p5_disp", [128, NT, 128], BF16)
    slots_bf = sb("p5_slots", [128, 1024], BF16)
    st4 = sb("p5_st4", [128, 6], F32)
    runC = sb("p5_runC", [128, 1], F32)
    cfin = sb("p5_cfin", [1, 128], F32)
    psL = pst("p5_psL", [128, 2, 512], F32)
    psS = pst("p5_psS", [128, 2, 512], F32)
    psT = pst("p5_psT", [128, 2, 128], BF16)
    psC = pst("p5_psC", [1, 128], F32)

    P.emit("gpsimd", lambda gp, o=runC: gp.memset(o, -3e38))
    mg = P.mark("gpsimd")
    P.wait("vector", *mg)
    P.dma("sync", xnrf, io["xnr_fm"].rearrange("(k p) n -> p k n", p=128))
    P.dma("sync", xnrt, io["xnr_tm"].rearrange("(s p) d -> p s d", p=128))

    for sc in range(NT):
        buf = sc % 2
        if sc >= 2:
            P.wait("sync", P.sem("c_a"))
        ld = P.dma("sync", se_sl[:, buf, :, :],
                   io["se_fm"].rearrange("(k p) n -> p k n", p=128)[:, :, sc * 128:(sc + 1) * 128])
        P.wait("tensor", P.sem("d_sync"), ld)
        if sc >= 1:
            P.wait("vector", P.sem("d_sync"))
        for ng in range(8):
            pb = ng % 2
            if ng >= 2:
                P.wait("tensor", P.sem("c_b"))
            mmgroup(P, psL[:, pb, :],
                    [(se_sl[:, buf, k, :], xnrf[:, k, ng * 512:(ng + 1) * 512]) for k in range(8)],
                    inc=(P.sem("c_a"), 1))
            P.wait("vector", P.sem("c_a"))
            P.emit("vector", lambda v, o=lT_sb[:, ng * 512:(ng + 1) * 512], x=psL[:, pb, :]:
                   v.tensor_copy(o, x), inc=(P.sem("c_b"), 1))
        P.fence("vector")
        mv = P.mark("vector")
        P.wait("sync", *mv)
        P.dma("sync", io["lT"][sc * 128:(sc + 1) * 128, :], lT_sb)
        P.emit("vector", lambda v, o=st4, x=lT_sb: v.tensor_reduce(o[:, 0:1], x, AX.X, ALU.max, negate=True))
        P.fence("vector")
        P.emit("vector", lambda v, o=st4: v.tensor_scalar(out=o[:, 3:4], in0=o[:, 0:1], scalar1=-1.0,
                                                          scalar2=None, op0=ALU.mult))
        P.fence("vector")
        P.emit("vector", lambda v, o=runC, x=st4: v.tensor_max(o, o, x[:, 3:4]))
        mv = P.mark("vector")
        P.wait("scalar", *mv)
        P.emit("scalar", lambda s, o=E, x=lT_sb, st=st4: s.activation(o, x, AF.Exp, bias=st[:, 0:1],
                                                                      accum_out=st[:, 1:2]))
        P.fence("scalar")
        ms = P.mark("scalar")
        P.wait("vector", *ms)
        P.wait("tensor", *ms)
        P.emit("vector", lambda v, o=st4: v.reciprocal(o[:, 2:3], o[:, 1:2]))
        P.fence("vector")
        mv2 = P.mark("vector")
        P.wait("scalar", *mv2)
        for st_ in range(NT):
            P.wait("tensor", P.sem("c_b"))
            P.emit("tensor", lambda t, o=psT[:, st_ % 2, :], x=E[:, st_ * 128:(st_ + 1) * 128],
                   i=g["ident_b"]: t.transpose(o, x, i), inc=(P.sem("c_a"), 1))
            P.wait("vector", P.sem("c_a"))
            P.emit("vector", lambda v, o=disp[:, st_, :], x=psT[:, st_ % 2, :]:
                   v.tensor_copy(o, x), inc=(P.sem("c_b"), 1))
        for ng2 in range(2):
            pb = ng2 % 2
            P.wait("tensor", P.sem("c_b"))
            P.wait("tensor", P.sem("c_c"))
            mmgroup(P, psS[:, pb, :],
                    [(disp[:, st_, :], xnrt[:, st_, ng2 * 512:(ng2 + 1) * 512]) for st_ in range(NT)],
                    inc=(P.sem("c_a"), 1))
            P.wait("scalar", P.sem("c_a"))
            P.emit("scalar", lambda s, o=slots_bf[:, ng2 * 512:(ng2 + 1) * 512], x=psS[:, pb, :],
                   st=st4: s.activation(o, x, AF.Identity, scale=st[:, 2:3]), inc=(P.sem("c_c"), 1))
        ms = P.mark("scalar")
        P.wait("sync", *ms)
        P.dma("sync", io["slots"][sc * 128:(sc + 1) * 128, :], slots_bf)
        P.wait("scalar", P.sem("d_sync"))
    P.fence("vector")
    mv = P.mark("vector")
    P.wait("tensor", *mv)
    P.emit("tensor", lambda t, o=psC, x=runC, i=g["ident_f"]: t.transpose(o, x, i),
           inc=(P.sem("c_a"), 1))
    P.wait("vector", P.sem("c_a"))
    P.emit("vector", lambda v, o=cfin, x=psC: v.tensor_copy(o, x))
    P.fence("vector")
    P.emit("vector", lambda v, o=cfin: v.tensor_reduce(o[0:1, 0:1], o[0:1, :], AX.X, ALU.max))
    P.fence("vector")
    mv = P.mark("vector")
    P.wait("sync", *mv)
    P.dma("sync", io["maxC"][0:1, 0:1], cfin[0:1, 0:1])
    P.barrier()


P6_STAGE = 99


def emit_p6(P, g, io, es):
    nc = P.nc
    sb = lambda n, s, d=F32: es.enter_context(nc.sbuf_tensor(n, s, d))[:]
    pst = lambda n, s, d=F32: es.enter_context(nc.psum_tensor(n, s, d))[:]
    w1b = sb("p6_w1b", [128, 8, 4096], BF16)
    w2b = sb("p6_w2b", [128, 32, 1024], BF16)
    wstage = sb("p6_wstage", [128, 8, 1024], F32)
    slo = sb("p6_slo", [128, 2, 2, 1024], BF16)
    sfm = sb("p6_sfm", [128, 8, 256], BF16)
    hfm = sb("p6_hfm", [128, 32, 256], BF16)
    pro = sb("p6_pro", [128, 2, 1024], BF16)
    psH = pst("p6_psH", [128, 2, 512], F32)
    psP = pst("p6_psP", [128, 2, 512], F32)
    psT = pst("p6_psT", [128, 2, 128], BF16)

    for qi in range(4):
        ld = P.dma("sync", wstage, io["w1"].rearrange("(k p) n -> p k n", p=128)[:, :, qi * 1024:(qi + 1) * 1024])
        P.wait("vector", P.sem("d_sync"), ld)
        P.emit("vector", lambda v, o=w1b[:, :, qi * 1024:(qi + 1) * 1024], x=wstage: v.tensor_copy(o, x))
        mv = P.mark("vector")
        P.wait("sync", *mv)
    for qi in range(4):
        ld = P.dma("sync", wstage, io["w2"].rearrange("(k p) n -> p k n", p=128)[:, qi * 8:(qi + 1) * 8, :])
        P.wait("vector", P.sem("d_sync"), ld)
        P.emit("vector", lambda v, o=w2b[:, qi * 8:(qi + 1) * 8, :], x=wstage: v.tensor_copy(o, x))
        mv = P.mark("vector")
        P.wait("sync", *mv)
    P.barrier()
    if P6_STAGE < 1:
        return
    for pr in range(16 if P6_STAGE >= 5 else 1):
        buf = pr % 2
        if pr >= 2:
            P.fence("tensor")
            mt = P.mark("tensor")
            P.wait("sync", *mt)
        ld = P.dma("sync", slo[:, buf, :, :],
                   io["slots"].rearrange("(n p) d -> p n d", p=128)[:, pr * 2:pr * 2 + 2, :])
        P.wait("tensor", P.sem("d_sync"), ld)
        for half in range(2):
            for kk in range(8):
                P.wait("tensor", P.sem("c_b"))
                P.emit("tensor", lambda t, o=psT[:, kk % 2, :], x=slo[:, buf, half, kk * 128:(kk + 1) * 128],
                       i=g["ident_b"]: t.transpose(o, x, i), inc=(P.sem("c_a"), 1))
                P.wait("vector", P.sem("c_a"))
                P.emit("vector", lambda v, o=sfm[:, kk, half * 128:(half + 1) * 128], x=psT[:, kk % 2, :]:
                       v.tensor_copy(o, x), inc=(P.sem("c_b"), 1))
        P.fence("vector")
        mv = P.mark("vector")
        P.wait("tensor", *mv)
        if P6_STAGE < 2:
            continue
        for mc in range(32):
            pb = mc % 2
            if mc >= 2:
                P.wait("tensor", P.sem("c_c"))
            mmgroup(P, psH[:, pb, 0:256],
                    [(w1b[:, k, mc * 128:(mc + 1) * 128], sfm[:, k, :]) for k in range(8)],
                    inc=(P.sem("c_a"), 1))
            P.wait("scalar", P.sem("c_a"))
            P.emit("scalar", lambda s, o=hfm[:, mc, :], x=psH[:, pb, 0:256]:
                   s.activation(o, x, AF.Gelu), inc=(P.sem("c_c"), 1))
        if P6_STAGE < 3:
            continue
        for mh in range(2):
            for ng in range(2):
                pb = (mh * 2 + ng) % 2
                if mh * 2 + ng >= 2:
                    P.wait("tensor", P.sem("c_d"))
                P.wait("tensor", P.sem("c_c"))
                mmgroup(P, psP[:, pb, :],
                        [(hfm[:, kf, mh * 128:(mh + 1) * 128], w2b[:, kf, ng * 512:(ng + 1) * 512])
                         for kf in range(32)],
                        inc=(P.sem("c_a"), 1))
                P.wait("vector", P.sem("c_a"))
                P.emit("vector", lambda v, o=pro[:, mh, ng * 512:(ng + 1) * 512], x=psP[:, pb, :]:
                       v.tensor_copy(o, x), inc=(P.sem("c_d"), 1))
        P.fence("vector")
        mv = P.mark("vector")
        P.wait("sync", *mv)
        P.dma("sync", io["processed"].rearrange("(n p) d -> p n d", p=128)[:, pr * 2:pr * 2 + 2, :],
              slo[:, buf, :, :] if False else pro.rearrange("p a d -> p a d"))
        P.wait("vector", P.sem("d_sync"))
    P.barrier()


def emit_p7(P, g, io, es):
    nc = P.nc
    sb = lambda n, s, d=F32: es.enter_context(nc.sbuf_tensor(n, s, d))[:]
    pst = lambda n, s, d=F32: es.enter_context(nc.psum_tensor(n, s, d))[:]
    pro = sb("p7_pro", [128, NT, 1024], BF16)
    lsl = sb("p7_lsl", [128, 2, 128], F32)
    ltm = sb("p7_ltm", [128, T], F32)
    Etm = sb("p7_Etm", [128, T], BF16)
    Ecol = sb("p7_Ecol", [128, NT, 128], BF16)
    st2 = sb("p7_st2", [128, 4], F32)
    g0bf = sb("p7_g0", [128, 1024], BF16)
    x0t = sb("p7_x0t", [128, 2, 1024], F32)
    s0t = sb("p7_s0t", [128, 2, 1024], BF16)
    tmp = sb("p7_tmp", [128, 1024], F32)
    x1t = sb("p7_x1t", [128, 1024], F32)
    psTf = pst("p7_psTf", [128, 2, 512], F32)
    psTb = pst("p7_psTb", [128, 2, 128], BF16)
    psG = pst("p7_psG", [128, 2, 512], F32)

    P.dma("sync", pro, io["processed"].rearrange("(n p) d -> p n d", p=128))
    P.barrier()

    for tt in range(NT):
        for nc_ in range(NT):
            buf = nc_ % 2
            if nc_ >= 2:
                P.wait("sync", P.sem("c_a"))
            ld = P.dma("sync", lsl[:, buf, :], io["lT"][nc_ * 128:(nc_ + 1) * 128, tt * 128:(tt + 1) * 128])
            P.wait("tensor", P.sem("d_sync"), ld)
            P.wait("tensor", P.sem("c_b"))
            P.emit("tensor", lambda t, o=psTf[:, buf, 0:128], x=lsl[:, buf, :], i=g["ident_f"]:
                   t.transpose(o, x, i), inc=(P.sem("c_a"), 1))
            P.wait("vector", P.sem("c_a"))
            P.emit("vector", lambda v, o=ltm[:, nc_ * 128:(nc_ + 1) * 128], x=psTf[:, buf, 0:128]:
                   v.tensor_copy(o, x), inc=(P.sem("c_b"), 1))
        P.emit("vector", lambda v, o=st2, x=ltm: v.tensor_reduce(o[:, 0:1], x, AX.X, ALU.max, negate=True))
        P.fence("vector")
        mv = P.mark("vector")
        P.wait("scalar", *mv)
        P.emit("scalar", lambda s, o=Etm, x=ltm, st=st2: s.activation(o, x, AF.Exp, bias=st[:, 0:1],
                                                                      accum_out=st[:, 1:2]))
        P.fence("scalar")
        ms = P.mark("scalar")
        P.wait("vector", *ms)
        P.wait("tensor", *ms)
        P.emit("vector", lambda v, o=st2: v.reciprocal(o[:, 2:3], o[:, 1:2]))
        P.fence("vector")
        mv2 = P.mark("vector")
        P.wait("scalar", *mv2)
        for nc_ in range(NT):
            P.wait("tensor", P.sem("c_b"))
            P.emit("tensor", lambda t, o=psTb[:, nc_ % 2, :], x=Etm[:, nc_ * 128:(nc_ + 1) * 128],
                   i=g["ident_b"]: t.transpose(o, x, i), inc=(P.sem("c_a"), 1))
            P.wait("vector", P.sem("c_a"))
            P.emit("vector", lambda v, o=Ecol[:, nc_, :], x=psTb[:, nc_ % 2, :]:
                   v.tensor_copy(o, x), inc=(P.sem("c_b"), 1))
        for ng in range(2):
            if ng == 0:
                P.wait("tensor", P.sem("c_b"))
                P.wait("tensor", P.sem("c_c"))
            mmgroup(P, psG[:, ng, :],
                    [(Ecol[:, nc_, :], pro[:, nc_, ng * 512:(ng + 1) * 512]) for nc_ in range(NT)],
                    inc=(P.sem("c_a"), 1) if ng == 1 else None)
        P.wait("scalar", P.sem("c_a"))
        P.emit("scalar", lambda s, o=g0bf, x=psG.rearrange("p a b -> p (a b)"), st=st2:
               s.activation(o, x, AF.Silu, scale=st[:, 2:3]), inc=(P.sem("c_c"), 1))
        buf = tt % 2
        if tt >= 2:
            P.fence("vector")
            mv = P.mark("vector")
            P.wait("sync", *mv)
        P.dma("sync", x0t[:, buf, :], io["x0"][tt * 128:(tt + 1) * 128, :])
        ld2 = P.dma("sync", s0t[:, buf, :], io["s0"][tt * 128:(tt + 1) * 128, :])
        P.wait("vector", P.sem("d_sync"), ld2)
        P.wait("vector", P.sem("c_c"))
        P.emit("vector", lambda v, o=tmp, a=s0t[:, buf, :], b=g0bf: v.tensor_mul(o, a, b))
        P.fence("vector")
        P.emit("vector", lambda v, o=x1t, a=x0t[:, buf, :], b=tmp: v.tensor_add(o, a, b))
        P.fence("vector")
        mv = P.mark("vector")
        P.wait("sync", *mv)
        P.dma("sync", io["x1"][tt * 128:(tt + 1) * 128, :], x1t)
        P.wait("vector", P.sem("d_sync"))
        P.wait("scalar", P.sem("d_sync"))
    P.barrier()


def emit_p8(P, g, io, es):
    nc = P.nc
    sb = lambda n, s, d=F32: es.enter_context(nc.sbuf_tensor(n, s, d))[:]
    pst = lambda n, s, d=F32: es.enter_context(nc.psum_tensor(n, s, d))[:]
    wts = {}
    wstage = sb("p8_wstage", [128, 8, 1024], F32)
    for wn in ("Wq", "Wk", "Wv"):
        wts[wn] = sb("p8_" + wn, [128, 8, 1024], BF16)
        ld = P.dma("sync", wstage, io[wn].rearrange("(k p) n -> p k n", p=128))
        P.wait("vector", P.sem("d_sync"), ld)
        P.emit("vector", lambda v, o=wts[wn], x=wstage: v.tensor_copy(o, x))
        mv = P.mark("vector")
        P.wait("sync", *mv)
    bb = {}
    rowbuf = sb("p8_rowbuf", [1, 1024], F32)
    c125 = sb("p8_c125", [1, 128], F32)
    psQKV = pst("p8_psQKV", [128, 2, 1024], F32)
    psB = psQKV[:, 0, :]
    P.emit("gpsimd", lambda gp, o=c125: gp.memset(o, 0.125))
    mg = P.mark("gpsimd")
    P.wait("tensor", *mg)
    for bn, lh in (("bq", c125), ("bk", g["ones1"]), ("bv", g["ones1"])):
        bb[bn] = sb("p8_b_" + bn, [128, 1024], F32)
        P.wait("gpsimd", P.sem("c_a"))
        ld = P.dma("gpsimd", rowbuf[0:1, :], io[bn][None, :])
        P.wait("tensor", P.sem("d_gp"), ld)
        P.wait("tensor", P.sem("c_b"))
        mmgroup(P, psB[:, 0:512], [(lh, rowbuf[0:1, 0:512])])
        mmgroup(P, psB[:, 512:1024], [(lh, rowbuf[0:1, 512:1024])], inc=(P.sem("c_a"), 1))
        P.wait("vector", P.sem("c_a"))
        P.emit("vector", lambda v, o=bb[bn], x=psB: v.tensor_copy(o[:], x[:]), inc=(P.sem("c_b"), 1))
    ct = sb("p8_ct", [128, NT, 32], F32)
    stb = sb("p8_st", [128, NT, 32], F32)
    P.dma("sync", ct, io["ctab"].rearrange("(tt p) i -> p tt i", p=128))
    P.dma("sync", stb, io["stab"].rearrange("(tt p) i -> p tt i", p=128))

    x1t = sb("p8_x1t", [128, 2, 1024], F32)
    tmp = sb("p8_tmp", [128, 1024], F32)
    sq = sb("p8_sq", [128, 1024], F32)
    stats = sb("p8_stats", [128, 4], F32)
    xnb = sb("p8_xnb", [128, 1024], BF16)
    xnf = sb("p8_xnf", [128, 8, 128], BF16)
    qt = sb("p8_qt", [128, 1024], F32)
    kt = sb("p8_kt", [128, 1024], F32)
    rp1 = sb("p8_rp1", [128, 512], F32)
    rp2 = sb("p8_rp2", [128, 512], F32)
    rot = sb("p8_rot", [128, 1024], BF16)
    vbf = sb("p8_vbf", [128, 1024], BF16)
    qkf = sb("p8_qkf", [128, 8, 128], BF16)
    psT = pst("p8_psT", [128, 2, 128], BF16)
    psQ = psQKV[:, 0, :]
    psK = psQKV[:, 1, :]
    P.barrier()

    def rope(src_f32, tt):
        s4 = src_f32.rearrange("p (h i two) -> p h i two", i=32, two=2)
        r4 = rot.rearrange("p (h i two) -> p h i two", i=32, two=2)
        cr = ct[:, tt, None, :].broadcast_to([128, 16, 32])
        sr = stb[:, tt, None, :].broadcast_to([128, 16, 32])
        qe, qo = s4[:, :, :, 0], s4[:, :, :, 1]
        a1 = rp1.rearrange("p (h i) -> p h i", i=32)
        a2 = rp2.rearrange("p (h i) -> p h i", i=32)
        P.emit("vector", lambda v, o=a1, x=qe, c=cr: v.tensor_mul(o, x, c))
        P.emit("vector", lambda v, o=a2, x=qo, c=sr: v.tensor_mul(o, x, c))
        P.fence("vector")
        P.emit("vector", lambda v, o=r4[:, :, :, 0], x=a1, y=a2: v.tensor_sub(o, x, y))
        P.emit("vector", lambda v, o=a1, x=qe, c=sr: v.tensor_mul(o, x, c))
        P.emit("vector", lambda v, o=a2, x=qo, c=cr: v.tensor_mul(o, x, c))
        P.fence("vector")
        P.emit("vector", lambda v, o=r4[:, :, :, 1], x=a1, y=a2: v.tensor_add(o, x, y))
        P.fence("vector")

    for tt in range(NT):
        buf = tt % 2
        if tt >= 2:
            P.fence("vector")
            mv = P.mark("vector")
            P.wait("sync", *mv)
        ld = P.dma("sync", x1t[:, buf, :], io["x1"][tt * 128:(tt + 1) * 128, :])
        P.wait("vector", P.sem("d_sync"), ld)
        emit_ln(P, x1t[:, buf, :], xnb, tmp, stats, sq=sq, g=g)
        P.fence("vector")
        mv = P.mark("vector")
        P.wait("tensor", *mv)
        for kk in range(8):
            P.wait("tensor", P.sem("c_b"))
            P.emit("tensor", lambda t, o=psT[:, kk % 2, :], x=xnb[:, kk * 128:(kk + 1) * 128],
                   i=g["ident_b"]: t.transpose(o, x, i), inc=(P.sem("c_a"), 1))
            P.wait("vector", P.sem("c_a"))
            P.emit("vector", lambda v, o=xnf[:, kk, :], x=psT[:, kk % 2, :]:
                   v.tensor_copy(o, x), inc=(P.sem("c_b"), 1))
        P.fence("vector")
        mv = P.mark("vector")
        P.wait("sync", *mv)
        P.wait("tensor", *mv)
        P.dma("sync", io["xn1_fm"].rearrange("(k p) n -> p k n", p=128)[:, :, tt * 128:(tt + 1) * 128],
              xnf)
        for wn, ps in (("Wq", psQ), ("Wk", psK)):
            for ng in range(2):
                mmgroup(P, ps[ng * 512:(ng + 1) * 512] if False else ps[:, ng * 512:(ng + 1) * 512],
                        [(xnf[:, k, :], wts[wn][:, k, ng * 512:(ng + 1) * 512]) for k in range(8)],
                        inc=(P.sem("c_a"), 1) if ng == 1 else None)
        P.wait("vector", P.sem("c_a"))
        P.emit("vector", lambda v, o=qt, x=psQ, b=bb["bq"]:
               v.scalar_tensor_tensor(out=o, in0=x, scalar=0.125, in1=b, op0=ALU.mult, op1=ALU.add))
        P.emit("vector", lambda v, o=kt, x=psK, b=bb["bk"]:
               v.scalar_tensor_tensor(out=o, in0=x, scalar=1.0, in1=b, op0=ALU.mult, op1=ALU.add),
               inc=(P.sem("c_b"), 1))
        P.wait("tensor", P.sem("c_b"))
        for ng in range(2):
            mmgroup(P, psQ[:, ng * 512:(ng + 1) * 512],
                    [(xnf[:, k, :], wts["Wv"][:, k, ng * 512:(ng + 1) * 512]) for k in range(8)],
                    inc=(P.sem("c_a"), 1) if ng == 1 else None)
        P.wait("vector", P.sem("c_a"))
        P.emit("vector", lambda v, o=vbf, x=psQ, b=bb["bv"]: v.tensor_add(o, x, b), inc=(P.sem("c_b"), 1))
        P.fence("vector")
        mv = P.mark("vector")
        P.wait("sync", *mv)
        P.dma("sync", io["v1"][tt * 128:(tt + 1) * 128, :].rearrange("p (h c) -> p h c", c=65)[:, :, 0:64],
              vbf.rearrange("p (h c) -> p h c", c=64))
        for src_f32, dstio in ((qt, "qf1"), (kt, "kf1")):
            rope(src_f32, tt)
            mv = P.mark("vector")
            P.wait("tensor", *mv)
            for kk in range(8):
                P.wait("tensor", P.sem("c_b"))
                P.emit("tensor", lambda t, o=psT[:, kk % 2, :], x=rot[:, kk * 128:(kk + 1) * 128],
                       i=g["ident_b"]: t.transpose(o, x, i), inc=(P.sem("c_a"), 1))
                P.wait("vector", P.sem("c_a"))
                P.emit("vector", lambda v, o=qkf[:, kk, :], x=psT[:, kk % 2, :]:
                       v.tensor_copy(o, x), inc=(P.sem("c_b"), 1))
            P.fence("vector")
            mv = P.mark("vector")
            P.wait("sync", *mv)
            P.dma("sync", io[dstio].rearrange("(k p) n -> p k n", p=128)[:, :, tt * 128:(tt + 1) * 128],
                  qkf)
            P.wait("vector", P.sem("d_sync"))
    P.barrier()


def emit_p9(P, g, io, es):
    nc = P.nc
    sb = lambda n, s, d=F32: es.enter_context(nc.sbuf_tensor(n, s, d))[:]
    pst = lambda n, s, d=F32: es.enter_context(nc.psum_tensor(n, s, d))[:]
    v_res = sb("p9_v", [128, NT, 1040], BF16)
    att = sb("p9_att", [128, NT, 1024], BF16)
    q2h = sb("p9_q2h", [128, 2, T], BF16)
    k2h = sb("p9_k2h", [128, 2, T], BF16)
    ET = sb("p9_ET", [128, 2, 512], BF16)
    s65 = sb("p9_s65", [65, 512], F32)
    rc = sb("p9_rc", [128, 4], F32)
    wpb = sb("p9_wpb", [128, 8, 1024], BF16)
    wstage = sb("p9_wstage", [128, 4, 1024], F32)
    s1bf = sb("p9_s1bf", [128, 1024], BF16)
    afm = sb("p9_afm", [128, 8, 128], BF16)
    psc = pst("p9_psc", [128, 2, 512], F32)
    ps65 = pst("p9_ps65", [65, 512], F32)
    psT4 = pst("p9_psT4", [128, 2, 512], F32)
    psP = pst("p9_psP", [128, 1024], F32)
    psTb = pst("p9_psTb", [128, 2, 128], BF16)

    for half in range(2):
        ld = P.dma("sync", wstage, io["Wp"].rearrange("(k p) n -> p k n", p=128)[:, half * 4:(half + 1) * 4, :])
        P.wait("vector", P.sem("d_sync"), ld)
        P.emit("vector", lambda v, o=wpb[:, half * 4:(half + 1) * 4, :], x=wstage: v.tensor_copy(o, x))
        mv = P.mark("vector")
        P.wait("sync", *mv)
    P.dma("sync", v_res, io["v1"].rearrange("(j p) d -> p j d", p=128))
    P.wait("gpsimd", P.sem("d_sync"))
    P.emit("gpsimd", lambda gp, o=v_res.rearrange("p j (h c) -> p j h c", c=65)[:, :, :, 64:65]:
           gp.memset(o, 1.0), inc=(P.sem("c_d"), 1))
    P.wait("tensor", P.sem("c_d"))
    P.barrier()

    for hg in range(8):
        buf = hg % 2
        if hg >= 2:
            P.wait("sync", P.sem("c_a"))
        P.dma("sync", q2h[:, buf, :], io["qf1"][hg * 128:(hg + 1) * 128, :])
        ld = P.dma("sync", k2h[:, buf, :], io["kf1"][hg * 128:(hg + 1) * 128, :])
        P.wait("tensor", P.sem("d_sync"), ld)
        for h2 in range(2):
            hb = 64 * h2
            hh = hg * 2 + h2
            for ib in range(8):
                i0 = ib * 512
                njt = min(NT, 4 * ib + 5)
                cb_base = P.sem("c_b").n
                cc_base = P.sem("c_c").n
                for jt in range(njt):
                    pb = jt % 2
                    if jt >= 2:
                        P.wait("tensor", P.sem("c_b"), cb_base + jt - 1)
                    P.emit("tensor", lambda t, o=psc[:, pb, :], l=k2h[:, buf, :][hb:hb + 64, jt * 128:(jt + 1) * 128],
                           r=q2h[:, buf, :][hb:hb + 64, i0:i0 + 512]:
                           t.matmul(o, l, r, start=True, stop=True), inc=(P.sem("c_a"), 1))
                    P.wait("scalar", P.sem("c_a"))
                    if jt >= 2:
                        P.wait("scalar", P.sem("c_c"), cc_base + jt - 1)
                    P.emit("scalar", lambda s, o=ET[:, pb, :], x=psc[:, pb, :]:
                           s.activation(o, x, AF.Exp), inc=(P.sem("c_b"), 1))
                    masked = jt >= 4 * ib
                    if masked:
                        P.wait("gpsimd", P.sem("c_b"))
                        P.emit("gpsimd", lambda gp, o=ET[:, pb, :], b=i0 + 64 - 128 * jt:
                               gp.affine_select(out=o, in_=o, compare_op=ALU.is_ge, fill=0.0,
                                                base=b, pattern=[[1, 512]], channel_multiplier=-1),
                               inc=(P.sem("c_d"), 1))
                        P.wait("tensor", P.sem("c_d"))
                    else:
                        P.wait("tensor", P.sem("c_b"))
                    P.emit("tensor", lambda t, o=ps65, l=v_res[:, jt, 65 * hh:65 * hh + 65],
                           r=ET[:, pb, :], st=(jt == 0), sp=(jt == njt - 1):
                           t.matmul(o, l, r, start=st, stop=sp),
                           inc=(P.sem("c_c"), 1))
                P.wait("vector", P.sem("c_c"))
                P.emit("vector", lambda v, o=s65, x=ps65: v.tensor_copy(o, x), inc=(P.sem("c_e"), 1))
                P.wait("tensor", P.sem("c_e"))
                for qq in range(4):
                    pb4 = qq % 2
                    if qq >= 2:
                        P.wait("tensor", P.sem("c_f"))
                    P.emit("tensor", lambda t, o=psT4[:, qq % 2, 0:65], x=s65[:, qq * 128:(qq + 1) * 128],
                           i=g["ident_f"][0:65, 0:65]: t.transpose(o, x, i), inc=(P.sem("c_a"), 1))
                    P.wait("vector", P.sem("c_a"))
                    P.emit("vector", lambda v, o=rc[:, qq:qq + 1], x=psT4[:, qq % 2, 64:65]:
                           v.reciprocal(o, x))
                    P.fence("vector")
                    mv = P.mark("vector")
                    P.wait("scalar", *mv)
                    tt = ib * 4 + qq
                    P.emit("scalar", lambda s, o=att[:, tt, 64 * hh:64 * hh + 64], x=psT4[:, qq % 2, 0:64],
                           r=rc[:, qq:qq + 1]: s.activation(o, x, AF.Identity, scale=r),
                           inc=(P.sem("c_f"), 1))
    P.barrier()
    for tt in range(NT):
        for kk in range(8):
            P.wait("tensor", P.sem("c_b"))
            P.emit("tensor", lambda t, o=psTb[:, kk % 2, :], x=att[:, tt, kk * 128:(kk + 1) * 128],
                   i=g["ident_b"]: t.transpose(o, x, i), inc=(P.sem("c_a"), 1))
            P.wait("vector", P.sem("c_a"))
            P.emit("vector", lambda v, o=afm[:, kk, :], x=psTb[:, kk % 2, :]:
                   v.tensor_copy(o, x), inc=(P.sem("c_b"), 1))
        P.fence("vector")
        mv = P.mark("vector")
        P.wait("tensor", *mv)
        P.wait("tensor", P.sem("c_f"))
        for ng in range(2):
            mmgroup(P, psP[:, ng * 512:(ng + 1) * 512],
                    [(afm[:, k, :], wpb[:, k, ng * 512:(ng + 1) * 512]) for k in range(8)],
                    inc=(P.sem("c_a"), 1) if ng == 1 else None)
        P.wait("vector", P.sem("c_a"))
        P.emit("vector", lambda v, o=s1bf, x=psP: v.tensor_copy(o, x), inc=(P.sem("c_f"), 1))
        mv = P.mark("vector")
        P.wait("sync", *mv)
        P.dma("sync", io["s1"][tt * 128:(tt + 1) * 128, :], s1bf)
        P.wait("vector", P.sem("d_sync"))
    P.barrier()


def emit_p11(P, g, io, es):
    nc = P.nc
    import contextlib as _ctx
    es2 = es.enter_context(_ctx.ExitStack())
    sb = lambda n, s, d=F32: es2.enter_context(nc.sbuf_tensor(n, s, d))[:]
    pst = lambda n, s, d=F32: es2.enter_context(nc.psum_tensor(n, s, d))[:]
    xnf = sb("p11_xnf", [128, 8, T], BF16)
    gwb = sb("p11_gwb", [128, 8, 2048], BF16)
    uwb = sb("p11_uwb", [128, 8, 2048], BF16)
    dwb = sb("p11_dwb", [128, 16, 1024], BF16)
    wstage = sb("p11_wstage", [128, 4096], F32)
    hfm = sb("p11_hfm", [128, 16, 512], BF16)
    gsil = sb("p11_gsil", [128, 512], BF16)
    dsb = sb("p11_dsb", [128, 2, 1024], F32)
    rw_s = sb("p11_rws", [128, 8, 12], BF16)
    rw_f = sb("p11_rwf", [128, 8, 12], F32)
    s12 = sb("p11_s12", [128, 12], F32)
    oh4 = sb("p11_oh4", [128, 4], F32)
    u4 = sb("p11_u4", [128, 4], F32)
    w8 = sb("p11_w8", [128, NT, 8], F32)
    st1 = sb("p11_st1", [128, 2], F32)
    psR = pst("p11_psR", [128, 512], F32)
    psA = pst("p11_psA", [128, 2, 512], F32)
    psD = pst("p11_psD", [128, 2, 1024], F32)

    P.dma("sync", xnf, io["xn1_fm"].rearrange("(k p) n -> p k n", p=128))
    P.dma("sync", rw_f, io["rw"].rearrange("(k p) n -> p k n", p=128))
    P.wait("vector", P.sem("d_sync"))
    P.emit("vector", lambda v, o=rw_s, x=rw_f: v.tensor_copy(o, x))
    P.barrier()
    for tt in range(NT):
        P.wait("tensor", P.sem("c_b"))
        mmgroup(P, psR[:, 0:12],
                [(xnf[:, k, tt * 128:(tt + 1) * 128], rw_s[:, k, :]) for k in range(8)],
                inc=(P.sem("c_a"), 1))
        P.wait("scalar", P.sem("c_a"))
        P.emit("scalar", lambda s, o=s12, x=psR[:, 0:12]: s.activation(o, x, AF.Sigmoid),
               inc=(P.sem("c_b"), 1))
        P.wait("vector", P.sem("c_b"))
        P.emit("vector", lambda v, o=st1, x=s12: v.tensor_reduce(o[:, 0:1], x[:, 0:4], AX.X, ALU.max))
        P.fence("vector")
        P.emit("vector", lambda v, o=oh4, x=s12, r=st1: v.tensor_scalar(out=o, in0=x[:, 0:4],
                                                                        scalar1=r[:, 0:1], scalar2=None,
                                                                        op0=ALU.is_ge))
        P.emit("vector", lambda v, o=u4, x=s12: v.tensor_add(o, x.rearrange("p (g e) -> p g e", e=2)[:, 4 // 2:, 0]
                                                             if False else x[:, 4:12:2], x[:, 5:12:2]))
        P.fence("vector")
        P.emit("vector", lambda v, o=u4: v.tensor_scalar(out=o, in0=o, scalar1=1e-9, scalar2=None,
                                                         op0=ALU.add))
        P.fence("vector")
        P.emit("vector", lambda v, o=u4: v.reciprocal(o, o))
        P.fence("vector")
        P.emit("vector", lambda v, o=u4, a=oh4: v.tensor_mul(o, o, a))
        P.fence("vector")
        P.emit("vector", lambda v, o=w8.rearrange("p t (g e) -> p t g e", e=2)[:, tt, :, :],
               x=s12.rearrange("p (g e) -> p g e", e=2)[:, 2:6, :], f=u4[:, :, None].broadcast_to([128, 4, 2]):
               v.tensor_mul(o, x, f))
        P.fence("vector")
    P.barrier()

    NTB = 8
    for e in range(8):
        for wn, dst in (("gate_w", gwb), ("up_w", uwb)):
            for qtr in range(4):
                P.fence("vector")
                mv = P.mark("vector")
                P.wait("sync", *mv)
                ld = P.dma("sync", wstage.rearrange("p (a n) -> p a n", n=2048),
                           io[wn][e].rearrange("(k p) n -> p k n", p=128)[:, qtr * 2:(qtr + 1) * 2, :])
                P.wait("vector", P.sem("d_sync"), ld)
                P.emit("vector", lambda v, o=dst[:, qtr * 2:(qtr + 1) * 2, :],
                       x=wstage.rearrange("p (a n) -> p a n", n=2048): v.tensor_copy(o, x))
        for qtr in range(4):
            P.fence("vector")
            mv = P.mark("vector")
            P.wait("sync", *mv)
            ld = P.dma("sync", wstage.rearrange("p (a n) -> p a n", n=1024),
                       io["down_w"][e].rearrange("(k p) n -> p k n", p=128)[:, qtr * 4:(qtr + 1) * 4, :])
            P.wait("vector", P.sem("d_sync"), ld)
            P.emit("vector", lambda v, o=dwb[:, qtr * 4:(qtr + 1) * 4, :],
                   x=wstage.rearrange("p (a n) -> p a n", n=1024): v.tensor_copy(o, x))
        P.fence("vector")
        mv = P.mark("vector")
        P.wait("tensor", *mv)
        for tb in range(NTB):
            t0 = tb * 512
            for mc in range(16):
                P.wait("tensor", P.sem("c_b"))
                mmgroup(P, psA[:, 0, :],
                        [(gwb[:, k, mc * 128:(mc + 1) * 128], xnf[:, k, t0:t0 + 512]) for k in range(8)],
                        inc=(P.sem("c_a"), 1))
                P.wait("scalar", P.sem("c_a"))
                P.emit("scalar", lambda s, o=gsil, x=psA[:, 0, :]: s.activation(o, x, AF.Silu),
                       inc=(P.sem("c_b"), 1))
                P.wait("tensor", P.sem("c_c"))
                mmgroup(P, psA[:, 1, :],
                        [(uwb[:, k, mc * 128:(mc + 1) * 128], xnf[:, k, t0:t0 + 512]) for k in range(8)],
                        inc=(P.sem("c_a"), 1))
                P.wait("vector", P.sem("c_a"))
                P.wait("vector", P.sem("c_b"))
                P.emit("vector", lambda v, o=hfm[:, mc, :], a=gsil, b=psA[:, 1, :]:
                       v.tensor_mul(o, a, b), inc=(P.sem("c_c"), 1))
            for ts8 in range(4):
                pb = ts8 % 2
                tt = tb * 4 + ts8
                if ts8 >= 2:
                    P.wait("tensor", P.sem("c_e"))
                P.wait("tensor", P.sem("c_c"))
                for ngd in range(2):
                    mmgroup(P, psD[:, pb, ngd * 512:(ngd + 1) * 512],
                            [(hfm[:, kf, ts8 * 128:(ts8 + 1) * 128], dwb[:, kf, ngd * 512:(ngd + 1) * 512])
                             for kf in range(16)],
                            inc=(P.sem("c_a"), 1) if ngd == 1 else None)
                P.wait("vector", P.sem("c_a"))
                P.wait("vector", P.sem("d_gp"))
                P.emit("vector", lambda v, o=dsb[:, pb, :], x=psD[:, pb, :], w=w8[:, tt, e:e + 1]:
                       v.tensor_scalar(out=o, in0=x, scalar1=w, scalar2=None, op0=ALU.mult),
                       inc=(P.sem("c_e"), 1))
                mv = P.mark("vector")
                P.wait("gpsimd", *mv)
                if e == 0:
                    P.dma("gpsimd", io["moe_acc"][tt * 128:(tt + 1) * 128, :], dsb[:, pb, :])
                else:
                    P.dma("gpsimd", io["moe_acc"][tt * 128:(tt + 1) * 128, :], dsb[:, pb, :],
                          accum_op=ALU.add)
    P.barrier()
    es2.close()
    sb = lambda n, s, d=F32: es.enter_context(nc.sbuf_tensor(n, s, d))[:]
    macc = sb("p11_macc", [128, 2, 1024], F32)
    x1t = sb("p11_x1t", [128, 2, 1024], F32)
    s1t = sb("p11_s1t", [128, 2, 1024], BF16)
    g1 = sb("p11_g1", [128, 1024], F32)
    tmpo = sb("p11_tmpo", [128, 1024], F32)
    for tt in range(NT):
        buf = tt % 2
        if tt >= 2:
            P.fence("vector")
            mv = P.mark("vector")
            P.wait("sync", *mv)
        P.dma("sync", macc[:, buf, :], io["moe_acc"][tt * 128:(tt + 1) * 128, :])
        P.dma("sync", x1t[:, buf, :], io["x1"][tt * 128:(tt + 1) * 128, :])
        ld = P.dma("sync", s1t[:, buf, :], io["s1"][tt * 128:(tt + 1) * 128, :])
        P.wait("scalar", P.sem("d_sync"), ld)
        P.emit("scalar", lambda s, o=g1, x=macc[:, buf, :]: s.activation(o, x, AF.Silu),
               inc=(P.sem("c_b"), 1))
        P.wait("vector", P.sem("c_b"))
        P.emit("vector", lambda v, o=tmpo, a=s1t[:, buf, :], b=g1: v.tensor_mul(o, a, b))
        P.fence("vector")
        P.emit("vector", lambda v, o=tmpo, a=x1t[:, buf, :], b=tmpo: v.tensor_add(o, a, b))
        P.fence("vector")
        mv = P.mark("vector")
        P.wait("sync", *mv)
        P.dma("sync", io["out"][tt * 128:(tt + 1) * 128, :], tmpo)
        P.wait("vector", P.sem("d_sync"))
        P.wait("scalar", P.sem("d_sync"))
    P.barrier()



SCRATCH = {
    "qf0": ([D, T], BF16), "kf0": ([D, T], BF16), "v0": ([T, D], BF16),
    "xnr_tm": ([T, D], BF16), "xnr_fm": ([D, T], BF16),
    "s0pre_fm": ([D, T], BF16),
    "s0": ([T, D], BF16),
    "se_fm": ([D, T], BF16),
    "lT": ([T, T], F32),
    "maxC": ([1, 1], F32),
    "slots": ([T, D], BF16),
    "processed": ([T, D], BF16),
    "x1": ([T, D], F32),
    "xn1_fm": ([D, T], BF16),
    "qf1": ([D, T], BF16), "kf1": ([D, T], BF16), "v1": ([T, 1040], BF16),
    "s1": ([T, D], BF16),
    "moe_acc": ([T, D], F32),
    "out": ([T, D], F32),
}

INPUTS = {
    "x0": ([T, D], F32),
    "W1q": ([D, D], F32), "W2q": ([D, D], F32), "W1k": ([D, D], F32), "W2k": ([D, D], F32),
    "W1v": ([D, D], F32), "W2v": ([D, D], F32),
    "b1q": ([D], F32), "b2q": ([D], F32), "b1k": ([D], F32), "b2k": ([D], F32),
    "b1v": ([D], F32), "b2v": ([D], F32),
    "ln_w": ([D], F32), "ln_b": ([D], F32),
    "gamma_x": ([D], F32), "gamma_slot": ([D], F32),
    "slot_embeds": ([T, D], F32),
    "w1": ([D, 4096], F32), "w2": ([4096, D], F32),
    "Wq": ([D, D], F32), "Wk": ([D, D], F32), "Wv": ([D, D], F32), "Wp": ([D, D], F32),
    "bq": ([D], F32), "bk": ([D], F32), "bv": ([D], F32),
    "ctab": ([T, 32], F32), "stab": ([T, 32], F32),
    "rw": ([D, 12], F32),
    "gate_w": ([8, D, 2048], F32), "up_w": ([8, D, 2048], F32), "down_w": ([8, 2048, D], F32),
}

PHASES = {
    "p1": (emit_p1, ["x0", "W1q", "W2q", "W1k", "W2k", "W1v", "W2v", "b1q", "b2q", "b1k",
                     "b2k", "b1v", "b2v", "gamma_x"],
           ["qf0", "kf0", "v0", "xnr_tm", "xnr_fm"]),
    "p2": (emit_p2, ["qf0", "kf0", "v0"], ["s0pre_fm"]),
    "p2b": (emit_p2b, ["s0pre_fm", "ln_w", "ln_b"], ["s0"]),
    "p3": (emit_p3, ["slot_embeds", "gamma_slot"], ["se_fm"]),
    "p5": (emit_p5, ["se_fm", "xnr_fm", "xnr_tm"], ["lT", "slots", "maxC"]),
    "p6": (emit_p6, ["slots", "w1", "w2"], ["processed"]),
    "p7": (emit_p7, ["lT", "processed", "x0", "s0"], ["x1"]),
    "p8": (emit_p8, ["x1", "Wq", "Wk", "Wv", "bq", "bk", "bv", "ctab", "stab"],
           ["qf1", "kf1", "v1", "xn1_fm"]),
    "p9": (emit_p9, ["qf1", "kf1", "v1", "Wp"], ["s1"]),
    "p11": (emit_p11, ["xn1_fm", "rw", "gate_w", "up_w", "down_w", "x1", "s1"],
            ["moe_acc", "out"]),
}


def build_launch(phase_names):
    nc = bass.Bass("TRN2", target_bir_lowering=False)
    P = Prog(nc)
    needed_in, needed_out = set(), set()
    for pn in phase_names:
        _, ins, outs = PHASES[pn]
        for n in ins:
            if n not in needed_out:
                needed_in.add(n)
        needed_out.update(outs)
    io = {}
    for n in sorted(needed_in):
        shape, dt = INPUTS[n] if n in INPUTS else SCRATCH[n]
        io[n] = nc.dram_tensor("in_" + n, list(shape), dt, kind="ExternalInput").ap()
    for n in sorted(needed_out):
        shape, dt = SCRATCH[n]
        io[n] = nc.dram_tensor("out_" + n, list(shape), dt, kind="ExternalOutput").ap()
    with contextlib.ExitStack() as es:
        for s in P.sems.values():
            s.h = es.enter_context(nc.semaphore(s.name))
        g = alloc_common(P, es)
        for pn in phase_names:
            emitter, _, _ = PHASES[pn]
            with contextlib.ExitStack() as pes:
                emitter(P, g, io, pes)
            P.barrier()
        block = es.enter_context(nc.Block())
        P.run(block)
    in_names = {n: ("in_" + n) for n in needed_in}
    out_names = {n: ("out_" + n) for n in needed_out}
    return nc, in_names, out_names


def prep_core_inputs(inputs, b):
    ctab, stab = host_constants()
    micro = np.ascontiguousarray(np.transpose(np.asarray(inputs["L1_ffn_micro_w"]), (1, 0, 2))
                                 ).reshape(D, 8)
    rw = np.concatenate([np.asarray(inputs["L1_ffn_macro_w"]), micro], axis=1)
    d = {
        "x0": np.ascontiguousarray(np.asarray(inputs["L0_attn_x"])[b]),
        "W1q": inputs["L0_attn_W1q"], "W2q": inputs["L0_attn_W2q"],
        "W1k": inputs["L0_attn_W1k"], "W2k": inputs["L0_attn_W2k"],
        "W1v": inputs["L0_attn_W1v"], "W2v": inputs["L0_attn_W2v"],
        "b1q": inputs["L0_attn_b1q"], "b2q": inputs["L0_attn_b2q"],
        "b1k": inputs["L0_attn_b1k"], "b2k": inputs["L0_attn_b2k"],
        "b1v": inputs["L0_attn_b1v"], "b2v": inputs["L0_attn_b2v"],
        "ln_w": inputs["L0_attn_ln_w"], "ln_b": inputs["L0_attn_ln_b"],
        "gamma_x": inputs["L0_ffn_gamma_x"], "gamma_slot": inputs["L0_ffn_gamma_slot"],
        "slot_embeds": np.asarray(inputs["L0_ffn_slot_embeds"]).reshape(T, D),
        "w1": inputs["L0_ffn_w1"], "w2": inputs["L0_ffn_w2"],
        "Wq": inputs["L1_attn_Wq"], "Wk": inputs["L1_attn_Wk"],
        "Wv": inputs["L1_attn_Wv"], "Wp": inputs["L1_attn_Wp"],
        "bq": inputs["L1_attn_bq"], "bk": inputs["L1_attn_bk"], "bv": inputs["L1_attn_bv"],
        "ctab": ctab, "stab": stab, "rw": rw,
        "gate_w": inputs["L1_ffn_gate_w"], "up_w": inputs["L1_ffn_up_w"],
        "down_w": inputs["L1_ffn_down_w"],
    }
    return {k: np.ascontiguousarray(np.asarray(v, dtype=np.float32)) for k, v in d.items()}


LAUNCH_PLAN = [["p1"], ["p2"], ["p2b"], ["p3"], ["p5"], ["p6"], ["p7"], ["p8"], ["p9"], ["p11"]]


def run_launches(per_core, launch_plan=None):
    plan = launch_plan if launch_plan is not None else LAUNCH_PLAN
    for phase_names in plan:
        nc, in_names, out_names = build_launch(phase_names)
        in_maps = []
        for b in range(8):
            m = {}
            for n, tn in in_names.items():
                m[tn] = per_core[b][n]
            in_maps.append(m)
        res = run_bass_kernel_spmd(nc, in_maps, core_ids=list(range(8)))
        for b in range(8):
            for n, tn in out_names.items():
                per_core[b][n] = np.asarray(res.results[b][tn])
    return per_core


def kernel(**inputs):
    per_core = [prep_core_inputs(inputs, b) for b in range(8)]
    run_launches(per_core)
    out = np.stack([per_core[b]["out"] for b in range(8)], axis=0)
    return out.astype(np.float32)


# version 03
# speedup vs baseline: 1.0111x; 1.0111x over previous
import numpy as np
import contextlib
import ml_dtypes
import concourse.bass as bass
import concourse.mybir as mybir
from concourse.bass_utils import run_bass_kernel_spmd

F32 = mybir.dt.float32
BF16 = mybir.dt.bfloat16
AF = mybir.ActivationFunctionType
ALU = mybir.AluOpType
AX = mybir.AxisListType

T = 4096
D = 1024
NT = T // 128
L0_SCALE = 1.0 / float(np.sqrt(np.float32(D * 2048), dtype=np.float32))
LN_EPS = 1e-5
BF = np.dtype(ml_dtypes.bfloat16)


class Sem:
    def __init__(self, name):
        self.name = name
        self.n = 0
        self.h = None


class Prog:
    ENGINES = ("sync", "tensor", "vector", "scalar", "gpsimd")

    def __init__(self, nc):
        self.nc = nc
        self.q = {e: [] for e in self.ENGINES}
        self.last = {e: None for e in self.ENGINES}
        self.sems = {}
        for nm in ("d_sync", "d_gp", "d_sc", "d_ve",
                   "x_ve", "x_sc", "x_gp", "x_te", "c_a", "c_b", "c_c", "c_d", "c_e", "c_f"):
            self.sems[nm] = Sem(nm)

    def sem(self, nm):
        return self.sems[nm]

    def emit(self, e, fn, inc=None, real=True):
        item = {"fn": fn, "inc": None}
        if inc is not None:
            sem, k = inc
            sem.n += k
            item["fn"] = (lambda eng, f=fn, s=sem, k=k: f(eng).then_inc(s.h, k))
            item["inc"] = (sem, sem.n)
        self.q[e].append(item)
        if real:
            self.last[e] = item
        return item

    def wait(self, e, sem, val=None):
        v = sem.n if val is None else val
        if v <= 0:
            return
        self.q[e].append({"fn": (lambda eng, s=sem, v=v: eng.wait_ge(s.h, v)), "inc": None})

    def dma(self, queue, dst, src, sem=None, **kw):
        s = sem if sem is not None else self.sems["d_" + ("gp" if queue == "gpsimd" else
                                                          "sc" if queue == "scalar" else
                                                          "ve" if queue == "vector" else "sync")]
        self.emit(queue, (lambda eng, d=dst, x=src, kw=kw: eng.dma_start(out=d, in_=x, **kw)),
                  inc=(s, 16))
        return s.n

    def fence(self, e):
        li = self.last[e]
        if li is None:
            return
        if li["inc"] is None:
            sname = {"vector": "x_ve", "scalar": "x_sc", "gpsimd": "x_gp", "tensor": "x_te"}.get(e)
            if sname is None:
                return
            sem = self.sems[sname]
            sem.n += 1
            f_old = li["fn"]
            li["fn"] = (lambda eng, f=f_old, s=sem: f(eng).then_inc(s.h, 1))
            li["inc"] = (sem, sem.n)
        sem, val = li["inc"]
        self.wait(e, sem, val)

    def mark(self, e):
        li = self.last[e]
        if li is None:
            return None
        if li["inc"] is None:
            sname = {"vector": "x_ve", "scalar": "x_sc", "gpsimd": "x_gp", "tensor": "x_te"}[e]
            sem = self.sems[sname]
            sem.n += 1
            f_old = li["fn"]
            li["fn"] = (lambda eng, f=f_old, s=sem: f(eng).then_inc(s.h, 1))
            li["inc"] = (sem, sem.n)
        return li["inc"]

    def barrier(self):
        marks = []
        for e in ("tensor", "vector", "scalar", "gpsimd"):
            m = self.mark(e)
            if m is not None:
                marks.append(m)
        for nm in ("d_sync", "d_gp", "d_sc", "d_ve"):
            s = self.sems[nm]
            if s.n > 0:
                marks.append((s, s.n))
        for e in self.ENGINES:
            for sem, val in marks:
                li = self.last[e]
                if li is not None and li["inc"] == (sem, val):
                    continue
                self.wait(e, sem, val)

    def run(self, block):
        for e in self.ENGINES:
            items = self.q[e]

            def mk(engine_items):
                def body(eng):
                    for it in engine_items:
                        it["fn"](eng)
                return body
            getattr(block, e)(mk(items))


def mmgroup(P, psum_ap, pairs, inc=None):
    n = len(pairs)
    for i, (lt, rh) in enumerate(pairs):
        fn = (lambda t, o=psum_ap, l=lt, r=rh, st=(i == 0), sp=(i == n - 1):
              t.matmul(o, l, r, start=st, stop=sp))
        P.emit("tensor", fn, inc=(inc if i == n - 1 else None))



def host_constants():
    HEAD_DIM = 64
    THETA = 10000.0
    inv = 1.0 / THETA ** (np.arange(0, HEAD_DIM, 2, dtype=np.float32) / HEAD_DIM)
    ang = np.arange(T, dtype=np.float32)[:, None] * inv[None, :]
    return np.cos(ang).astype(np.float32), np.sin(ang).astype(np.float32)



def alloc_common(P, es):
    nc = P.nc
    g = {}
    g["ident_b"] = es.enter_context(nc.sbuf_tensor("g_ident_b", [128, 128], BF16))[:]
    g["ident_f"] = es.enter_context(nc.sbuf_tensor("g_ident_f", [128, 128], F32))[:]
    g["ones1"] = es.enter_context(nc.sbuf_tensor("g_ones1", [1, 128], F32))[:]
    g["consts"] = es.enter_context(nc.sbuf_tensor("g_consts", [128, 8], F32))[:]
    def setup(gp, g=g):
        gp.memset(g["ident_b"][:], 0.0)
        gp.affine_select(out=g["ident_b"][:], in_=g["ident_b"][:], compare_op=ALU.not_equal,
                         fill=1.0, base=0, pattern=[[-1, 128]], channel_multiplier=1)
        gp.memset(g["ident_f"][:], 0.0)
        gp.affine_select(out=g["ident_f"][:], in_=g["ident_f"][:], compare_op=ALU.not_equal,
                         fill=1.0, base=0, pattern=[[-1, 128]], channel_multiplier=1)
        gp.memset(g["ones1"][:], 1.0)
        gp.memset(g["consts"][:, 0:1], LN_EPS)
        gp.memset(g["consts"][:, 1:2], 1e-12)
        gp.memset(g["consts"][:, 2:3], 1e-9)
        gp.memset(g["consts"][:, 3:4], 1.0)
        return gp.memset(g["consts"][:, 4:5], 1.0 / 1024.0)
    P.emit("gpsimd", setup)
    P.mark("gpsimd")
    P.barrier()
    return g


def emit_ln(P, xt, xn_out, tmp, stats, sq=None, out2=None, rho_out=None, g=None):
    nm = stats[:, 0:1]
    ssq = stats[:, 1:2]
    rstd = stats[:, 2:3]
    t3 = stats[:, 3:4]
    P.emit("vector", lambda v, o=nm, x=xt: v.tensor_reduce(o, x, AX.X, ALU.add, negate=True))
    P.fence("vector")
    P.emit("vector", lambda v, o=nm: v.tensor_scalar(out=o, in0=o, scalar1=1.0 / 1024.0,
                                                     scalar2=None, op0=ALU.mult))
    P.fence("vector")
    P.emit("vector", lambda v, o=tmp, x=xt, s=nm: v.tensor_scalar(out=o, in0=x, scalar1=s,
                                                                  scalar2=None, op0=ALU.add))
    P.fence("vector")
    mv = P.mark("vector")
    P.wait("scalar", *mv)
    P.emit("scalar", lambda s, o=ssq, x=tmp, d=sq: s.activation(d, x, AF.Square, accum_out=o))
    P.fence("scalar")
    P.emit("scalar", lambda s, o=rstd, x=ssq, c=g["consts"]: s.activation(o, x, AF.Sqrt,
                                                                          bias=c[:, 0:1], scale=1.0 / 1024.0))
    ms = P.mark("scalar")
    P.wait("vector", *ms)
    if rho_out is not None:
        P.emit("scalar", lambda s, o=t3, x=ssq: s.activation(o, x, AF.Sqrt))
        P.fence("scalar")
    ms2 = P.mark("scalar")
    P.wait("vector", *ms2)
    P.emit("vector", lambda v, o=rstd: v.reciprocal(o, o))
    P.fence("vector")
    if rho_out is not None:
        P.emit("vector", lambda v, o=t3, r=rstd: v.tensor_tensor(o, o, r, ALU.mult))
        P.fence("vector")
        P.emit("vector", lambda v, o=t3, c=g["consts"]: v.tensor_scalar(out=o, in0=o, scalar1=c[:, 1:2],
                                                                        scalar2=None, op0=ALU.max))
        P.fence("vector")
        P.emit("vector", lambda v, o=rho_out, x=t3: v.reciprocal(o, x))
        P.fence("vector")
        P.emit("vector", lambda v, o=rho_out: v.tensor_scalar(out=o, in0=o, scalar1=32.0,
                                                              scalar2=None, op0=ALU.mult))
        P.fence("vector")
    P.emit("vector", lambda v, o=xn_out, x=tmp, r=rstd: v.tensor_scalar(out=o, in0=x, scalar1=r,
                                                                        scalar2=None, op0=ALU.mult))
    if out2 is not None:
        P.emit("vector", lambda v, o=out2, x=tmp, r=rstd: v.tensor_scalar(out=o, in0=x, scalar1=r,
                                                                          scalar2=None, op0=ALU.mult))


def transpose_tile(P, psT, src_bf, dst, ident, w=128):
    P.emit("tensor", lambda t, o=psT, x=src_bf, i=ident: t.transpose(o, x, i), inc=(P.sem("c_a"), 1))
    P.wait("vector", P.sem("c_a"))
    P.emit("vector", lambda v, o=dst, x=psT: v.tensor_copy(o, x))


def emit_p1(P, g, io, es):
    nc = P.nc
    sb = lambda n, s, d=F32: es.enter_context(nc.sbuf_tensor(n, s, d))[:]
    pst = lambda n, s, d=F32: es.enter_context(nc.psum_tensor(n, s, d))[:]

    wnames = ["W1q", "W2q", "W1k", "W2k", "W1v", "W2v"]
    wb = {n: sb("p1_" + n, [128, 8, 1024], BF16) for n in wnames}
    wstage = sb("p1_wstage", [128, 8, 1024], F32)
    bia = sb("p1_bias", [128, 8, 8], F32)
    bvb = sb("p1_bvb", [128, 1024], F32)
    gxb = sb("p1_gxb", [128, 1024], F32)
    rowbuf = sb("p1_rowbuf", [1, 1024], F32)

    for i, n in enumerate(wnames):
        ld = P.dma("sync", wstage[:], io[n].rearrange("(k p) n -> p k n", p=128))
        P.wait("vector", P.sem("d_sync"), ld)
        if i > 0:
            P.fence("vector")
        P.emit("vector", lambda v, o=wb[n], x=wstage: v.tensor_copy(o[:], x[:]))
        mv = P.mark("vector")
        P.wait("sync", *mv)
    for i, bn in enumerate(["b1q", "b2q", "b1k", "b2k", "b1v", "b2v"]):
        for mc in range(8):
            P.dma("gpsimd", bia[:, i, mc:mc + 1], io[bn][mc * 128:(mc + 1) * 128].unsqueeze(1))
    P.wait("vector", P.sem("d_gp"))
    P.emit("vector", lambda v, o=bia: v.tensor_scalar(out=o[:, 1, :], in0=o[:, 1, :],
                                                      scalar1=L0_SCALE, scalar2=None, op0=ALU.mult))
    P.fence("vector")
    P.dma("gpsimd", rowbuf[0:1, :], io["b2v"][None, :])
    ps_b = pst("p1_psb", [128, 1024], F32)
    P.wait("tensor", P.sem("d_gp"))
    mmgroup(P, ps_b[:, 0:512], [(g["ones1"][:], rowbuf[0:1, 0:512])])
    mmgroup(P, ps_b[:, 512:1024], [(g["ones1"][:], rowbuf[0:1, 512:1024])], inc=(P.sem("c_a"), 1))
    P.wait("vector", P.sem("c_a"))
    P.emit("vector", lambda v, o=bvb, x=ps_b: v.tensor_copy(o[:], x[:]), inc=(P.sem("c_b"), 1))
    P.wait("gpsimd", P.sem("c_a"))
    ld = P.dma("gpsimd", rowbuf[0:1, :], io["gamma_x"][None, :])
    P.wait("tensor", P.sem("d_gp"), ld)
    P.wait("tensor", P.sem("c_b"))
    mmgroup(P, ps_b[:, 0:512], [(g["ones1"][:], rowbuf[0:1, 0:512])])
    mmgroup(P, ps_b[:, 512:1024], [(g["ones1"][:], rowbuf[0:1, 512:1024])], inc=(P.sem("c_a"), 1))
    P.wait("vector", P.sem("c_a"))
    P.emit("vector", lambda v, o=gxb, x=ps_b: v.tensor_copy(o[:], x[:]))

    xt = sb("p1_xt", [128, 2, 1024], F32)
    tmp = sb("p1_tmp", [128, 1024], F32)
    sqscr = sb("p1_sqscr", [128, 1024], F32)
    stats = sb("p1_stats", [128, 4], F32)
    rho = sb("p1_rho", [128, 1], F32)
    xn_bf = sb("p1_xnbf", [128, 1024], BF16)
    xnr_bf = sb("p1_xnrbf", [128, 1024], BF16)
    xnf = sb("p1_xnf", [128, 8, 512], BF16)
    xnrf = sb("p1_xnrf", [128, 8, 512], BF16)
    hq = sb("p1_hq", [128, 8, 512], BF16)
    qf = sb("p1_qf", [128, 8, 512], BF16)
    vtm = sb("p1_vtm", [128, 1024], BF16)
    psT = pst("p1_psT", [128, 2, 128], BF16)
    psH = pst("p1_psH", [128, 2, 512], F32)
    psV = pst("p1_psV", [128, 1024], F32)

    P.barrier()

    for ch in range(8):
        for tt4 in range(4):
            tt = ch * 4 + tt4
            buf = tt % 2
            if tt >= 2:
                P.fence("vector")
                mv = P.mark("vector")
                P.wait("sync", *mv)
            ld = P.dma("sync", xt[:, buf, :], io["x0"][tt * 128:(tt + 1) * 128, :])
            P.wait("vector", P.sem("d_sync"), ld)
            emit_ln(P, xt[:, buf, :], xn_bf, tmp, stats, sq=sqscr, rho_out=rho[:, 0:1], g=g)
            P.fence("vector")
            P.emit("vector", lambda v, o=xnr_bf, x=xn_bf, r=rho, w=gxb:
                   v.scalar_tensor_tensor(out=o, in0=x, scalar=r[:, 0:1], in1=w, op0=ALU.mult, op1=ALU.mult))
            P.fence("vector")
            mv = P.mark("vector")
            P.wait("tensor", *mv)
            P.wait("sync", *mv)
            P.dma("sync", io["xnr_tm"][tt * 128:(tt + 1) * 128, :], xnr_bf[:])
            P.wait("vector", P.sem("d_sync"))
            for k in range(8):
                pb = k % 2
                if k >= 2:
                    P.wait("tensor", P.sem("c_b"))
                P.emit("tensor", lambda t, o=psT[:, k % 2, :], x=xn_bf[:, k * 128:(k + 1) * 128],
                       i=g["ident_b"][:]: t.transpose(o, x, i), inc=(P.sem("c_a"), 1))
                P.wait("vector", P.sem("c_a"))
                P.emit("vector", lambda v, o=xnf[:, k, tt4 * 128:(tt4 + 1) * 128], x=psT[:, k % 2, :]:
                       v.tensor_copy(o, x), inc=(P.sem("c_b"), 1))
            for k in range(8):
                P.wait("tensor", P.sem("c_b"))
                P.emit("tensor", lambda t, o=psT[:, k % 2, :], x=xnr_bf[:, k * 128:(k + 1) * 128],
                       i=g["ident_b"][:]: t.transpose(o, x, i), inc=(P.sem("c_a"), 1))
                P.wait("vector", P.sem("c_a"))
                P.emit("vector", lambda v, o=xnrf[:, k, tt4 * 128:(tt4 + 1) * 128], x=psT[:, k % 2, :]:
                       v.tensor_copy(o, x), inc=(P.sem("c_b"), 1))
        mv = P.mark("vector")
        P.wait("sync", *mv)
        P.dma("sync", io["xnr_fm"].rearrange("(k p) n -> p k n", p=128)[:, :, ch * 512:(ch + 1) * 512],
              xnrf[:])
        i0 = ch * 512
        for proj, (w1n, w2n, b1c, b2c) in enumerate([("W1q", "W2q", 0, 1), ("W1k", "W2k", 2, 3),
                                                     ("W1v", "W2v", 4, 5)]):
            for mc in range(8):
                pb = mc % 2
                if mc >= 2:
                    P.wait("tensor", P.sem("c_c"))
                mmgroup(P, psH[:, pb, :],
                        [(wb[w1n][:, k, mc * 128:(mc + 1) * 128], xnf[:, k, :]) for k in range(8)],
                        inc=(P.sem("c_a"), 1))
                P.wait("scalar", P.sem("c_a"))
                P.emit("scalar", lambda s, o=hq[:, mc, :], x=psH[:, pb, :], b=bia[:, b1c, mc:mc + 1]:
                       s.activation(o, x, AF.Gelu, bias=b), inc=(P.sem("c_c"), 1))
            if proj < 2:
                scl = L0_SCALE if proj == 0 else 1.0
                for mc in range(8):
                    pb = mc % 2
                    if mc >= 2:
                        P.wait("tensor", P.sem("c_d"))
                    P.wait("tensor", P.sem("c_c"))
                    mmgroup(P, psH[:, pb, :],
                            [(wb[w2n][:, k, mc * 128:(mc + 1) * 128], hq[:, k, :]) for k in range(8)],
                            inc=(P.sem("c_a"), 1))
                    P.wait("scalar", P.sem("c_a"))
                    P.emit("scalar", lambda s, o=qf[:, mc, :], x=psH[:, pb, :], b=bia[:, b2c, mc:mc + 1], sc=scl:
                           s.activation(o, x, AF.Identity, bias=b, scale=sc), inc=(P.sem("c_d"), 1))
                ms = P.mark("scalar")
                P.wait("sync", *ms)
                dst = io["qf0"] if proj == 0 else io["kf0"]
                P.dma("sync", dst.rearrange("(k p) n -> p k n", p=128)[:, :, i0:i0 + 512], qf[:])
                P.wait("scalar", P.sem("d_sync"))
            else:
                for tt4 in range(4):
                    P.wait("tensor", P.sem("c_c"))
                    for half in range(2):
                        mmgroup(P, psV[:, half * 512:(half + 1) * 512],
                                [(hq[:, k, tt4 * 128:(tt4 + 1) * 128],
                                  wb[w2n][:, k, half * 512:(half + 1) * 512]) for k in range(8)],
                                inc=(P.sem("c_a"), 1) if half == 1 else None)
                    P.wait("vector", P.sem("c_a"))
                    P.emit("vector", lambda v, o=vtm, x=psV, b=bvb: v.tensor_add(o[:], x[:], b[:]),
                           inc=(P.sem("c_b"), 1))
                    mv = P.mark("vector")
                    P.wait("sync", *mv)
                    tt = ch * 4 + tt4
                    P.dma("sync", io["v0"][tt * 128:(tt + 1) * 128, :], vtm[:])
                    P.wait("tensor", P.sem("d_sync"))
                    P.wait("vector", P.sem("d_sync"))
    P.barrier()




def emit_p2(P, g, io, es):
    nc = P.nc
    sb = lambda n, s, d=F32: es.enter_context(nc.sbuf_tensor(n, s, d))[:]
    pst = lambda n, s, d=F32: es.enter_context(nc.psum_tensor(n, s, d))[:]

    k_res = sb("p2_k", [128, 8, T], BF16)
    v_res = sb("p2_v", [128, NT, D], BF16)
    qc = sb("p2_q", [128, 2, 8, 512], BF16)
    sc_sb = sb("p2_sc", [128, NT, 512], BF16)
    s0pre = sb("p2_s0pre", [128, 8, 512], BF16)
    psc = pst("p2_psc", [128, 2, 512], F32)
    pso = pst("p2_pso", [128, 2, 512], F32)

    P.dma("sync", k_res, io["kf0"].rearrange("(k p) n -> p k n", p=128))
    P.dma("sync", v_res, io["v0"].rearrange("(j p) d -> p j d", p=128))

    for ib in range(8):
        i0 = ib * 512
        njt = 4 * ib + 4
        buf = ib % 2
        if ib >= 2:
            P.fence("tensor")
            mt = P.mark("tensor")
            P.wait("sync", *mt)
        P.dma("sync", qc[:, buf, :, :], io["qf0"].rearrange("(k p) n -> p k n", p=128)[:, :, i0:i0 + 512])
        P.wait("tensor", P.sem("d_sync"))
        for jt in range(njt):
            pb = jt % 2
            if jt >= 2:
                P.wait("tensor", P.sem("c_c"))
            mmgroup(P, psc[:, pb, :],
                    [(k_res[:, k, jt * 128:(jt + 1) * 128], qc[:, buf, k, :]) for k in range(8)],
                    inc=(P.sem("c_a"), 1))
            P.wait("scalar", P.sem("c_a"))
            P.wait("scalar", P.sem("c_d"))
            P.emit("scalar", lambda s, o=sc_sb[:, jt, :], x=psc[:, pb, :]:
                   s.activation(o, x, AF.Identity), inc=(P.sem("c_c"), 1))
            if jt >= 4 * ib:
                P.wait("gpsimd", P.sem("c_c"))
                P.emit("gpsimd", lambda gp, o=sc_sb[:, jt, :], b=i0 - 128 * jt:
                       gp.affine_select(out=o, in_=o, compare_op=ALU.is_ge, fill=0.0,
                                        base=b, pattern=[[1, 512]], channel_multiplier=-1),
                       inc=(P.sem("c_d"), 1))
        for dv in range(8):
            pb = dv % 2
            if dv >= 2:
                P.wait("tensor", P.sem("c_b"))
            P.wait("tensor", P.sem("c_c"))
            P.wait("tensor", P.sem("c_d"))
            mmgroup(P, pso[:, pb, :],
                    [(v_res[:, 4 * ib0 + r, dv * 128:(dv + 1) * 128], sc_sb[:, 4 * ib0 + r, :])
                     for ib0 in range(ib + 1) for r in range(4)][:njt],
                    inc=(P.sem("c_a"), 1))
            P.wait("vector", P.sem("c_a"))
            P.emit("vector", lambda v, o=s0pre[:, dv, :], x=pso[:, pb, :]:
                   v.tensor_copy(o, x), inc=(P.sem("c_b"), 1))
        P.fence("vector")
        mv = P.mark("vector")
        P.wait("sync", *mv)
        P.dma("sync", io["s0pre_fm"].rearrange("(k p) n -> p k n", p=128)[:, :, i0:i0 + 512], s0pre)
        P.wait("vector", P.sem("d_sync"))
    P.barrier()


def emit_p2b(P, g, io, es):
    nc = P.nc
    sb = lambda n, s, d=F32: es.enter_context(nc.sbuf_tensor(n, s, d))[:]
    pst = lambda n, s, d=F32: es.enter_context(nc.psum_tensor(n, s, d))[:]
    sfm = sb("p2b_sfm", [128, 2, 8, 128], BF16)
    s0tm = sb("p2b_s0tm", [128, 1024], F32)
    tmp = sb("p2b_tmp", [128, 1024], F32)
    sq = sb("p2b_sq", [128, 1024], F32)
    stats = sb("p2b_stats", [128, 4], F32)
    s0bf = sb("p2b_s0bf", [128, 1024], BF16)
    lnw_b = sb("p2b_lnwb", [128, 1024], F32)
    lnb_b = sb("p2b_lnbb", [128, 1024], F32)
    rowbuf = sb("p2b_rowbuf", [1, 1024], F32)
    psT = pst("p2b_psT", [128, 2, 128], BF16)
    psB = pst("p2b_psB", [128, 1024], F32)

    for nm, dst in (("ln_w", lnw_b), ("ln_b", lnb_b)):
        P.wait("gpsimd", P.sem("c_a"))
        ld = P.dma("gpsimd", rowbuf[0:1, :], io[nm][None, :])
        P.wait("tensor", P.sem("d_gp"), ld)
        P.wait("tensor", P.sem("c_b"))
        mmgroup(P, psB[:, 0:512], [(g["ones1"][:], rowbuf[0:1, 0:512])])
        mmgroup(P, psB[:, 512:1024], [(g["ones1"][:], rowbuf[0:1, 512:1024])], inc=(P.sem("c_a"), 1))
        P.wait("vector", P.sem("c_a"))
        P.emit("vector", lambda v, o=dst, x=psB: v.tensor_copy(o[:], x[:]), inc=(P.sem("c_b"), 1))

    for tt in range(NT):
        buf = tt % 2
        if tt >= 2:
            P.fence("tensor")
            mt = P.mark("tensor")
            P.wait("sync", *mt)
        ld = P.dma("sync", sfm[:, buf, :, :],
                   io["s0pre_fm"].rearrange("(k p) n -> p k n", p=128)[:, :, tt * 128:(tt + 1) * 128])
        P.wait("tensor", P.sem("d_sync"), ld)
        for kk in range(8):
            P.wait("tensor", P.sem("c_b"))
            P.emit("tensor", lambda t, o=psT[:, kk % 2, :], x=sfm[:, buf, kk, :],
                   i=g["ident_b"]: t.transpose(o, x, i), inc=(P.sem("c_a"), 1))
            P.wait("vector", P.sem("c_a"))
            P.emit("vector", lambda v, o=s0tm[:, kk * 128:(kk + 1) * 128], x=psT[:, kk % 2, :]:
                   v.tensor_copy(o, x), inc=(P.sem("c_b"), 1))
        emit_ln(P, s0tm, tmp, tmp, stats, sq=sq, g=g)
        P.fence("vector")
        P.emit("vector", lambda v, o=tmp, w=lnw_b: v.tensor_tensor(o, o, w, ALU.mult))
        P.fence("vector")
        P.emit("vector", lambda v, o=s0bf, x=tmp, b=lnb_b: v.tensor_tensor(o, x, b, ALU.add))
        P.fence("vector")
        mv = P.mark("vector")
        P.wait("sync", *mv)
        P.dma("sync", io["s0"][tt * 128:(tt + 1) * 128, :], s0bf)
        P.wait("vector", P.sem("d_sync"))
    P.barrier()


def emit_p3(P, g, io, es):
    nc = P.nc
    sb = lambda n, s, d=F32: es.enter_context(nc.sbuf_tensor(n, s, d))[:]
    pst = lambda n, s, d=F32: es.enter_context(nc.psum_tensor(n, s, d))[:]
    set_ = sb("p3_se", [128, 2, 1024], F32)
    sqd = sb("p3_sqd", [128, 1024], F32)
    ssq = sb("p3_ssq", [128, 4], F32)
    senb = sb("p3_senb", [128, 1024], BF16)
    sef = sb("p3_sef", [128, 8, 128], BF16)
    gsb = sb("p3_gsb", [128, 1024], F32)
    rowbuf = sb("p3_rowbuf", [1, 1024], F32)
    c32 = sb("p3_c32", [1, 128], F32)
    psT = pst("p3_psT", [128, 2, 128], BF16)
    psB = pst("p3_psB", [128, 1024], F32)

    P.emit("gpsimd", lambda gp, o=c32: gp.memset(o, 32.0))
    P.wait("gpsimd", P.sem("c_a"))
    ld = P.dma("gpsimd", rowbuf[0:1, :], io["gamma_slot"][None, :])
    mg = P.mark("gpsimd")
    P.wait("tensor", *mg)
    P.wait("tensor", P.sem("d_gp"), ld)
    P.wait("tensor", P.sem("c_b"))
    mmgroup(P, psB[:, 0:512], [(c32, rowbuf[0:1, 0:512])])
    mmgroup(P, psB[:, 512:1024], [(c32, rowbuf[0:1, 512:1024])], inc=(P.sem("c_a"), 1))
    P.wait("vector", P.sem("c_a"))
    P.emit("vector", lambda v, o=gsb, x=psB: v.tensor_copy(o[:], x[:]), inc=(P.sem("c_b"), 1))

    for tt in range(NT):
        buf = tt % 2
        if tt >= 2:
            P.fence("vector")
            mv = P.mark("vector")
            P.wait("sync", *mv)
        ld = P.dma("sync", set_[:, buf, :], io["slot_embeds"][tt * 128:(tt + 1) * 128, :])
        P.wait("vector", P.sem("d_sync"), ld)
        P.emit("vector", lambda v, o=sqd, x=set_[:, buf, :]: v.tensor_mul(o, x, x))
        P.fence("vector")
        P.emit("vector", lambda v, o=ssq, x=sqd: v.tensor_reduce(o[:, 0:1], x, AX.X, ALU.add))
        P.fence("vector")
        mv = P.mark("vector")
        P.wait("scalar", *mv)
        P.emit("scalar", lambda s, o=ssq: s.activation(o[:, 1:2], o[:, 0:1], AF.Sqrt))
        ms = P.mark("scalar")
        P.wait("vector", *ms)
        P.emit("vector", lambda v, o=ssq, c=g["consts"]: v.tensor_scalar(out=o[:, 1:2], in0=o[:, 1:2],
                                                                         scalar1=c[:, 1:2], scalar2=None,
                                                                         op0=ALU.max))
        P.fence("vector")
        P.emit("vector", lambda v, o=ssq: v.reciprocal(o[:, 2:3], o[:, 1:2]))
        P.fence("vector")
        P.emit("vector", lambda v, o=senb, x=set_[:, buf, :], r=ssq, w=gsb:
               v.scalar_tensor_tensor(out=o, in0=x, scalar=r[:, 2:3], in1=w, op0=ALU.mult, op1=ALU.mult))
        P.fence("vector")
        mv = P.mark("vector")
        P.wait("tensor", *mv)
        for kk in range(8):
            P.wait("tensor", P.sem("c_b"))
            P.emit("tensor", lambda t, o=psT[:, kk % 2, :], x=senb[:, kk * 128:(kk + 1) * 128],
                   i=g["ident_b"]: t.transpose(o, x, i), inc=(P.sem("c_a"), 1))
            P.wait("vector", P.sem("c_a"))
            P.emit("vector", lambda v, o=sef[:, kk, :], x=psT[:, kk % 2, :]:
                   v.tensor_copy(o, x), inc=(P.sem("c_b"), 1))
        P.fence("vector")
        mv = P.mark("vector")
        P.wait("sync", *mv)
        P.dma("sync", io["se_fm"].rearrange("(k p) n -> p k n", p=128)[:, :, tt * 128:(tt + 1) * 128], sef)
        P.wait("vector", P.sem("d_sync"))
    P.barrier()




def emit_p5(P, g, io, es):
    nc = P.nc
    sb = lambda n, s, d=F32: es.enter_context(nc.sbuf_tensor(n, s, d))[:]
    pst = lambda n, s, d=F32: es.enter_context(nc.psum_tensor(n, s, d))[:]
    xnrf = sb("p5_xnrf", [128, 8, T], BF16)
    xnrt = sb("p5_xnrt", [128, NT, D], BF16)
    se_sl = sb("p5_sesl", [128, 2, 8, 128], BF16)
    lT_sb = sb("p5_lt", [128, T], F32)
    E = sb("p5_E", [128, T], BF16)
    disp = sb("p5_disp", [128, NT, 128], BF16)
    slots_bf = sb("p5_slots", [128, 1024], BF16)
    st4 = sb("p5_st4", [128, 6], F32)
    runC = sb("p5_runC", [128, 1], F32)
    cfin = sb("p5_cfin", [1, 128], F32)
    psL = pst("p5_psL", [128, 2, 512], F32)
    psS = pst("p5_psS", [128, 2, 512], F32)
    psT = pst("p5_psT", [128, 2, 128], BF16)
    psC = pst("p5_psC", [1, 128], F32)

    P.emit("gpsimd", lambda gp, o=runC: gp.memset(o, -3e38))
    mg = P.mark("gpsimd")
    P.wait("vector", *mg)
    P.dma("sync", xnrf, io["xnr_fm"].rearrange("(k p) n -> p k n", p=128))
    P.dma("sync", xnrt, io["xnr_tm"].rearrange("(s p) d -> p s d", p=128))

    for sc in range(NT):
        buf = sc % 2
        if sc >= 2:
            P.wait("sync", P.sem("c_a"))
        ld = P.dma("sync", se_sl[:, buf, :, :],
                   io["se_fm"].rearrange("(k p) n -> p k n", p=128)[:, :, sc * 128:(sc + 1) * 128])
        P.wait("tensor", P.sem("d_sync"), ld)
        if sc >= 1:
            P.wait("vector", P.sem("d_sync"))
        for ng in range(8):
            pb = ng % 2
            if ng >= 2:
                P.wait("tensor", P.sem("c_b"))
            mmgroup(P, psL[:, pb, :],
                    [(se_sl[:, buf, k, :], xnrf[:, k, ng * 512:(ng + 1) * 512]) for k in range(8)],
                    inc=(P.sem("c_a"), 1))
            P.wait("vector", P.sem("c_a"))
            P.emit("vector", lambda v, o=lT_sb[:, ng * 512:(ng + 1) * 512], x=psL[:, pb, :]:
                   v.tensor_copy(o, x), inc=(P.sem("c_b"), 1))
        P.fence("vector")
        mv = P.mark("vector")
        P.wait("sync", *mv)
        P.dma("sync", io["lT"][sc * 128:(sc + 1) * 128, :], lT_sb)
        P.emit("vector", lambda v, o=st4, x=lT_sb: v.tensor_reduce(o[:, 0:1], x, AX.X, ALU.max, negate=True))
        P.fence("vector")
        P.emit("vector", lambda v, o=st4: v.tensor_scalar(out=o[:, 3:4], in0=o[:, 0:1], scalar1=-1.0,
                                                          scalar2=None, op0=ALU.mult))
        P.fence("vector")
        P.emit("vector", lambda v, o=runC, x=st4: v.tensor_max(o, o, x[:, 3:4]))
        mv = P.mark("vector")
        P.wait("scalar", *mv)
        P.emit("scalar", lambda s, o=E, x=lT_sb, st=st4: s.activation(o, x, AF.Exp, bias=st[:, 0:1],
                                                                      accum_out=st[:, 1:2]))
        P.fence("scalar")
        ms = P.mark("scalar")
        P.wait("vector", *ms)
        P.wait("tensor", *ms)
        P.emit("vector", lambda v, o=st4: v.reciprocal(o[:, 2:3], o[:, 1:2]))
        P.fence("vector")
        mv2 = P.mark("vector")
        P.wait("scalar", *mv2)
        for st_ in range(NT):
            P.wait("tensor", P.sem("c_b"))
            P.emit("tensor", lambda t, o=psT[:, st_ % 2, :], x=E[:, st_ * 128:(st_ + 1) * 128],
                   i=g["ident_b"]: t.transpose(o, x, i), inc=(P.sem("c_a"), 1))
            P.wait("vector", P.sem("c_a"))
            P.emit("vector", lambda v, o=disp[:, st_, :], x=psT[:, st_ % 2, :]:
                   v.tensor_copy(o, x), inc=(P.sem("c_b"), 1))
        for ng2 in range(2):
            pb = ng2 % 2
            P.wait("tensor", P.sem("c_b"))
            P.wait("tensor", P.sem("c_c"))
            mmgroup(P, psS[:, pb, :],
                    [(disp[:, st_, :], xnrt[:, st_, ng2 * 512:(ng2 + 1) * 512]) for st_ in range(NT)],
                    inc=(P.sem("c_a"), 1))
            P.wait("scalar", P.sem("c_a"))
            P.emit("scalar", lambda s, o=slots_bf[:, ng2 * 512:(ng2 + 1) * 512], x=psS[:, pb, :],
                   st=st4: s.activation(o, x, AF.Identity, scale=st[:, 2:3]), inc=(P.sem("c_c"), 1))
        ms = P.mark("scalar")
        P.wait("sync", *ms)
        P.dma("sync", io["slots"][sc * 128:(sc + 1) * 128, :], slots_bf)
        P.wait("scalar", P.sem("d_sync"))
    P.fence("vector")
    mv = P.mark("vector")
    P.wait("tensor", *mv)
    P.emit("tensor", lambda t, o=psC, x=runC, i=g["ident_f"]: t.transpose(o, x, i),
           inc=(P.sem("c_a"), 1))
    P.wait("vector", P.sem("c_a"))
    P.emit("vector", lambda v, o=cfin, x=psC: v.tensor_copy(o, x))
    P.fence("vector")
    P.emit("vector", lambda v, o=cfin: v.tensor_reduce(o[0:1, 0:1], o[0:1, :], AX.X, ALU.max))
    P.fence("vector")
    mv = P.mark("vector")
    P.wait("sync", *mv)
    P.dma("sync", io["maxC"][0:1, 0:1], cfin[0:1, 0:1])
    P.barrier()


P6_STAGE = 99


def emit_p6(P, g, io, es):
    nc = P.nc
    sb = lambda n, s, d=F32: es.enter_context(nc.sbuf_tensor(n, s, d))[:]
    pst = lambda n, s, d=F32: es.enter_context(nc.psum_tensor(n, s, d))[:]
    w1b = sb("p6_w1b", [128, 8, 4096], BF16)
    w2b = sb("p6_w2b", [128, 32, 1024], BF16)
    wstage = sb("p6_wstage", [128, 8, 1024], F32)
    slo = sb("p6_slo", [128, 2, 2, 1024], BF16)
    sfm = sb("p6_sfm", [128, 8, 256], BF16)
    hfm = sb("p6_hfm", [128, 32, 256], BF16)
    pro = sb("p6_pro", [128, 2, 1024], BF16)
    psH = pst("p6_psH", [128, 2, 512], F32)
    psP = pst("p6_psP", [128, 2, 512], F32)
    psT = pst("p6_psT", [128, 2, 128], BF16)

    for qi in range(4):
        ld = P.dma("sync", wstage, io["w1"].rearrange("(k p) n -> p k n", p=128)[:, :, qi * 1024:(qi + 1) * 1024])
        P.wait("vector", P.sem("d_sync"), ld)
        P.emit("vector", lambda v, o=w1b[:, :, qi * 1024:(qi + 1) * 1024], x=wstage: v.tensor_copy(o, x))
        mv = P.mark("vector")
        P.wait("sync", *mv)
    for qi in range(4):
        ld = P.dma("sync", wstage, io["w2"].rearrange("(k p) n -> p k n", p=128)[:, qi * 8:(qi + 1) * 8, :])
        P.wait("vector", P.sem("d_sync"), ld)
        P.emit("vector", lambda v, o=w2b[:, qi * 8:(qi + 1) * 8, :], x=wstage: v.tensor_copy(o, x))
        mv = P.mark("vector")
        P.wait("sync", *mv)
    P.barrier()
    if P6_STAGE < 1:
        return
    for pr in range(16 if P6_STAGE >= 5 else 1):
        buf = pr % 2
        if pr >= 2:
            P.fence("tensor")
            mt = P.mark("tensor")
            P.wait("sync", *mt)
        ld = P.dma("sync", slo[:, buf, :, :],
                   io["slots"].rearrange("(n p) d -> p n d", p=128)[:, pr * 2:pr * 2 + 2, :])
        P.wait("tensor", P.sem("d_sync"), ld)
        for half in range(2):
            for kk in range(8):
                P.wait("tensor", P.sem("c_b"))
                P.emit("tensor", lambda t, o=psT[:, kk % 2, :], x=slo[:, buf, half, kk * 128:(kk + 1) * 128],
                       i=g["ident_b"]: t.transpose(o, x, i), inc=(P.sem("c_a"), 1))
                P.wait("vector", P.sem("c_a"))
                P.emit("vector", lambda v, o=sfm[:, kk, half * 128:(half + 1) * 128], x=psT[:, kk % 2, :]:
                       v.tensor_copy(o, x), inc=(P.sem("c_b"), 1))
        P.fence("vector")
        mv = P.mark("vector")
        P.wait("tensor", *mv)
        if P6_STAGE < 2:
            continue
        for mc in range(32):
            pb = mc % 2
            if mc >= 2:
                P.wait("tensor", P.sem("c_c"))
            mmgroup(P, psH[:, pb, 0:256],
                    [(w1b[:, k, mc * 128:(mc + 1) * 128], sfm[:, k, :]) for k in range(8)],
                    inc=(P.sem("c_a"), 1))
            P.wait("scalar", P.sem("c_a"))
            P.emit("scalar", lambda s, o=hfm[:, mc, :], x=psH[:, pb, 0:256]:
                   s.activation(o, x, AF.Gelu), inc=(P.sem("c_c"), 1))
        if P6_STAGE < 3:
            continue
        for mh in range(2):
            for ng in range(2):
                pb = (mh * 2 + ng) % 2
                if mh * 2 + ng >= 2:
                    P.wait("tensor", P.sem("c_d"))
                P.wait("tensor", P.sem("c_c"))
                mmgroup(P, psP[:, pb, :],
                        [(hfm[:, kf, mh * 128:(mh + 1) * 128], w2b[:, kf, ng * 512:(ng + 1) * 512])
                         for kf in range(32)],
                        inc=(P.sem("c_a"), 1))
                P.wait("vector", P.sem("c_a"))
                P.emit("vector", lambda v, o=pro[:, mh, ng * 512:(ng + 1) * 512], x=psP[:, pb, :]:
                       v.tensor_copy(o, x), inc=(P.sem("c_d"), 1))
        P.fence("vector")
        mv = P.mark("vector")
        P.wait("sync", *mv)
        P.dma("sync", io["processed"].rearrange("(n p) d -> p n d", p=128)[:, pr * 2:pr * 2 + 2, :],
              slo[:, buf, :, :] if False else pro.rearrange("p a d -> p a d"))
        P.wait("vector", P.sem("d_sync"))
    P.barrier()


def emit_p7(P, g, io, es):
    nc = P.nc
    sb = lambda n, s, d=F32: es.enter_context(nc.sbuf_tensor(n, s, d))[:]
    pst = lambda n, s, d=F32: es.enter_context(nc.psum_tensor(n, s, d))[:]
    pro = sb("p7_pro", [128, NT, 1024], BF16)
    lsl = sb("p7_lsl", [128, 2, 128], F32)
    ltm = sb("p7_ltm", [128, T], F32)
    Etm = sb("p7_Etm", [128, T], BF16)
    Ecol = sb("p7_Ecol", [128, NT, 128], BF16)
    st2 = sb("p7_st2", [128, 4], F32)
    g0bf = sb("p7_g0", [128, 1024], BF16)
    x0t = sb("p7_x0t", [128, 2, 1024], F32)
    s0t = sb("p7_s0t", [128, 2, 1024], BF16)
    tmp = sb("p7_tmp", [128, 1024], F32)
    x1t = sb("p7_x1t", [128, 1024], F32)
    psTf = pst("p7_psTf", [128, 2, 512], F32)
    psTb = pst("p7_psTb", [128, 2, 128], BF16)
    psG = pst("p7_psG", [128, 2, 512], F32)

    P.dma("sync", pro, io["processed"].rearrange("(n p) d -> p n d", p=128))
    P.barrier()

    for tt in range(NT):
        for nc_ in range(NT):
            buf = nc_ % 2
            if nc_ >= 2:
                P.wait("sync", P.sem("c_a"))
            ld = P.dma("sync", lsl[:, buf, :], io["lT"][nc_ * 128:(nc_ + 1) * 128, tt * 128:(tt + 1) * 128])
            P.wait("tensor", P.sem("d_sync"), ld)
            P.wait("tensor", P.sem("c_b"))
            P.emit("tensor", lambda t, o=psTf[:, buf, 0:128], x=lsl[:, buf, :], i=g["ident_f"]:
                   t.transpose(o, x, i), inc=(P.sem("c_a"), 1))
            P.wait("vector", P.sem("c_a"))
            P.emit("vector", lambda v, o=ltm[:, nc_ * 128:(nc_ + 1) * 128], x=psTf[:, buf, 0:128]:
                   v.tensor_copy(o, x), inc=(P.sem("c_b"), 1))
        P.emit("vector", lambda v, o=st2, x=ltm: v.tensor_reduce(o[:, 0:1], x, AX.X, ALU.max, negate=True))
        P.fence("vector")
        mv = P.mark("vector")
        P.wait("scalar", *mv)
        P.emit("scalar", lambda s, o=Etm, x=ltm, st=st2: s.activation(o, x, AF.Exp, bias=st[:, 0:1],
                                                                      accum_out=st[:, 1:2]))
        P.fence("scalar")
        ms = P.mark("scalar")
        P.wait("vector", *ms)
        P.wait("tensor", *ms)
        P.emit("vector", lambda v, o=st2: v.reciprocal(o[:, 2:3], o[:, 1:2]))
        P.fence("vector")
        mv2 = P.mark("vector")
        P.wait("scalar", *mv2)
        for nc_ in range(NT):
            P.wait("tensor", P.sem("c_b"))
            P.emit("tensor", lambda t, o=psTb[:, nc_ % 2, :], x=Etm[:, nc_ * 128:(nc_ + 1) * 128],
                   i=g["ident_b"]: t.transpose(o, x, i), inc=(P.sem("c_a"), 1))
            P.wait("vector", P.sem("c_a"))
            P.emit("vector", lambda v, o=Ecol[:, nc_, :], x=psTb[:, nc_ % 2, :]:
                   v.tensor_copy(o, x), inc=(P.sem("c_b"), 1))
        for ng in range(2):
            if ng == 0:
                P.wait("tensor", P.sem("c_b"))
                P.wait("tensor", P.sem("c_c"))
            mmgroup(P, psG[:, ng, :],
                    [(Ecol[:, nc_, :], pro[:, nc_, ng * 512:(ng + 1) * 512]) for nc_ in range(NT)],
                    inc=(P.sem("c_a"), 1) if ng == 1 else None)
        P.wait("scalar", P.sem("c_a"))
        P.emit("scalar", lambda s, o=g0bf, x=psG.rearrange("p a b -> p (a b)"), st=st2:
               s.activation(o, x, AF.Silu, scale=st[:, 2:3]), inc=(P.sem("c_c"), 1))
        buf = tt % 2
        if tt >= 2:
            P.fence("vector")
            mv = P.mark("vector")
            P.wait("sync", *mv)
        P.dma("sync", x0t[:, buf, :], io["x0"][tt * 128:(tt + 1) * 128, :])
        ld2 = P.dma("sync", s0t[:, buf, :], io["s0"][tt * 128:(tt + 1) * 128, :])
        P.wait("vector", P.sem("d_sync"), ld2)
        P.wait("vector", P.sem("c_c"))
        P.emit("vector", lambda v, o=tmp, a=s0t[:, buf, :], b=g0bf: v.tensor_mul(o, a, b))
        P.fence("vector")
        P.emit("vector", lambda v, o=x1t, a=x0t[:, buf, :], b=tmp: v.tensor_add(o, a, b))
        P.fence("vector")
        mv = P.mark("vector")
        P.wait("sync", *mv)
        P.dma("sync", io["x1"][tt * 128:(tt + 1) * 128, :], x1t)
        P.wait("vector", P.sem("d_sync"))
        P.wait("scalar", P.sem("d_sync"))
    P.barrier()


def emit_p8(P, g, io, es):
    nc = P.nc
    sb = lambda n, s, d=F32: es.enter_context(nc.sbuf_tensor(n, s, d))[:]
    pst = lambda n, s, d=F32: es.enter_context(nc.psum_tensor(n, s, d))[:]
    wts = {}
    wstage = sb("p8_wstage", [128, 8, 1024], F32)
    for wn in ("Wq", "Wk", "Wv"):
        wts[wn] = sb("p8_" + wn, [128, 8, 1024], BF16)
        ld = P.dma("sync", wstage, io[wn].rearrange("(k p) n -> p k n", p=128))
        P.wait("vector", P.sem("d_sync"), ld)
        P.emit("vector", lambda v, o=wts[wn], x=wstage: v.tensor_copy(o, x))
        mv = P.mark("vector")
        P.wait("sync", *mv)
    bb = {}
    rowbuf = sb("p8_rowbuf", [1, 1024], F32)
    c125 = sb("p8_c125", [1, 128], F32)
    psQKV = pst("p8_psQKV", [128, 2, 1024], F32)
    psB = psQKV[:, 0, :]
    P.emit("gpsimd", lambda gp, o=c125: gp.memset(o, 0.125))
    mg = P.mark("gpsimd")
    P.wait("tensor", *mg)
    for bn, lh in (("bq", c125), ("bk", g["ones1"]), ("bv", g["ones1"])):
        bb[bn] = sb("p8_b_" + bn, [128, 1024], F32)
        P.wait("gpsimd", P.sem("c_a"))
        ld = P.dma("gpsimd", rowbuf[0:1, :], io[bn][None, :])
        P.wait("tensor", P.sem("d_gp"), ld)
        P.wait("tensor", P.sem("c_b"))
        mmgroup(P, psB[:, 0:512], [(lh, rowbuf[0:1, 0:512])])
        mmgroup(P, psB[:, 512:1024], [(lh, rowbuf[0:1, 512:1024])], inc=(P.sem("c_a"), 1))
        P.wait("vector", P.sem("c_a"))
        P.emit("vector", lambda v, o=bb[bn], x=psB: v.tensor_copy(o[:], x[:]), inc=(P.sem("c_b"), 1))
    ct = sb("p8_ct", [128, NT, 32], F32)
    stb = sb("p8_st", [128, NT, 32], F32)
    P.dma("sync", ct, io["ctab"].rearrange("(tt p) i -> p tt i", p=128))
    P.dma("sync", stb, io["stab"].rearrange("(tt p) i -> p tt i", p=128))

    x1t = sb("p8_x1t", [128, 2, 1024], F32)
    tmp = sb("p8_tmp", [128, 1024], F32)
    sq = sb("p8_sq", [128, 1024], F32)
    stats = sb("p8_stats", [128, 4], F32)
    xnb = sb("p8_xnb", [128, 1024], BF16)
    xnf = sb("p8_xnf", [128, 8, 128], BF16)
    qt = sb("p8_qt", [128, 1024], F32)
    kt = sb("p8_kt", [128, 1024], F32)
    rp1 = sb("p8_rp1", [128, 512], F32)
    rp2 = sb("p8_rp2", [128, 512], F32)
    rot = sb("p8_rot", [128, 1024], BF16)
    vbf = sb("p8_vbf", [128, 1024], BF16)
    qkf = sb("p8_qkf", [128, 8, 128], BF16)
    psT = pst("p8_psT", [128, 2, 128], BF16)
    psQ = psQKV[:, 0, :]
    psK = psQKV[:, 1, :]
    P.barrier()

    def rope(src_f32, tt):
        s4 = src_f32.rearrange("p (h i two) -> p h i two", i=32, two=2)
        r4 = rot.rearrange("p (h i two) -> p h i two", i=32, two=2)
        cr = ct[:, tt, None, :].broadcast_to([128, 16, 32])
        sr = stb[:, tt, None, :].broadcast_to([128, 16, 32])
        qe, qo = s4[:, :, :, 0], s4[:, :, :, 1]
        a1 = rp1.rearrange("p (h i) -> p h i", i=32)
        a2 = rp2.rearrange("p (h i) -> p h i", i=32)
        P.emit("vector", lambda v, o=a1, x=qe, c=cr: v.tensor_mul(o, x, c))
        P.emit("vector", lambda v, o=a2, x=qo, c=sr: v.tensor_mul(o, x, c))
        P.fence("vector")
        P.emit("vector", lambda v, o=r4[:, :, :, 0], x=a1, y=a2: v.tensor_sub(o, x, y))
        P.emit("vector", lambda v, o=a1, x=qe, c=sr: v.tensor_mul(o, x, c))
        P.emit("vector", lambda v, o=a2, x=qo, c=cr: v.tensor_mul(o, x, c))
        P.fence("vector")
        P.emit("vector", lambda v, o=r4[:, :, :, 1], x=a1, y=a2: v.tensor_add(o, x, y))
        P.fence("vector")

    for tt in range(NT):
        buf = tt % 2
        if tt >= 2:
            P.fence("vector")
            mv = P.mark("vector")
            P.wait("sync", *mv)
        ld = P.dma("sync", x1t[:, buf, :], io["x1"][tt * 128:(tt + 1) * 128, :])
        P.wait("vector", P.sem("d_sync"), ld)
        emit_ln(P, x1t[:, buf, :], xnb, tmp, stats, sq=sq, g=g)
        P.fence("vector")
        mv = P.mark("vector")
        P.wait("tensor", *mv)
        for kk in range(8):
            P.wait("tensor", P.sem("c_b"))
            P.emit("tensor", lambda t, o=psT[:, kk % 2, :], x=xnb[:, kk * 128:(kk + 1) * 128],
                   i=g["ident_b"]: t.transpose(o, x, i), inc=(P.sem("c_a"), 1))
            P.wait("vector", P.sem("c_a"))
            P.emit("vector", lambda v, o=xnf[:, kk, :], x=psT[:, kk % 2, :]:
                   v.tensor_copy(o, x), inc=(P.sem("c_b"), 1))
        P.fence("vector")
        mv = P.mark("vector")
        P.wait("sync", *mv)
        P.wait("tensor", *mv)
        P.dma("sync", io["xn1_fm"].rearrange("(k p) n -> p k n", p=128)[:, :, tt * 128:(tt + 1) * 128],
              xnf)
        for wn, ps in (("Wq", psQ), ("Wk", psK)):
            for ng in range(2):
                mmgroup(P, ps[ng * 512:(ng + 1) * 512] if False else ps[:, ng * 512:(ng + 1) * 512],
                        [(xnf[:, k, :], wts[wn][:, k, ng * 512:(ng + 1) * 512]) for k in range(8)],
                        inc=(P.sem("c_a"), 1) if ng == 1 else None)
        P.wait("vector", P.sem("c_a"))
        P.emit("vector", lambda v, o=qt, x=psQ, b=bb["bq"]:
               v.scalar_tensor_tensor(out=o, in0=x, scalar=0.125, in1=b, op0=ALU.mult, op1=ALU.add))
        P.emit("vector", lambda v, o=kt, x=psK, b=bb["bk"]:
               v.scalar_tensor_tensor(out=o, in0=x, scalar=1.0, in1=b, op0=ALU.mult, op1=ALU.add),
               inc=(P.sem("c_b"), 1))
        P.wait("tensor", P.sem("c_b"))
        for ng in range(2):
            mmgroup(P, psQ[:, ng * 512:(ng + 1) * 512],
                    [(xnf[:, k, :], wts["Wv"][:, k, ng * 512:(ng + 1) * 512]) for k in range(8)],
                    inc=(P.sem("c_a"), 1) if ng == 1 else None)
        P.wait("vector", P.sem("c_a"))
        P.emit("vector", lambda v, o=vbf, x=psQ, b=bb["bv"]: v.tensor_add(o, x, b), inc=(P.sem("c_b"), 1))
        P.fence("vector")
        mv = P.mark("vector")
        P.wait("sync", *mv)
        P.dma("sync", io["v1"][tt * 128:(tt + 1) * 128, :].rearrange("p (h c) -> p h c", c=65)[:, :, 0:64],
              vbf.rearrange("p (h c) -> p h c", c=64))
        for src_f32, dstio in ((qt, "qf1"), (kt, "kf1")):
            rope(src_f32, tt)
            mv = P.mark("vector")
            P.wait("tensor", *mv)
            for kk in range(8):
                P.wait("tensor", P.sem("c_b"))
                P.emit("tensor", lambda t, o=psT[:, kk % 2, :], x=rot[:, kk * 128:(kk + 1) * 128],
                       i=g["ident_b"]: t.transpose(o, x, i), inc=(P.sem("c_a"), 1))
                P.wait("vector", P.sem("c_a"))
                P.emit("vector", lambda v, o=qkf[:, kk, :], x=psT[:, kk % 2, :]:
                       v.tensor_copy(o, x), inc=(P.sem("c_b"), 1))
            P.fence("vector")
            mv = P.mark("vector")
            P.wait("sync", *mv)
            P.dma("sync", io[dstio].rearrange("(k p) n -> p k n", p=128)[:, :, tt * 128:(tt + 1) * 128],
                  qkf)
            P.wait("vector", P.sem("d_sync"))
    P.barrier()


def emit_p9(P, g, io, es):
    nc = P.nc
    sb = lambda n, s, d=F32: es.enter_context(nc.sbuf_tensor(n, s, d))[:]
    pst = lambda n, s, d=F32: es.enter_context(nc.psum_tensor(n, s, d))[:]
    v_res = sb("p9_v", [128, NT, 1040], BF16)
    att = sb("p9_att", [128, NT, 1024], BF16)
    q2h = sb("p9_q2h", [128, 2, T], BF16)
    k2h = sb("p9_k2h", [128, 2, T], BF16)
    ET = sb("p9_ET", [128, 2, 512], BF16)
    s65 = sb("p9_s65", [65, 512], F32)
    rc = sb("p9_rc", [128, 4], F32)
    wpb = sb("p9_wpb", [128, 8, 1024], BF16)
    wstage = sb("p9_wstage", [128, 4, 1024], F32)
    s1bf = sb("p9_s1bf", [128, 1024], BF16)
    afm = sb("p9_afm", [128, 8, 128], BF16)
    psc = pst("p9_psc", [128, 2, 512], F32)
    ps65 = pst("p9_ps65", [65, 512], F32)
    psT4 = pst("p9_psT4", [128, 2, 512], F32)
    psP = pst("p9_psP", [128, 1024], F32)
    psTb = pst("p9_psTb", [128, 2, 128], BF16)

    for half in range(2):
        ld = P.dma("sync", wstage, io["Wp"].rearrange("(k p) n -> p k n", p=128)[:, half * 4:(half + 1) * 4, :])
        P.wait("vector", P.sem("d_sync"), ld)
        P.emit("vector", lambda v, o=wpb[:, half * 4:(half + 1) * 4, :], x=wstage: v.tensor_copy(o, x))
        mv = P.mark("vector")
        P.wait("sync", *mv)
    P.dma("sync", v_res, io["v1"].rearrange("(j p) d -> p j d", p=128))
    P.wait("gpsimd", P.sem("d_sync"))
    P.emit("gpsimd", lambda gp, o=v_res.rearrange("p j (h c) -> p j h c", c=65)[:, :, :, 64:65]:
           gp.memset(o, 1.0), inc=(P.sem("c_d"), 1))
    P.wait("tensor", P.sem("c_d"))
    P.barrier()

    for hg in range(8):
        buf = hg % 2
        if hg >= 2:
            P.wait("sync", P.sem("c_a"))
        P.dma("sync", q2h[:, buf, :], io["qf1"][hg * 128:(hg + 1) * 128, :])
        ld = P.dma("sync", k2h[:, buf, :], io["kf1"][hg * 128:(hg + 1) * 128, :])
        P.wait("tensor", P.sem("d_sync"), ld)
        for h2 in range(2):
            hb = 64 * h2
            hh = hg * 2 + h2
            for ib in range(8):
                i0 = ib * 512
                njt = min(NT, 4 * ib + 5)
                cb_base = P.sem("c_b").n
                cc_base = P.sem("c_c").n
                for jt in range(njt):
                    pb = jt % 2
                    if jt >= 2:
                        P.wait("tensor", P.sem("c_b"), cb_base + jt - 1)
                    P.emit("tensor", lambda t, o=psc[:, pb, :], l=k2h[:, buf, :][hb:hb + 64, jt * 128:(jt + 1) * 128],
                           r=q2h[:, buf, :][hb:hb + 64, i0:i0 + 512]:
                           t.matmul(o, l, r, start=True, stop=True), inc=(P.sem("c_a"), 1))
                    P.wait("scalar", P.sem("c_a"))
                    if jt >= 2:
                        P.wait("scalar", P.sem("c_c"), cc_base + jt - 1)
                    P.emit("scalar", lambda s, o=ET[:, pb, :], x=psc[:, pb, :]:
                           s.activation(o, x, AF.Exp), inc=(P.sem("c_b"), 1))
                    masked = jt >= 4 * ib
                    if masked:
                        P.wait("gpsimd", P.sem("c_b"))
                        P.emit("gpsimd", lambda gp, o=ET[:, pb, :], b=i0 + 64 - 128 * jt:
                               gp.affine_select(out=o, in_=o, compare_op=ALU.is_ge, fill=0.0,
                                                base=b, pattern=[[1, 512]], channel_multiplier=-1),
                               inc=(P.sem("c_d"), 1))
                        P.wait("tensor", P.sem("c_d"))
                    else:
                        P.wait("tensor", P.sem("c_b"))
                    P.emit("tensor", lambda t, o=ps65, l=v_res[:, jt, 65 * hh:65 * hh + 65],
                           r=ET[:, pb, :], st=(jt == 0), sp=(jt == njt - 1):
                           t.matmul(o, l, r, start=st, stop=sp),
                           inc=(P.sem("c_c"), 1))
                P.wait("vector", P.sem("c_c"))
                P.emit("vector", lambda v, o=s65, x=ps65: v.tensor_copy(o, x), inc=(P.sem("c_e"), 1))
                P.wait("tensor", P.sem("c_e"))
                for qq in range(4):
                    pb4 = qq % 2
                    if qq >= 2:
                        P.wait("tensor", P.sem("c_f"))
                    P.emit("tensor", lambda t, o=psT4[:, qq % 2, 0:65], x=s65[:, qq * 128:(qq + 1) * 128],
                           i=g["ident_f"][0:65, 0:65]: t.transpose(o, x, i), inc=(P.sem("c_a"), 1))
                    P.wait("vector", P.sem("c_a"))
                    P.emit("vector", lambda v, o=rc[:, qq:qq + 1], x=psT4[:, qq % 2, 64:65]:
                           v.reciprocal(o, x))
                    P.fence("vector")
                    mv = P.mark("vector")
                    P.wait("scalar", *mv)
                    tt = ib * 4 + qq
                    P.emit("scalar", lambda s, o=att[:, tt, 64 * hh:64 * hh + 64], x=psT4[:, qq % 2, 0:64],
                           r=rc[:, qq:qq + 1]: s.activation(o, x, AF.Identity, scale=r),
                           inc=(P.sem("c_f"), 1))
    P.barrier()
    for tt in range(NT):
        for kk in range(8):
            P.wait("tensor", P.sem("c_b"))
            P.emit("tensor", lambda t, o=psTb[:, kk % 2, :], x=att[:, tt, kk * 128:(kk + 1) * 128],
                   i=g["ident_b"]: t.transpose(o, x, i), inc=(P.sem("c_a"), 1))
            P.wait("vector", P.sem("c_a"))
            P.emit("vector", lambda v, o=afm[:, kk, :], x=psTb[:, kk % 2, :]:
                   v.tensor_copy(o, x), inc=(P.sem("c_b"), 1))
        P.fence("vector")
        mv = P.mark("vector")
        P.wait("tensor", *mv)
        P.wait("tensor", P.sem("c_f"))
        for ng in range(2):
            mmgroup(P, psP[:, ng * 512:(ng + 1) * 512],
                    [(afm[:, k, :], wpb[:, k, ng * 512:(ng + 1) * 512]) for k in range(8)],
                    inc=(P.sem("c_a"), 1) if ng == 1 else None)
        P.wait("vector", P.sem("c_a"))
        P.emit("vector", lambda v, o=s1bf, x=psP: v.tensor_copy(o, x), inc=(P.sem("c_f"), 1))
        mv = P.mark("vector")
        P.wait("sync", *mv)
        P.dma("sync", io["s1"][tt * 128:(tt + 1) * 128, :], s1bf)
        P.wait("vector", P.sem("d_sync"))
    P.barrier()


def emit_p11(P, g, io, es):
    nc = P.nc
    import contextlib as _ctx
    es2 = es.enter_context(_ctx.ExitStack())
    sb = lambda n, s, d=F32: es2.enter_context(nc.sbuf_tensor(n, s, d))[:]
    pst = lambda n, s, d=F32: es2.enter_context(nc.psum_tensor(n, s, d))[:]
    xnf = sb("p11_xnf", [128, 8, T], BF16)
    gwb = sb("p11_gwb", [128, 8, 2048], BF16)
    uwb = sb("p11_uwb", [128, 8, 2048], BF16)
    dwb = sb("p11_dwb", [128, 16, 1024], BF16)
    wstage = sb("p11_wstage", [128, 4096], F32)
    hfm = sb("p11_hfm", [128, 16, 512], BF16)
    gsil = sb("p11_gsil", [128, 512], BF16)
    dsb = sb("p11_dsb", [128, 2, 1024], F32)
    rw_s = sb("p11_rws", [128, 8, 12], BF16)
    rw_f = sb("p11_rwf", [128, 8, 12], F32)
    s12 = sb("p11_s12", [128, 12], F32)
    oh4 = sb("p11_oh4", [128, 4], F32)
    u4 = sb("p11_u4", [128, 4], F32)
    w8 = sb("p11_w8", [128, NT, 8], F32)
    st1 = sb("p11_st1", [128, 2], F32)
    psR = pst("p11_psR", [128, 512], F32)
    psA = pst("p11_psA", [128, 2, 512], F32)
    psD = pst("p11_psD", [128, 2, 1024], F32)

    P.dma("sync", xnf, io["xn1_fm"].rearrange("(k p) n -> p k n", p=128))
    P.dma("sync", rw_f, io["rw"].rearrange("(k p) n -> p k n", p=128))
    P.wait("vector", P.sem("d_sync"))
    P.emit("vector", lambda v, o=rw_s, x=rw_f: v.tensor_copy(o, x))
    P.barrier()
    for tt in range(NT):
        P.wait("tensor", P.sem("c_b"))
        mmgroup(P, psR[:, 0:12],
                [(xnf[:, k, tt * 128:(tt + 1) * 128], rw_s[:, k, :]) for k in range(8)],
                inc=(P.sem("c_a"), 1))
        P.wait("scalar", P.sem("c_a"))
        P.emit("scalar", lambda s, o=s12, x=psR[:, 0:12]: s.activation(o, x, AF.Sigmoid),
               inc=(P.sem("c_b"), 1))
        P.wait("vector", P.sem("c_b"))
        P.emit("vector", lambda v, o=st1, x=s12: v.tensor_reduce(o[:, 0:1], x[:, 0:4], AX.X, ALU.max))
        P.fence("vector")
        P.emit("vector", lambda v, o=oh4, x=s12, r=st1: v.tensor_scalar(out=o, in0=x[:, 0:4],
                                                                        scalar1=r[:, 0:1], scalar2=None,
                                                                        op0=ALU.is_ge))
        P.emit("vector", lambda v, o=u4, x=s12: v.tensor_add(o, x.rearrange("p (g e) -> p g e", e=2)[:, 4 // 2:, 0]
                                                             if False else x[:, 4:12:2], x[:, 5:12:2]))
        P.fence("vector")
        P.emit("vector", lambda v, o=u4: v.tensor_scalar(out=o, in0=o, scalar1=1e-9, scalar2=None,
                                                         op0=ALU.add))
        P.fence("vector")
        P.emit("vector", lambda v, o=u4: v.reciprocal(o, o))
        P.fence("vector")
        P.emit("vector", lambda v, o=u4, a=oh4: v.tensor_mul(o, o, a))
        P.fence("vector")
        P.emit("vector", lambda v, o=w8.rearrange("p t (g e) -> p t g e", e=2)[:, tt, :, :],
               x=s12.rearrange("p (g e) -> p g e", e=2)[:, 2:6, :], f=u4[:, :, None].broadcast_to([128, 4, 2]):
               v.tensor_mul(o, x, f))
        P.fence("vector")
    P.barrier()

    NTB = 8
    for e in range(8):
        for wn, dst in (("gate_w", gwb), ("up_w", uwb)):
            for qtr in range(4):
                P.fence("vector")
                mv = P.mark("vector")
                P.wait("sync", *mv)
                ld = P.dma("sync", wstage.rearrange("p (a n) -> p a n", n=2048),
                           io[wn][e].rearrange("(k p) n -> p k n", p=128)[:, qtr * 2:(qtr + 1) * 2, :])
                P.wait("vector", P.sem("d_sync"), ld)
                P.emit("vector", lambda v, o=dst[:, qtr * 2:(qtr + 1) * 2, :],
                       x=wstage.rearrange("p (a n) -> p a n", n=2048): v.tensor_copy(o, x))
        for qtr in range(4):
            P.fence("vector")
            mv = P.mark("vector")
            P.wait("sync", *mv)
            ld = P.dma("sync", wstage.rearrange("p (a n) -> p a n", n=1024),
                       io["down_w"][e].rearrange("(k p) n -> p k n", p=128)[:, qtr * 4:(qtr + 1) * 4, :])
            P.wait("vector", P.sem("d_sync"), ld)
            P.emit("vector", lambda v, o=dwb[:, qtr * 4:(qtr + 1) * 4, :],
                   x=wstage.rearrange("p (a n) -> p a n", n=1024): v.tensor_copy(o, x))
        P.fence("vector")
        mv = P.mark("vector")
        P.wait("tensor", *mv)
        for tb in range(NTB):
            t0 = tb * 512
            for mc in range(16):
                P.wait("tensor", P.sem("c_b"))
                mmgroup(P, psA[:, 0, :],
                        [(gwb[:, k, mc * 128:(mc + 1) * 128], xnf[:, k, t0:t0 + 512]) for k in range(8)],
                        inc=(P.sem("c_a"), 1))
                P.wait("scalar", P.sem("c_a"))
                P.emit("scalar", lambda s, o=gsil, x=psA[:, 0, :]: s.activation(o, x, AF.Silu),
                       inc=(P.sem("c_b"), 1))
                P.wait("tensor", P.sem("c_c"))
                mmgroup(P, psA[:, 1, :],
                        [(uwb[:, k, mc * 128:(mc + 1) * 128], xnf[:, k, t0:t0 + 512]) for k in range(8)],
                        inc=(P.sem("c_a"), 1))
                P.wait("vector", P.sem("c_a"))
                P.wait("vector", P.sem("c_b"))
                P.emit("vector", lambda v, o=hfm[:, mc, :], a=gsil, b=psA[:, 1, :]:
                       v.tensor_mul(o, a, b), inc=(P.sem("c_c"), 1))
            for ts8 in range(4):
                pb = ts8 % 2
                tt = tb * 4 + ts8
                if ts8 >= 2:
                    P.wait("tensor", P.sem("c_e"))
                P.wait("tensor", P.sem("c_c"))
                for ngd in range(2):
                    mmgroup(P, psD[:, pb, ngd * 512:(ngd + 1) * 512],
                            [(hfm[:, kf, ts8 * 128:(ts8 + 1) * 128], dwb[:, kf, ngd * 512:(ngd + 1) * 512])
                             for kf in range(16)],
                            inc=(P.sem("c_a"), 1) if ngd == 1 else None)
                P.wait("vector", P.sem("c_a"))
                P.wait("vector", P.sem("d_gp"))
                P.emit("vector", lambda v, o=dsb[:, pb, :], x=psD[:, pb, :], w=w8[:, tt, e:e + 1]:
                       v.tensor_scalar(out=o, in0=x, scalar1=w, scalar2=None, op0=ALU.mult),
                       inc=(P.sem("c_e"), 1))
                mv = P.mark("vector")
                P.wait("gpsimd", *mv)
                if e == 0:
                    P.dma("gpsimd", io["moe_acc"][tt * 128:(tt + 1) * 128, :], dsb[:, pb, :])
                else:
                    P.dma("gpsimd", io["moe_acc"][tt * 128:(tt + 1) * 128, :], dsb[:, pb, :],
                          accum_op=ALU.add)
    P.barrier()
    es2.close()
    sb = lambda n, s, d=F32: es.enter_context(nc.sbuf_tensor(n, s, d))[:]
    macc = sb("p11_macc", [128, 2, 1024], F32)
    x1t = sb("p11_x1t", [128, 2, 1024], F32)
    s1t = sb("p11_s1t", [128, 2, 1024], BF16)
    g1 = sb("p11_g1", [128, 1024], F32)
    tmpo = sb("p11_tmpo", [128, 1024], F32)
    for tt in range(NT):
        buf = tt % 2
        if tt >= 2:
            P.fence("vector")
            mv = P.mark("vector")
            P.wait("sync", *mv)
        P.dma("sync", macc[:, buf, :], io["moe_acc"][tt * 128:(tt + 1) * 128, :])
        P.dma("sync", x1t[:, buf, :], io["x1"][tt * 128:(tt + 1) * 128, :])
        ld = P.dma("sync", s1t[:, buf, :], io["s1"][tt * 128:(tt + 1) * 128, :])
        P.wait("scalar", P.sem("d_sync"), ld)
        P.emit("scalar", lambda s, o=g1, x=macc[:, buf, :]: s.activation(o, x, AF.Silu),
               inc=(P.sem("c_b"), 1))
        P.wait("vector", P.sem("c_b"))
        P.emit("vector", lambda v, o=tmpo, a=s1t[:, buf, :], b=g1: v.tensor_mul(o, a, b))
        P.fence("vector")
        P.emit("vector", lambda v, o=tmpo, a=x1t[:, buf, :], b=tmpo: v.tensor_add(o, a, b))
        P.fence("vector")
        mv = P.mark("vector")
        P.wait("sync", *mv)
        P.dma("sync", io["out"][tt * 128:(tt + 1) * 128, :], tmpo)
        P.wait("vector", P.sem("d_sync"))
        P.wait("scalar", P.sem("d_sync"))
    P.barrier()



SCRATCH = {
    "qf0": ([D, T], BF16), "kf0": ([D, T], BF16), "v0": ([T, D], BF16),
    "xnr_tm": ([T, D], BF16), "xnr_fm": ([D, T], BF16),
    "s0pre_fm": ([D, T], BF16),
    "s0": ([T, D], BF16),
    "se_fm": ([D, T], BF16),
    "lT": ([T, T], F32),
    "maxC": ([1, 1], F32),
    "slots": ([T, D], BF16),
    "processed": ([T, D], BF16),
    "x1": ([T, D], F32),
    "xn1_fm": ([D, T], BF16),
    "qf1": ([D, T], BF16), "kf1": ([D, T], BF16), "v1": ([T, 1040], BF16),
    "s1": ([T, D], BF16),
    "moe_acc": ([T, D], F32),
    "out": ([T, D], F32),
}

INPUTS = {
    "x0": ([T, D], F32),
    "W1q": ([D, D], F32), "W2q": ([D, D], F32), "W1k": ([D, D], F32), "W2k": ([D, D], F32),
    "W1v": ([D, D], F32), "W2v": ([D, D], F32),
    "b1q": ([D], F32), "b2q": ([D], F32), "b1k": ([D], F32), "b2k": ([D], F32),
    "b1v": ([D], F32), "b2v": ([D], F32),
    "ln_w": ([D], F32), "ln_b": ([D], F32),
    "gamma_x": ([D], F32), "gamma_slot": ([D], F32),
    "slot_embeds": ([T, D], F32),
    "w1": ([D, 4096], F32), "w2": ([4096, D], F32),
    "Wq": ([D, D], F32), "Wk": ([D, D], F32), "Wv": ([D, D], F32), "Wp": ([D, D], F32),
    "bq": ([D], F32), "bk": ([D], F32), "bv": ([D], F32),
    "ctab": ([T, 32], F32), "stab": ([T, 32], F32),
    "rw": ([D, 12], F32),
    "gate_w": ([8, D, 2048], F32), "up_w": ([8, D, 2048], F32), "down_w": ([8, 2048, D], F32),
}

PHASES = {
    "p1": (emit_p1, ["x0", "W1q", "W2q", "W1k", "W2k", "W1v", "W2v", "b1q", "b2q", "b1k",
                     "b2k", "b1v", "b2v", "gamma_x"],
           ["qf0", "kf0", "v0", "xnr_tm", "xnr_fm"]),
    "p2": (emit_p2, ["qf0", "kf0", "v0"], ["s0pre_fm"]),
    "p2b": (emit_p2b, ["s0pre_fm", "ln_w", "ln_b"], ["s0"]),
    "p3": (emit_p3, ["slot_embeds", "gamma_slot"], ["se_fm"]),
    "p5": (emit_p5, ["se_fm", "xnr_fm", "xnr_tm"], ["lT", "slots", "maxC"]),
    "p6": (emit_p6, ["slots", "w1", "w2"], ["processed"]),
    "p7": (emit_p7, ["lT", "processed", "x0", "s0"], ["x1"]),
    "p8": (emit_p8, ["x1", "Wq", "Wk", "Wv", "bq", "bk", "bv", "ctab", "stab"],
           ["qf1", "kf1", "v1", "xn1_fm"]),
    "p9": (emit_p9, ["qf1", "kf1", "v1", "Wp"], ["s1"]),
    "p11": (emit_p11, ["xn1_fm", "rw", "gate_w", "up_w", "down_w", "x1", "s1"],
            ["moe_acc", "out"]),
}


def build_launch(phase_names, needed_later=None):
    nc = bass.Bass("TRN2", target_bir_lowering=False)
    P = Prog(nc)
    needed_in, needed_out = set(), set()
    for pn in phase_names:
        _, ins, outs = PHASES[pn]
        for n in ins:
            if n not in needed_out:
                needed_in.add(n)
        needed_out.update(outs)
    if needed_later is None:
        ext_out = set(needed_out)
    else:
        ext_out = needed_out & needed_later
    io = {}
    for n in sorted(needed_in):
        shape, dt = INPUTS[n] if n in INPUTS else SCRATCH[n]
        io[n] = nc.dram_tensor("in_" + n, list(shape), dt, kind="ExternalInput").ap()
    for n in sorted(needed_out):
        shape, dt = SCRATCH[n]
        kind = "ExternalOutput" if n in ext_out else "Internal"
        io[n] = nc.dram_tensor(("out_" if kind == "ExternalOutput" else "scr_") + n,
                               list(shape), dt, kind=kind).ap()
    with contextlib.ExitStack() as es:
        for s in P.sems.values():
            s.h = es.enter_context(nc.semaphore(s.name))
        g = alloc_common(P, es)
        for pn in phase_names:
            emitter, _, _ = PHASES[pn]
            with contextlib.ExitStack() as pes:
                emitter(P, g, io, pes)
            P.barrier()
        block = es.enter_context(nc.Block())
        P.run(block)
    in_names = {n: ("in_" + n) for n in needed_in}
    out_names = {n: ("out_" + n) for n in sorted(ext_out)}
    return nc, in_names, out_names


def prep_core_inputs(inputs, b):
    ctab, stab = host_constants()
    micro = np.ascontiguousarray(np.transpose(np.asarray(inputs["L1_ffn_micro_w"]), (1, 0, 2))
                                 ).reshape(D, 8)
    rw = np.concatenate([np.asarray(inputs["L1_ffn_macro_w"]), micro], axis=1)
    d = {
        "x0": np.ascontiguousarray(np.asarray(inputs["L0_attn_x"])[b]),
        "W1q": inputs["L0_attn_W1q"], "W2q": inputs["L0_attn_W2q"],
        "W1k": inputs["L0_attn_W1k"], "W2k": inputs["L0_attn_W2k"],
        "W1v": inputs["L0_attn_W1v"], "W2v": inputs["L0_attn_W2v"],
        "b1q": inputs["L0_attn_b1q"], "b2q": inputs["L0_attn_b2q"],
        "b1k": inputs["L0_attn_b1k"], "b2k": inputs["L0_attn_b2k"],
        "b1v": inputs["L0_attn_b1v"], "b2v": inputs["L0_attn_b2v"],
        "ln_w": inputs["L0_attn_ln_w"], "ln_b": inputs["L0_attn_ln_b"],
        "gamma_x": inputs["L0_ffn_gamma_x"], "gamma_slot": inputs["L0_ffn_gamma_slot"],
        "slot_embeds": np.asarray(inputs["L0_ffn_slot_embeds"]).reshape(T, D),
        "w1": inputs["L0_ffn_w1"], "w2": inputs["L0_ffn_w2"],
        "Wq": inputs["L1_attn_Wq"], "Wk": inputs["L1_attn_Wk"],
        "Wv": inputs["L1_attn_Wv"], "Wp": inputs["L1_attn_Wp"],
        "bq": inputs["L1_attn_bq"], "bk": inputs["L1_attn_bk"], "bv": inputs["L1_attn_bv"],
        "ctab": ctab, "stab": stab, "rw": rw,
        "gate_w": inputs["L1_ffn_gate_w"], "up_w": inputs["L1_ffn_up_w"],
        "down_w": inputs["L1_ffn_down_w"],
    }
    return {k: np.ascontiguousarray(np.asarray(v, dtype=np.float32)) for k, v in d.items()}


LAUNCH_PLAN = [["p1", "p2", "p2b", "p3", "p5", "p6", "p7", "p8", "p9", "p11"]]


def run_launches(per_core, launch_plan=None):
    plan = launch_plan if launch_plan is not None else LAUNCH_PLAN
    for li, phase_names in enumerate(plan):
        later = set(["out"])
        for pns in plan[li + 1:]:
            for pn in pns:
                later.update(PHASES[pn][1])
        nc, in_names, out_names = build_launch(phase_names, needed_later=later)
        in_maps = []
        for b in range(8):
            m = {}
            for n, tn in in_names.items():
                m[tn] = per_core[b][n]
            in_maps.append(m)
        res = run_bass_kernel_spmd(nc, in_maps, core_ids=list(range(8)))
        for b in range(8):
            for n, tn in out_names.items():
                per_core[b][n] = np.asarray(res.results[b][tn])
    return per_core


def kernel(**inputs):
    per_core = [prep_core_inputs(inputs, b) for b in range(8)]
    run_launches(per_core)
    out = np.stack([per_core[b]["out"] for b in range(8)], axis=0)
    return out.astype(np.float32)


# version 06
# speedup vs baseline: 1.2121x; 1.1988x over previous
import numpy as np
import contextlib
import ml_dtypes
import concourse.bass as bass
import concourse.mybir as mybir
from concourse.bass_utils import run_bass_kernel_spmd

F32 = mybir.dt.float32
BF16 = mybir.dt.bfloat16
AF = mybir.ActivationFunctionType
ALU = mybir.AluOpType
AX = mybir.AxisListType

T = 4096
D = 1024
NT = T // 128
L0_SCALE = 1.0 / float(np.sqrt(np.float32(D * 2048), dtype=np.float32))
LN_EPS = 1e-5
BF = np.dtype(ml_dtypes.bfloat16)


class Sem:
    def __init__(self, name):
        self.name = name
        self.n = 0
        self.h = None


class Prog:
    ENGINES = ("sync", "tensor", "vector", "scalar", "gpsimd")

    def __init__(self, nc):
        self.nc = nc
        self.q = {e: [] for e in self.ENGINES}
        self.last = {e: None for e in self.ENGINES}
        self.sems = {}
        for nm in ("d_sync", "d_gp", "d_sc", "d_ve",
                   "x_ve", "x_sc", "x_gp", "x_te", "c_a", "c_b", "c_c", "c_d", "c_e", "c_f"):
            self.sems[nm] = Sem(nm)

    def sem(self, nm):
        return self.sems[nm]

    def emit(self, e, fn, inc=None, real=True):
        item = {"fn": fn, "inc": None}
        if inc is not None:
            sem, k = inc
            sem.n += k
            item["fn"] = (lambda eng, f=fn, s=sem, k=k: f(eng).then_inc(s.h, k))
            item["inc"] = (sem, sem.n)
        self.q[e].append(item)
        if real:
            self.last[e] = item
        return item

    def wait(self, e, sem, val=None):
        v = sem.n if val is None else val
        if v <= 0:
            return
        self.q[e].append({"fn": (lambda eng, s=sem, v=v: eng.wait_ge(s.h, v)), "inc": None})

    def dma(self, queue, dst, src, sem=None, **kw):
        s = sem if sem is not None else self.sems["d_" + ("gp" if queue == "gpsimd" else
                                                          "sc" if queue == "scalar" else
                                                          "ve" if queue == "vector" else "sync")]
        self.emit(queue, (lambda eng, d=dst, x=src, kw=kw: eng.dma_start(out=d, in_=x, **kw)),
                  inc=(s, 16))
        return s.n

    def fence(self, e):
        li = self.last[e]
        if li is None:
            return
        if li["inc"] is None:
            sname = {"vector": "x_ve", "scalar": "x_sc", "gpsimd": "x_gp", "tensor": "x_te"}.get(e)
            if sname is None:
                return
            sem = self.sems[sname]
            sem.n += 1
            f_old = li["fn"]
            li["fn"] = (lambda eng, f=f_old, s=sem: f(eng).then_inc(s.h, 1))
            li["inc"] = (sem, sem.n)
        sem, val = li["inc"]
        self.wait(e, sem, val)

    def mark(self, e):
        li = self.last[e]
        if li is None:
            return None
        if li["inc"] is None:
            sname = {"vector": "x_ve", "scalar": "x_sc", "gpsimd": "x_gp", "tensor": "x_te"}[e]
            sem = self.sems[sname]
            sem.n += 1
            f_old = li["fn"]
            li["fn"] = (lambda eng, f=f_old, s=sem: f(eng).then_inc(s.h, 1))
            li["inc"] = (sem, sem.n)
        return li["inc"]

    def barrier(self):
        marks = []
        for e in ("tensor", "vector", "scalar", "gpsimd"):
            m = self.mark(e)
            if m is not None:
                marks.append(m)
        for nm in ("d_sync", "d_gp", "d_sc", "d_ve"):
            s = self.sems[nm]
            if s.n > 0:
                marks.append((s, s.n))
        for e in self.ENGINES:
            for sem, val in marks:
                li = self.last[e]
                if li is not None and li["inc"] == (sem, val):
                    continue
                self.wait(e, sem, val)

    def run(self, block):
        for e in self.ENGINES:
            items = self.q[e]

            def mk(engine_items):
                def body(eng):
                    for it in engine_items:
                        it["fn"](eng)
                return body
            getattr(block, e)(mk(items))


def mmgroup(P, psum_ap, pairs, inc=None):
    n = len(pairs)
    for i, (lt, rh) in enumerate(pairs):
        fn = (lambda t, o=psum_ap, l=lt, r=rh, st=(i == 0), sp=(i == n - 1):
              t.matmul(o, l, r, start=st, stop=sp))
        P.emit("tensor", fn, inc=(inc if i == n - 1 else None))



def host_constants():
    HEAD_DIM = 64
    THETA = 10000.0
    inv = 1.0 / THETA ** (np.arange(0, HEAD_DIM, 2, dtype=np.float32) / HEAD_DIM)
    ang = np.arange(T, dtype=np.float32)[:, None] * inv[None, :]
    return np.cos(ang).astype(np.float32), np.sin(ang).astype(np.float32)



def alloc_common(P, es):
    nc = P.nc
    g = {}
    g["ident_b"] = es.enter_context(nc.sbuf_tensor("g_ident_b", [128, 128], BF16))[:]
    g["ident_f"] = es.enter_context(nc.sbuf_tensor("g_ident_f", [128, 128], F32))[:]
    g["ones1"] = es.enter_context(nc.sbuf_tensor("g_ones1", [1, 128], F32))[:]
    g["consts"] = es.enter_context(nc.sbuf_tensor("g_consts", [128, 8], F32))[:]
    def setup(gp, g=g):
        gp.memset(g["ident_b"][:], 0.0)
        gp.affine_select(out=g["ident_b"][:], in_=g["ident_b"][:], compare_op=ALU.not_equal,
                         fill=1.0, base=0, pattern=[[-1, 128]], channel_multiplier=1)
        gp.memset(g["ident_f"][:], 0.0)
        gp.affine_select(out=g["ident_f"][:], in_=g["ident_f"][:], compare_op=ALU.not_equal,
                         fill=1.0, base=0, pattern=[[-1, 128]], channel_multiplier=1)
        gp.memset(g["ones1"][:], 1.0)
        gp.memset(g["consts"][:, 0:1], LN_EPS)
        gp.memset(g["consts"][:, 1:2], 1e-12)
        gp.memset(g["consts"][:, 2:3], 1e-9)
        gp.memset(g["consts"][:, 3:4], 1.0)
        return gp.memset(g["consts"][:, 4:5], 1.0 / 1024.0)
    P.emit("gpsimd", setup)
    P.mark("gpsimd")
    P.barrier()
    return g


def emit_ln(P, xt, xn_out, tmp, stats, sq=None, out2=None, rho_out=None, g=None):
    nm = stats[:, 0:1]
    ssq = stats[:, 1:2]
    rstd = stats[:, 2:3]
    t3 = stats[:, 3:4]
    P.emit("vector", lambda v, o=nm, x=xt: v.tensor_reduce(o, x, AX.X, ALU.add, negate=True))
    P.fence("vector")
    P.emit("vector", lambda v, o=nm: v.tensor_scalar(out=o, in0=o, scalar1=1.0 / 1024.0,
                                                     scalar2=None, op0=ALU.mult))
    P.fence("vector")
    P.emit("vector", lambda v, o=tmp, x=xt, s=nm: v.tensor_scalar(out=o, in0=x, scalar1=s,
                                                                  scalar2=None, op0=ALU.add))
    P.fence("vector")
    mv = P.mark("vector")
    P.wait("scalar", *mv)
    P.emit("scalar", lambda s, o=ssq, x=tmp, d=sq: s.activation(d, x, AF.Square, accum_out=o))
    P.fence("scalar")
    P.emit("scalar", lambda s, o=rstd, x=ssq, c=g["consts"]: s.activation(o, x, AF.Sqrt,
                                                                          bias=c[:, 0:1], scale=1.0 / 1024.0))
    ms = P.mark("scalar")
    P.wait("vector", *ms)
    if rho_out is not None:
        P.emit("scalar", lambda s, o=t3, x=ssq: s.activation(o, x, AF.Sqrt))
        P.fence("scalar")
    ms2 = P.mark("scalar")
    P.wait("vector", *ms2)
    P.emit("vector", lambda v, o=rstd: v.reciprocal(o, o))
    P.fence("vector")
    if rho_out is not None:
        P.emit("vector", lambda v, o=t3, r=rstd: v.tensor_tensor(o, o, r, ALU.mult))
        P.fence("vector")
        P.emit("vector", lambda v, o=t3, c=g["consts"]: v.tensor_scalar(out=o, in0=o, scalar1=c[:, 1:2],
                                                                        scalar2=None, op0=ALU.max))
        P.fence("vector")
        P.emit("vector", lambda v, o=rho_out, x=t3: v.reciprocal(o, x))
        P.fence("vector")
        P.emit("vector", lambda v, o=rho_out: v.tensor_scalar(out=o, in0=o, scalar1=32.0,
                                                              scalar2=None, op0=ALU.mult))
        P.fence("vector")
    P.emit("vector", lambda v, o=xn_out, x=tmp, r=rstd: v.tensor_scalar(out=o, in0=x, scalar1=r,
                                                                        scalar2=None, op0=ALU.mult))
    if out2 is not None:
        P.emit("vector", lambda v, o=out2, x=tmp, r=rstd: v.tensor_scalar(out=o, in0=x, scalar1=r,
                                                                          scalar2=None, op0=ALU.mult))


def transpose_tile(P, psT, src_bf, dst, ident, w=128):
    P.emit("tensor", lambda t, o=psT, x=src_bf, i=ident: t.transpose(o, x, i), inc=(P.sem("c_a"), 1))
    P.wait("vector", P.sem("c_a"))
    P.emit("vector", lambda v, o=dst, x=psT: v.tensor_copy(o, x))


def emit_p1(P, g, io, es):
    nc = P.nc
    sb = lambda n, s, d=F32: es.enter_context(nc.sbuf_tensor(n, s, d))[:]
    pst = lambda n, s, d=F32: es.enter_context(nc.psum_tensor(n, s, d))[:]

    wnames = ["W1q", "W2q", "W1k", "W2k", "W1v", "W2v"]
    wb = {n: sb("p1_" + n, [128, 8, 1024], BF16) for n in wnames}
    wstage = sb("p1_wstage", [128, 8, 1024], F32)
    bia = sb("p1_bias", [128, 8, 8], F32)
    bvb = sb("p1_bvb", [128, 1024], F32)
    gxb = sb("p1_gxb", [128, 1024], F32)
    rowbuf = sb("p1_rowbuf", [1, 1024], F32)

    for i, n in enumerate(wnames):
        ld = P.dma("sync", wstage[:], io[n].rearrange("(k p) n -> p k n", p=128))
        P.wait("vector", P.sem("d_sync"), ld)
        if i > 0:
            P.fence("vector")
        P.emit("vector", lambda v, o=wb[n], x=wstage: v.tensor_copy(o[:], x[:]))
        mv = P.mark("vector")
        P.wait("sync", *mv)
    for i, bn in enumerate(["b1q", "b2q", "b1k", "b2k", "b1v", "b2v"]):
        for mc in range(8):
            P.dma("gpsimd", bia[:, i, mc:mc + 1], io[bn][mc * 128:(mc + 1) * 128].unsqueeze(1))
    P.wait("vector", P.sem("d_gp"))
    P.emit("vector", lambda v, o=bia: v.tensor_scalar(out=o[:, 1, :], in0=o[:, 1, :],
                                                      scalar1=L0_SCALE, scalar2=None, op0=ALU.mult))
    P.fence("vector")
    P.dma("gpsimd", rowbuf[0:1, :], io["b2v"][None, :])
    psV = pst("p1_psV", [128, 2, 1024], F32)
    ps_b = psV[:, 0, :]
    P.wait("tensor", P.sem("d_gp"))
    mmgroup(P, ps_b[:, 0:512], [(g["ones1"][:], rowbuf[0:1, 0:512])])
    mmgroup(P, ps_b[:, 512:1024], [(g["ones1"][:], rowbuf[0:1, 512:1024])], inc=(P.sem("c_a"), 1))
    P.wait("vector", P.sem("c_a"))
    P.emit("vector", lambda v, o=bvb, x=ps_b: v.tensor_copy(o[:], x[:]), inc=(P.sem("c_b"), 1))
    P.wait("gpsimd", P.sem("c_a"))
    ld = P.dma("gpsimd", rowbuf[0:1, :], io["gamma_x"][None, :])
    P.wait("tensor", P.sem("d_gp"), ld)
    P.wait("tensor", P.sem("c_b"))
    mmgroup(P, ps_b[:, 0:512], [(g["ones1"][:], rowbuf[0:1, 0:512])])
    mmgroup(P, ps_b[:, 512:1024], [(g["ones1"][:], rowbuf[0:1, 512:1024])], inc=(P.sem("c_a"), 1))
    P.wait("vector", P.sem("c_a"))
    P.emit("vector", lambda v, o=gxb, x=ps_b: v.tensor_copy(o[:], x[:]))

    xt = sb("p1_xt", [128, 2, 1024], F32)
    tmp = sb("p1_tmp", [128, 1024], F32)
    sqscr = sb("p1_sqscr", [128, 1024], F32)
    stats = sb("p1_stats", [128, 4], F32)
    rho = sb("p1_rho", [128, 1], F32)
    xn_bf = sb("p1_xnbf", [128, 1024], BF16)
    xnr_bf = sb("p1_xnrbf", [128, 1024], BF16)
    xnf = sb("p1_xnf", [128, 8, 512], BF16)
    xnrf = sb("p1_xnrf", [128, 8, 512], BF16)
    hq = sb("p1_hq", [128, 8, 512], BF16)
    qf = sb("p1_qf", [128, 8, 512], BF16)
    vtm = sb("p1_vtm", [128, 2, 1024], BF16)
    psT = pst("p1_psT", [128, 2, 1024], BF16)
    psH = pst("p1_psH", [128, 2, 512], F32)

    P.barrier()

    for ch in range(8):
        for tt4 in range(4):
            tt = ch * 4 + tt4
            buf = tt % 2
            if tt >= 2:
                P.fence("vector")
                mv = P.mark("vector")
                P.wait("sync", *mv)
            ld = P.dma("sync", xt[:, buf, :], io["x0"][tt * 128:(tt + 1) * 128, :])
            P.wait("vector", P.sem("d_sync"), ld)
            emit_ln(P, xt[:, buf, :], xn_bf, tmp, stats, sq=sqscr, rho_out=rho[:, 0:1], g=g)
            P.fence("vector")
            P.emit("vector", lambda v, o=xnr_bf, x=xn_bf, r=rho, w=gxb:
                   v.scalar_tensor_tensor(out=o, in0=x, scalar=r[:, 0:1], in1=w, op0=ALU.mult, op1=ALU.mult))
            P.fence("vector")
            mv = P.mark("vector")
            P.wait("tensor", *mv)
            P.wait("sync", *mv)
            P.dma("sync", io["xnr_tm"][tt * 128:(tt + 1) * 128, :], xnr_bf[:])
            P.wait("vector", P.sem("d_sync"))
            cb0 = P.sem("c_b").n
            for ki in range(16):
                k = ki % 8
                srcb = xn_bf if ki < 8 else xnr_bf
                dstb = xnf if ki < 8 else xnrf
                P.wait("tensor", P.sem("c_b"), cb0 + ki - 1 if ki >= 2 else cb0)
                P.emit("tensor", lambda t, o=psT[:, ki % 2, 0:128], x=srcb[:, k * 128:(k + 1) * 128],
                       i=g["ident_b"][:]: t.transpose(o, x, i), inc=(P.sem("c_a"), 1))
                P.wait("vector", P.sem("c_a"))
                P.emit("vector", lambda v, o=dstb[:, k, tt4 * 128:(tt4 + 1) * 128], x=psT[:, ki % 2, 0:128]:
                       v.tensor_copy(o, x), inc=(P.sem("c_b"), 1))
        mv = P.mark("vector")
        P.wait("sync", *mv)
        P.dma("sync", io["xnr_fm"].rearrange("(k p) n -> p k n", p=128)[:, :, ch * 512:(ch + 1) * 512],
              xnrf[:])
        i0 = ch * 512
        for proj, (w1n, w2n, b1c, b2c) in enumerate([("W1q", "W2q", 0, 1), ("W1k", "W2k", 2, 3),
                                                     ("W1v", "W2v", 4, 5)]):
            cc0 = P.sem("c_c").n
            for mc in range(8):
                pb = mc % 2
                P.wait("tensor", P.sem("c_c"), cc0 + mc - 1 if mc >= 2 else cc0)
                mmgroup(P, psH[:, pb, :],
                        [(wb[w1n][:, k, mc * 128:(mc + 1) * 128], xnf[:, k, :]) for k in range(8)],
                        inc=(P.sem("c_a"), 1))
                P.wait("scalar", P.sem("c_a"))
                P.emit("scalar", lambda s, o=hq[:, mc, :], x=psH[:, pb, :], b=bia[:, b1c, mc:mc + 1]:
                       s.activation(o, x, AF.Gelu, bias=b), inc=(P.sem("c_c"), 1))
            if proj < 2:
                scl = L0_SCALE if proj == 0 else 1.0
                P.wait("tensor", P.sem("c_c"))
                cd0 = P.sem("c_d").n
                for mc in range(8):
                    pb = mc % 2
                    P.wait("tensor", P.sem("c_d"), cd0 + mc - 1 if mc >= 2 else cd0)
                    mmgroup(P, psH[:, pb, :],
                            [(wb[w2n][:, k, mc * 128:(mc + 1) * 128], hq[:, k, :]) for k in range(8)],
                            inc=(P.sem("c_a"), 1))
                    P.wait("scalar", P.sem("c_a"))
                    P.emit("scalar", lambda s, o=qf[:, mc, :], x=psH[:, pb, :], b=bia[:, b2c, mc:mc + 1], sc=scl:
                           s.activation(o, x, AF.Identity, bias=b, scale=sc), inc=(P.sem("c_d"), 1))
                ms = P.mark("scalar")
                P.wait("sync", *ms)
                dst = io["qf0"] if proj == 0 else io["kf0"]
                P.dma("sync", dst.rearrange("(k p) n -> p k n", p=128)[:, :, i0:i0 + 512], qf[:])
                P.wait("scalar", P.sem("d_sync"))
            else:
                P.wait("tensor", P.sem("c_c"))
                cb0 = P.sem("c_b").n
                ds0 = P.sem("d_sync").n
                for tt4 in range(4):
                    P.wait("tensor", P.sem("c_b"), cb0 + tt4 - 1 if tt4 >= 2 else cb0)
                    if tt4 >= 2:
                        P.wait("vector", P.sem("d_sync"), ds0 + 16 * (tt4 - 1))
                    for half in range(2):
                        mmgroup(P, psV[:, tt4 % 2, half * 512:(half + 1) * 512],
                                [(hq[:, k, tt4 * 128:(tt4 + 1) * 128],
                                  wb[w2n][:, k, half * 512:(half + 1) * 512]) for k in range(8)],
                                inc=(P.sem("c_a"), 1) if half == 1 else None)
                    P.wait("vector", P.sem("c_a"))
                    P.emit("vector", lambda v, o=vtm[:, tt4 % 2, :], x=psV[:, tt4 % 2, :], b=bvb:
                           v.tensor_add(o, x, b), inc=(P.sem("c_b"), 1))
                    mv = P.mark("vector")
                    P.wait("sync", *mv)
                    tt = ch * 4 + tt4
                    P.dma("sync", io["v0"][tt * 128:(tt + 1) * 128, :], vtm[:, tt4 % 2, :])
    P.barrier()




def emit_p2(P, g, io, es):
    nc = P.nc
    sb = lambda n, s, d=F32: es.enter_context(nc.sbuf_tensor(n, s, d))[:]
    pst = lambda n, s, d=F32: es.enter_context(nc.psum_tensor(n, s, d))[:]

    k_res = sb("p2_k", [128, 8, T], BF16)
    v_res = sb("p2_v", [128, NT, D], BF16)
    qc = sb("p2_q", [128, 2, 8, 512], BF16)
    sc_sb = sb("p2_sc", [128, NT, 512], BF16)
    s0pre = sb("p2_s0pre", [128, 8, 512], BF16)
    psc = pst("p2_psc", [128, 2, 512], F32)
    pso = pst("p2_pso", [128, 2, 512], F32)

    P.dma("sync", k_res, io["kf0"].rearrange("(k p) n -> p k n", p=128))
    P.dma("sync", v_res, io["v0"].rearrange("(j p) d -> p j d", p=128))

    for ib in range(8):
        i0 = ib * 512
        njt = 4 * ib + 4
        buf = ib % 2
        if ib >= 2:
            P.fence("tensor")
            mt = P.mark("tensor")
            P.wait("sync", *mt)
        P.dma("sync", qc[:, buf, :, :], io["qf0"].rearrange("(k p) n -> p k n", p=128)[:, :, i0:i0 + 512])
        P.wait("tensor", P.sem("d_sync"))
        cc0 = P.sem("c_c").n
        for jt in range(njt):
            pb = jt % 2
            P.wait("tensor", P.sem("c_c"), cc0 + jt - 1 if jt >= 2 else cc0)
            mmgroup(P, psc[:, pb, :],
                    [(k_res[:, k, jt * 128:(jt + 1) * 128], qc[:, buf, k, :]) for k in range(8)],
                    inc=(P.sem("c_a"), 1))
            P.wait("scalar", P.sem("c_a"))
            P.wait("scalar", P.sem("c_d"))
            P.emit("scalar", lambda s, o=sc_sb[:, jt, :], x=psc[:, pb, :]:
                   s.activation(o, x, AF.Identity), inc=(P.sem("c_c"), 1))
            if jt >= 4 * ib:
                P.wait("gpsimd", P.sem("c_c"))
                P.emit("gpsimd", lambda gp, o=sc_sb[:, jt, :], b=i0 - 128 * jt:
                       gp.affine_select(out=o, in_=o, compare_op=ALU.is_ge, fill=0.0,
                                        base=b, pattern=[[1, 512]], channel_multiplier=-1),
                       inc=(P.sem("c_d"), 1))
        P.wait("tensor", P.sem("c_c"))
        P.wait("tensor", P.sem("c_d"))
        cb0 = P.sem("c_b").n
        for dv in range(8):
            pb = dv % 2
            P.wait("tensor", P.sem("c_b"), cb0 + dv - 1 if dv >= 2 else cb0)
            mmgroup(P, pso[:, pb, :],
                    [(v_res[:, 4 * ib0 + r, dv * 128:(dv + 1) * 128], sc_sb[:, 4 * ib0 + r, :])
                     for ib0 in range(ib + 1) for r in range(4)][:njt],
                    inc=(P.sem("c_a"), 1))
            P.wait("vector", P.sem("c_a"))
            P.emit("vector", lambda v, o=s0pre[:, dv, :], x=pso[:, pb, :]:
                   v.tensor_copy(o, x), inc=(P.sem("c_b"), 1))
        P.fence("vector")
        mv = P.mark("vector")
        P.wait("sync", *mv)
        P.dma("sync", io["s0pre_fm"].rearrange("(k p) n -> p k n", p=128)[:, :, i0:i0 + 512], s0pre)
        P.wait("vector", P.sem("d_sync"))
    P.barrier()


def emit_p2b(P, g, io, es):
    nc = P.nc
    sb = lambda n, s, d=F32: es.enter_context(nc.sbuf_tensor(n, s, d))[:]
    pst = lambda n, s, d=F32: es.enter_context(nc.psum_tensor(n, s, d))[:]
    sfm = sb("p2b_sfm", [128, 2, 8, 128], BF16)
    s0tm = sb("p2b_s0tm", [128, 1024], F32)
    tmp = sb("p2b_tmp", [128, 1024], F32)
    sq = sb("p2b_sq", [128, 1024], F32)
    stats = sb("p2b_stats", [128, 4], F32)
    s0bf = sb("p2b_s0bf", [128, 1024], BF16)
    lnw_b = sb("p2b_lnwb", [128, 1024], F32)
    lnb_b = sb("p2b_lnbb", [128, 1024], F32)
    rowbuf = sb("p2b_rowbuf", [1, 1024], F32)
    psT = pst("p2b_psT", [128, 2, 1024], BF16)
    psB = pst("p2b_psB", [128, 1024], F32)

    for nm, dst in (("ln_w", lnw_b), ("ln_b", lnb_b)):
        P.wait("gpsimd", P.sem("c_a"))
        ld = P.dma("gpsimd", rowbuf[0:1, :], io[nm][None, :])
        P.wait("tensor", P.sem("d_gp"), ld)
        P.wait("tensor", P.sem("c_b"))
        mmgroup(P, psB[:, 0:512], [(g["ones1"][:], rowbuf[0:1, 0:512])])
        mmgroup(P, psB[:, 512:1024], [(g["ones1"][:], rowbuf[0:1, 512:1024])], inc=(P.sem("c_a"), 1))
        P.wait("vector", P.sem("c_a"))
        P.emit("vector", lambda v, o=dst, x=psB: v.tensor_copy(o[:], x[:]), inc=(P.sem("c_b"), 1))

    for tt in range(NT):
        buf = tt % 2
        if tt >= 2:
            P.fence("tensor")
            mt = P.mark("tensor")
            P.wait("sync", *mt)
        ld = P.dma("sync", sfm[:, buf, :, :],
                   io["s0pre_fm"].rearrange("(k p) n -> p k n", p=128)[:, :, tt * 128:(tt + 1) * 128])
        P.wait("tensor", P.sem("d_sync"), ld)
        for kk in range(8):
            P.wait("tensor", P.sem("c_b"))
            P.emit("tensor", lambda t, o=psT[:, kk % 2, 0:128], x=sfm[:, buf, kk, :],
                   i=g["ident_b"]: t.transpose(o, x, i), inc=(P.sem("c_a"), 1))
            P.wait("vector", P.sem("c_a"))
            P.emit("vector", lambda v, o=s0tm[:, kk * 128:(kk + 1) * 128], x=psT[:, kk % 2, 0:128]:
                   v.tensor_copy(o, x), inc=(P.sem("c_b"), 1))
        emit_ln(P, s0tm, tmp, tmp, stats, sq=sq, g=g)
        P.fence("vector")
        P.emit("vector", lambda v, o=tmp, w=lnw_b: v.tensor_tensor(o, o, w, ALU.mult))
        P.fence("vector")
        P.emit("vector", lambda v, o=s0bf, x=tmp, b=lnb_b: v.tensor_tensor(o, x, b, ALU.add))
        P.fence("vector")
        mv = P.mark("vector")
        P.wait("sync", *mv)
        P.dma("sync", io["s0"][tt * 128:(tt + 1) * 128, :], s0bf)
        P.wait("vector", P.sem("d_sync"))
    P.barrier()


def emit_p3(P, g, io, es):
    nc = P.nc
    sb = lambda n, s, d=F32: es.enter_context(nc.sbuf_tensor(n, s, d))[:]
    pst = lambda n, s, d=F32: es.enter_context(nc.psum_tensor(n, s, d))[:]
    set_ = sb("p3_se", [128, 2, 1024], F32)
    sqd = sb("p3_sqd", [128, 1024], F32)
    ssq = sb("p3_ssq", [128, 4], F32)
    senb = sb("p3_senb", [128, 1024], BF16)
    sef = sb("p3_sef", [128, 8, 128], BF16)
    gsb = sb("p3_gsb", [128, 1024], F32)
    rowbuf = sb("p3_rowbuf", [1, 1024], F32)
    c32 = sb("p3_c32", [1, 128], F32)
    psT = pst("p3_psT", [128, 2, 1024], BF16)
    psB = pst("p3_psB", [128, 1024], F32)

    P.emit("gpsimd", lambda gp, o=c32: gp.memset(o, 32.0))
    P.wait("gpsimd", P.sem("c_a"))
    ld = P.dma("gpsimd", rowbuf[0:1, :], io["gamma_slot"][None, :])
    mg = P.mark("gpsimd")
    P.wait("tensor", *mg)
    P.wait("tensor", P.sem("d_gp"), ld)
    P.wait("tensor", P.sem("c_b"))
    mmgroup(P, psB[:, 0:512], [(c32, rowbuf[0:1, 0:512])])
    mmgroup(P, psB[:, 512:1024], [(c32, rowbuf[0:1, 512:1024])], inc=(P.sem("c_a"), 1))
    P.wait("vector", P.sem("c_a"))
    P.emit("vector", lambda v, o=gsb, x=psB: v.tensor_copy(o[:], x[:]), inc=(P.sem("c_b"), 1))

    for tt in range(NT):
        buf = tt % 2
        if tt >= 2:
            P.fence("vector")
            mv = P.mark("vector")
            P.wait("sync", *mv)
        ld = P.dma("sync", set_[:, buf, :], io["slot_embeds"][tt * 128:(tt + 1) * 128, :])
        P.wait("vector", P.sem("d_sync"), ld)
        P.emit("vector", lambda v, o=sqd, x=set_[:, buf, :]: v.tensor_mul(o, x, x))
        P.fence("vector")
        P.emit("vector", lambda v, o=ssq, x=sqd: v.tensor_reduce(o[:, 0:1], x, AX.X, ALU.add))
        P.fence("vector")
        mv = P.mark("vector")
        P.wait("scalar", *mv)
        P.emit("scalar", lambda s, o=ssq: s.activation(o[:, 1:2], o[:, 0:1], AF.Sqrt))
        ms = P.mark("scalar")
        P.wait("vector", *ms)
        P.emit("vector", lambda v, o=ssq, c=g["consts"]: v.tensor_scalar(out=o[:, 1:2], in0=o[:, 1:2],
                                                                         scalar1=c[:, 1:2], scalar2=None,
                                                                         op0=ALU.max))
        P.fence("vector")
        P.emit("vector", lambda v, o=ssq: v.reciprocal(o[:, 2:3], o[:, 1:2]))
        P.fence("vector")
        P.emit("vector", lambda v, o=senb, x=set_[:, buf, :], r=ssq, w=gsb:
               v.scalar_tensor_tensor(out=o, in0=x, scalar=r[:, 2:3], in1=w, op0=ALU.mult, op1=ALU.mult))
        P.fence("vector")
        mv = P.mark("vector")
        P.wait("tensor", *mv)
        for kk in range(8):
            P.wait("tensor", P.sem("c_b"))
            P.emit("tensor", lambda t, o=psT[:, kk % 2, 0:128], x=senb[:, kk * 128:(kk + 1) * 128],
                   i=g["ident_b"]: t.transpose(o, x, i), inc=(P.sem("c_a"), 1))
            P.wait("vector", P.sem("c_a"))
            P.emit("vector", lambda v, o=sef[:, kk, :], x=psT[:, kk % 2, 0:128]:
                   v.tensor_copy(o, x), inc=(P.sem("c_b"), 1))
        P.fence("vector")
        mv = P.mark("vector")
        P.wait("sync", *mv)
        P.dma("sync", io["se_fm"].rearrange("(k p) n -> p k n", p=128)[:, :, tt * 128:(tt + 1) * 128], sef)
        P.wait("vector", P.sem("d_sync"))
    P.barrier()




def emit_p5(P, g, io, es):
    nc = P.nc
    sb = lambda n, s, d=F32: es.enter_context(nc.sbuf_tensor(n, s, d))[:]
    pst = lambda n, s, d=F32: es.enter_context(nc.psum_tensor(n, s, d))[:]
    xnrf = sb("p5_xnrf", [128, 8, T], BF16)
    xnrt = sb("p5_xnrt", [128, NT, D], BF16)
    se_sl = sb("p5_sesl", [128, 2, 8, 128], BF16)
    lT_sb = sb("p5_lt", [128, T], F32)
    E = sb("p5_E", [128, T], BF16)
    disp = sb("p5_disp", [128, NT, 128], BF16)
    slots_bf = sb("p5_slots", [128, 1024], BF16)
    st4 = sb("p5_st4", [128, 6], F32)
    runC = sb("p5_runC", [128, 1], F32)
    cfin = sb("p5_cfin", [1, 128], F32)
    psL = pst("p5_psL", [128, 2, 512], F32)
    psS = pst("p5_psS", [128, 2, 512], F32)
    psT = pst("p5_psT", [128, 2, 1024], BF16)
    psC = pst("p5_psC", [1, 128], F32)

    P.emit("gpsimd", lambda gp, o=runC: gp.memset(o, -3e38))
    mg = P.mark("gpsimd")
    P.wait("vector", *mg)
    P.dma("sync", xnrf, io["xnr_fm"].rearrange("(k p) n -> p k n", p=128))
    P.dma("sync", xnrt, io["xnr_tm"].rearrange("(s p) d -> p s d", p=128))

    for sc in range(NT):
        buf = sc % 2
        if sc >= 2:
            P.wait("sync", P.sem("c_a"))
        ld = P.dma("sync", se_sl[:, buf, :, :],
                   io["se_fm"].rearrange("(k p) n -> p k n", p=128)[:, :, sc * 128:(sc + 1) * 128])
        P.wait("tensor", P.sem("d_sync"), ld)
        if sc >= 1:
            P.wait("vector", P.sem("d_sync"))
        cb0 = P.sem("c_b").n
        for ng in range(8):
            pb = ng % 2
            P.wait("tensor", P.sem("c_b"), cb0 + ng - 1 if ng >= 2 else cb0)
            mmgroup(P, psL[:, pb, :],
                    [(se_sl[:, buf, k, :], xnrf[:, k, ng * 512:(ng + 1) * 512]) for k in range(8)],
                    inc=(P.sem("c_a"), 1))
            P.wait("vector", P.sem("c_a"))
            P.emit("vector", lambda v, o=lT_sb[:, ng * 512:(ng + 1) * 512], x=psL[:, pb, :]:
                   v.tensor_copy(o, x), inc=(P.sem("c_b"), 1))
        P.fence("vector")
        mv = P.mark("vector")
        P.wait("sync", *mv)
        P.dma("sync", io["lT"][sc * 128:(sc + 1) * 128, :], lT_sb)
        P.emit("vector", lambda v, o=st4, x=lT_sb: v.tensor_reduce(o[:, 0:1], x, AX.X, ALU.max, negate=True))
        P.fence("vector")
        P.emit("vector", lambda v, o=st4: v.tensor_scalar(out=o[:, 3:4], in0=o[:, 0:1], scalar1=-1.0,
                                                          scalar2=None, op0=ALU.mult))
        P.fence("vector")
        P.emit("vector", lambda v, o=runC, x=st4: v.tensor_max(o, o, x[:, 3:4]))
        mv = P.mark("vector")
        P.wait("scalar", *mv)
        P.emit("scalar", lambda s, o=E, x=lT_sb, st=st4: s.activation(o, x, AF.Exp, bias=st[:, 0:1],
                                                                      accum_out=st[:, 1:2]))
        P.fence("scalar")
        ms = P.mark("scalar")
        P.wait("vector", *ms)
        P.wait("tensor", *ms)
        P.emit("vector", lambda v, o=st4: v.reciprocal(o[:, 2:3], o[:, 1:2]))
        P.fence("vector")
        mv2 = P.mark("vector")
        P.wait("scalar", *mv2)
        cb1 = P.sem("c_b").n
        for st_ in range(NT):
            P.wait("tensor", P.sem("c_b"), cb1 + st_ - 1 if st_ >= 2 else cb1)
            P.emit("tensor", lambda t, o=psT[:, st_ % 2, 0:128], x=E[:, st_ * 128:(st_ + 1) * 128],
                   i=g["ident_b"]: t.transpose(o, x, i), inc=(P.sem("c_a"), 1))
            P.wait("vector", P.sem("c_a"))
            P.emit("vector", lambda v, o=disp[:, st_, :], x=psT[:, st_ % 2, 0:128]:
                   v.tensor_copy(o, x), inc=(P.sem("c_b"), 1))
        for ng2 in range(2):
            pb = ng2 % 2
            P.wait("tensor", P.sem("c_b"))
            P.wait("tensor", P.sem("c_c"))
            mmgroup(P, psS[:, pb, :],
                    [(disp[:, st_, :], xnrt[:, st_, ng2 * 512:(ng2 + 1) * 512]) for st_ in range(NT)],
                    inc=(P.sem("c_a"), 1))
            P.wait("scalar", P.sem("c_a"))
            P.emit("scalar", lambda s, o=slots_bf[:, ng2 * 512:(ng2 + 1) * 512], x=psS[:, pb, :],
                   st=st4: s.activation(o, x, AF.Identity, scale=st[:, 2:3]), inc=(P.sem("c_c"), 1))
        ms = P.mark("scalar")
        P.wait("sync", *ms)
        P.dma("sync", io["slots"][sc * 128:(sc + 1) * 128, :], slots_bf)
        P.wait("scalar", P.sem("d_sync"))
    P.fence("vector")
    mv = P.mark("vector")
    P.wait("tensor", *mv)
    P.emit("tensor", lambda t, o=psC, x=runC, i=g["ident_f"]: t.transpose(o, x, i),
           inc=(P.sem("c_a"), 1))
    P.wait("vector", P.sem("c_a"))
    P.emit("vector", lambda v, o=cfin, x=psC: v.tensor_copy(o, x))
    P.fence("vector")
    P.emit("vector", lambda v, o=cfin: v.tensor_reduce(o[0:1, 0:1], o[0:1, :], AX.X, ALU.max))
    P.fence("vector")
    mv = P.mark("vector")
    P.wait("sync", *mv)
    P.dma("sync", io["maxC"][0:1, 0:1], cfin[0:1, 0:1])
    P.barrier()


P6_STAGE = 99


def emit_p6(P, g, io, es):
    nc = P.nc
    sb = lambda n, s, d=F32: es.enter_context(nc.sbuf_tensor(n, s, d))[:]
    pst = lambda n, s, d=F32: es.enter_context(nc.psum_tensor(n, s, d))[:]
    w1b = sb("p6_w1b", [128, 8, 4096], BF16)
    w2b = sb("p6_w2b", [128, 32, 1024], BF16)
    wstage = sb("p6_wstage", [128, 2048], F32)
    slo = sb("p6_slo", [128, 2, 4, 1024], BF16)
    sfm = sb("p6_sfm", [128, 8, 512], BF16)
    hfm = sb("p6_hfm", [128, 32, 512], BF16)
    pro = sb("p6_pro", [128, 4, 1024], BF16)
    psH = pst("p6_psH", [128, 2, 512], F32)
    psP = pst("p6_psP", [128, 2, 512], F32)
    psT = pst("p6_psT", [128, 2, 1024], BF16)

    for qi in range(8):
        for half in range(2):
            ld = P.dma("sync", wstage,
                       io["w1"].rearrange("(k p) n -> p k n", p=128)[:, qi, half * 2048:(half + 1) * 2048])
            P.wait("vector", P.sem("d_sync"), ld)
            P.emit("vector", lambda v, o=w1b[:, qi, half * 2048:(half + 1) * 2048], x=wstage:
                   v.tensor_copy(o, x))
            mv = P.mark("vector")
            P.wait("sync", *mv)
    for qi in range(16):
        ld = P.dma("sync", wstage.rearrange("p (a n) -> p a n", n=1024)[:, 0:2, :],
                   io["w2"].rearrange("(k p) n -> p k n", p=128)[:, qi * 2:(qi + 1) * 2, :])
        P.wait("vector", P.sem("d_sync"), ld)
        P.emit("vector", lambda v, o=w2b[:, qi * 2:(qi + 1) * 2, :],
               x=wstage.rearrange("p (a n) -> p a n", n=1024)[:, 0:2, :]: v.tensor_copy(o, x))
        mv = P.mark("vector")
        P.wait("sync", *mv)
    P.barrier()

    for pr in range(8):
        buf = pr % 2
        if pr >= 2:
            P.fence("tensor")
            mt = P.mark("tensor")
            P.wait("sync", *mt)
        ld = P.dma("sync", slo[:, buf, :, :],
                   io["slots"].rearrange("(n p) d -> p n d", p=128)[:, pr * 4:pr * 4 + 4, :])
        P.wait("tensor", P.sem("d_sync"), ld)
        cb0 = P.sem("c_b").n
        ti = 0
        for half in range(4):
            for kk in range(8):
                P.wait("tensor", P.sem("c_b"), cb0 + ti - 1 if ti >= 2 else cb0)
                P.emit("tensor", lambda t, o=psT[:, ti % 2, 0:128], x=slo[:, buf, half, kk * 128:(kk + 1) * 128],
                       i=g["ident_b"]: t.transpose(o, x, i), inc=(P.sem("c_a"), 1))
                P.wait("vector", P.sem("c_a"))
                P.emit("vector", lambda v, o=sfm[:, kk, half * 128:(half + 1) * 128], x=psT[:, ti % 2, 0:128]:
                       v.tensor_copy(o, x), inc=(P.sem("c_b"), 1))
                ti += 1
        P.fence("vector")
        mv = P.mark("vector")
        P.wait("tensor", *mv)
        cc0 = P.sem("c_c").n
        for mc in range(32):
            pb = mc % 2
            P.wait("tensor", P.sem("c_c"), cc0 + mc - 1 if mc >= 2 else cc0)
            mmgroup(P, psH[:, pb, :],
                    [(w1b[:, k, mc * 128:(mc + 1) * 128], sfm[:, k, :]) for k in range(8)],
                    inc=(P.sem("c_a"), 1))
            P.wait("scalar", P.sem("c_a"))
            P.emit("scalar", lambda s, o=hfm[:, mc, :], x=psH[:, pb, :]:
                   s.activation(o, x, AF.Gelu), inc=(P.sem("c_c"), 1))
        P.wait("tensor", P.sem("c_c"))
        cd0 = P.sem("c_d").n
        di = 0
        for mh in range(4):
            for ng in range(2):
                pb = di % 2
                P.wait("tensor", P.sem("c_d"), cd0 + di - 1 if di >= 2 else cd0)
                mmgroup(P, psP[:, pb, :],
                        [(hfm[:, kf, mh * 128:(mh + 1) * 128], w2b[:, kf, ng * 512:(ng + 1) * 512])
                         for kf in range(32)],
                        inc=(P.sem("c_a"), 1))
                P.wait("vector", P.sem("c_a"))
                P.emit("vector", lambda v, o=pro[:, mh, ng * 512:(ng + 1) * 512], x=psP[:, pb, :]:
                       v.tensor_copy(o, x), inc=(P.sem("c_d"), 1))
                di += 1
        P.fence("vector")
        mv = P.mark("vector")
        P.wait("sync", *mv)
        P.dma("sync", io["processed"].rearrange("(n p) d -> p n d", p=128)[:, pr * 4:pr * 4 + 4, :], pro)
        P.wait("vector", P.sem("d_sync"))
    P.barrier()


def emit_p7(P, g, io, es):
    nc = P.nc
    sb = lambda n, s, d=F32: es.enter_context(nc.sbuf_tensor(n, s, d))[:]
    pst = lambda n, s, d=F32: es.enter_context(nc.psum_tensor(n, s, d))[:]
    pro = sb("p7_pro", [128, NT, 1024], BF16)
    lsl = sb("p7_lsl", [128, 2, 128], F32)
    ltm = sb("p7_ltm", [128, T], F32)
    Etm = sb("p7_Etm", [128, T], BF16)
    Ecol = sb("p7_Ecol", [128, NT, 128], BF16)
    st2 = sb("p7_st2", [128, 4], F32)
    g0bf = sb("p7_g0", [128, 1024], BF16)
    x0t = sb("p7_x0t", [128, 2, 1024], F32)
    s0t = sb("p7_s0t", [128, 2, 1024], BF16)
    tmp = sb("p7_tmp", [128, 1024], F32)
    x1t = sb("p7_x1t", [128, 1024], F32)
    psTf = pst("p7_psTf", [128, 2, 512], F32)
    psTb = pst("p7_psTb", [128, 2, 1024], BF16)
    psG = pst("p7_psG", [128, 2, 512], F32)

    P.dma("sync", pro, io["processed"].rearrange("(n p) d -> p n d", p=128))
    P.barrier()

    for tt in range(NT):
        ca0 = P.sem("c_a").n
        cb0 = P.sem("c_b").n
        for nc_ in range(NT):
            buf = nc_ % 2
            if nc_ >= 2:
                P.wait("sync", P.sem("c_a"), ca0 + nc_ - 1)
            ld = P.dma("sync", lsl[:, buf, :], io["lT"][nc_ * 128:(nc_ + 1) * 128, tt * 128:(tt + 1) * 128])
            P.wait("tensor", P.sem("d_sync"), ld)
            P.wait("tensor", P.sem("c_b"), cb0 + nc_ - 1 if nc_ >= 2 else cb0)
            P.emit("tensor", lambda t, o=psTf[:, buf, 0:128], x=lsl[:, buf, :], i=g["ident_f"]:
                   t.transpose(o, x, i), inc=(P.sem("c_a"), 1))
            P.wait("vector", P.sem("c_a"))
            P.emit("vector", lambda v, o=ltm[:, nc_ * 128:(nc_ + 1) * 128], x=psTf[:, buf, 0:128]:
                   v.tensor_copy(o, x), inc=(P.sem("c_b"), 1))
        P.emit("vector", lambda v, o=st2, x=ltm: v.tensor_reduce(o[:, 0:1], x, AX.X, ALU.max, negate=True))
        P.fence("vector")
        mv = P.mark("vector")
        P.wait("scalar", *mv)
        P.emit("scalar", lambda s, o=Etm, x=ltm, st=st2: s.activation(o, x, AF.Exp, bias=st[:, 0:1],
                                                                      accum_out=st[:, 1:2]))
        P.fence("scalar")
        ms = P.mark("scalar")
        P.wait("vector", *ms)
        P.wait("tensor", *ms)
        P.emit("vector", lambda v, o=st2: v.reciprocal(o[:, 2:3], o[:, 1:2]))
        P.fence("vector")
        mv2 = P.mark("vector")
        P.wait("scalar", *mv2)
        cb1 = P.sem("c_b").n
        for nc_ in range(NT):
            P.wait("tensor", P.sem("c_b"), cb1 + nc_ - 1 if nc_ >= 2 else cb1)
            P.emit("tensor", lambda t, o=psTb[:, nc_ % 2, 0:128], x=Etm[:, nc_ * 128:(nc_ + 1) * 128],
                   i=g["ident_b"]: t.transpose(o, x, i), inc=(P.sem("c_a"), 1))
            P.wait("vector", P.sem("c_a"))
            P.emit("vector", lambda v, o=Ecol[:, nc_, :], x=psTb[:, nc_ % 2, 0:128]:
                   v.tensor_copy(o, x), inc=(P.sem("c_b"), 1))
        for ng in range(2):
            if ng == 0:
                P.wait("tensor", P.sem("c_b"))
                P.wait("tensor", P.sem("c_c"))
            mmgroup(P, psG[:, ng, :],
                    [(Ecol[:, nc_, :], pro[:, nc_, ng * 512:(ng + 1) * 512]) for nc_ in range(NT)],
                    inc=(P.sem("c_a"), 1) if ng == 1 else None)
        P.wait("scalar", P.sem("c_a"))
        P.emit("scalar", lambda s, o=g0bf, x=psG.rearrange("p a b -> p (a b)"), st=st2:
               s.activation(o, x, AF.Silu, scale=st[:, 2:3]), inc=(P.sem("c_c"), 1))
        buf = tt % 2
        if tt >= 2:
            P.fence("vector")
            mv = P.mark("vector")
            P.wait("sync", *mv)
        P.dma("sync", x0t[:, buf, :], io["x0"][tt * 128:(tt + 1) * 128, :])
        ld2 = P.dma("sync", s0t[:, buf, :], io["s0"][tt * 128:(tt + 1) * 128, :])
        P.wait("vector", P.sem("d_sync"), ld2)
        P.wait("vector", P.sem("c_c"))
        P.emit("vector", lambda v, o=tmp, a=s0t[:, buf, :], b=g0bf: v.tensor_mul(o, a, b))
        P.fence("vector")
        P.emit("vector", lambda v, o=x1t, a=x0t[:, buf, :], b=tmp: v.tensor_add(o, a, b))
        P.fence("vector")
        mv = P.mark("vector")
        P.wait("sync", *mv)
        P.dma("sync", io["x1"][tt * 128:(tt + 1) * 128, :], x1t)
        P.wait("vector", P.sem("d_sync"))
        P.wait("scalar", P.sem("d_sync"))
    P.barrier()


def emit_p8(P, g, io, es):
    nc = P.nc
    sb = lambda n, s, d=F32: es.enter_context(nc.sbuf_tensor(n, s, d))[:]
    pst = lambda n, s, d=F32: es.enter_context(nc.psum_tensor(n, s, d))[:]
    wts = {}
    wstage = sb("p8_wstage", [128, 8, 1024], F32)
    for wn in ("Wq", "Wk", "Wv"):
        wts[wn] = sb("p8_" + wn, [128, 8, 1024], BF16)
        ld = P.dma("sync", wstage, io[wn].rearrange("(k p) n -> p k n", p=128))
        P.wait("vector", P.sem("d_sync"), ld)
        P.emit("vector", lambda v, o=wts[wn], x=wstage: v.tensor_copy(o, x))
        mv = P.mark("vector")
        P.wait("sync", *mv)
    bb = {}
    rowbuf = sb("p8_rowbuf", [1, 1024], F32)
    c125 = sb("p8_c125", [1, 128], F32)
    psQKV = pst("p8_psQKV", [128, 2, 1024], F32)
    psB = psQKV[:, 0, :]
    P.emit("gpsimd", lambda gp, o=c125: gp.memset(o, 0.125))
    mg = P.mark("gpsimd")
    P.wait("tensor", *mg)
    for bn, lh in (("bq", c125), ("bk", g["ones1"]), ("bv", g["ones1"])):
        bb[bn] = sb("p8_b_" + bn, [128, 1024], F32)
        P.wait("gpsimd", P.sem("c_a"))
        ld = P.dma("gpsimd", rowbuf[0:1, :], io[bn][None, :])
        P.wait("tensor", P.sem("d_gp"), ld)
        P.wait("tensor", P.sem("c_b"))
        mmgroup(P, psB[:, 0:512], [(lh, rowbuf[0:1, 0:512])])
        mmgroup(P, psB[:, 512:1024], [(lh, rowbuf[0:1, 512:1024])], inc=(P.sem("c_a"), 1))
        P.wait("vector", P.sem("c_a"))
        P.emit("vector", lambda v, o=bb[bn], x=psB: v.tensor_copy(o[:], x[:]), inc=(P.sem("c_b"), 1))
    ct = sb("p8_ct", [128, NT, 32], F32)
    stb = sb("p8_st", [128, NT, 32], F32)
    P.dma("sync", ct, io["ctab"].rearrange("(tt p) i -> p tt i", p=128))
    P.dma("sync", stb, io["stab"].rearrange("(tt p) i -> p tt i", p=128))

    x1t = sb("p8_x1t", [128, 2, 1024], F32)
    tmp = sb("p8_tmp", [128, 1024], F32)
    sq = sb("p8_sq", [128, 1024], F32)
    stats = sb("p8_stats", [128, 4], F32)
    xnb = sb("p8_xnb", [128, 1024], BF16)
    xnf = sb("p8_xnf", [128, 8, 128], BF16)
    qt = sb("p8_qt", [128, 1024], F32)
    kt = sb("p8_kt", [128, 1024], F32)
    rp1 = sb("p8_rp1", [128, 512], F32)
    rp2 = sb("p8_rp2", [128, 512], F32)
    rot = sb("p8_rot", [128, 1024], BF16)
    vbf = sb("p8_vbf", [128, 1024], BF16)
    qkf = sb("p8_qkf", [128, 8, 128], BF16)
    psT = pst("p8_psT", [128, 2, 1024], BF16)
    psQ = psQKV[:, 0, :]
    psK = psQKV[:, 1, :]
    P.barrier()

    def rope(src_f32, tt):
        s4 = src_f32.rearrange("p (h i two) -> p h i two", i=32, two=2)
        r4 = rot.rearrange("p (h i two) -> p h i two", i=32, two=2)
        cr = ct[:, tt, None, :].broadcast_to([128, 16, 32])
        sr = stb[:, tt, None, :].broadcast_to([128, 16, 32])
        qe, qo = s4[:, :, :, 0], s4[:, :, :, 1]
        a1 = rp1.rearrange("p (h i) -> p h i", i=32)
        a2 = rp2.rearrange("p (h i) -> p h i", i=32)
        P.emit("vector", lambda v, o=a1, x=qe, c=cr: v.tensor_mul(o, x, c))
        P.emit("vector", lambda v, o=a2, x=qo, c=sr: v.tensor_mul(o, x, c))
        P.fence("vector")
        P.emit("vector", lambda v, o=r4[:, :, :, 0], x=a1, y=a2: v.tensor_sub(o, x, y))
        P.emit("vector", lambda v, o=a1, x=qe, c=sr: v.tensor_mul(o, x, c))
        P.emit("vector", lambda v, o=a2, x=qo, c=cr: v.tensor_mul(o, x, c))
        P.fence("vector")
        P.emit("vector", lambda v, o=r4[:, :, :, 1], x=a1, y=a2: v.tensor_add(o, x, y))
        P.fence("vector")

    for tt in range(NT):
        buf = tt % 2
        if tt >= 2:
            P.fence("vector")
            mv = P.mark("vector")
            P.wait("sync", *mv)
        ld = P.dma("sync", x1t[:, buf, :], io["x1"][tt * 128:(tt + 1) * 128, :])
        P.wait("vector", P.sem("d_sync"), ld)
        emit_ln(P, x1t[:, buf, :], xnb, tmp, stats, sq=sq, g=g)
        P.fence("vector")
        mv = P.mark("vector")
        P.wait("tensor", *mv)
        for kk in range(8):
            P.wait("tensor", P.sem("c_b"))
            P.emit("tensor", lambda t, o=psT[:, kk % 2, 0:128], x=xnb[:, kk * 128:(kk + 1) * 128],
                   i=g["ident_b"]: t.transpose(o, x, i), inc=(P.sem("c_a"), 1))
            P.wait("vector", P.sem("c_a"))
            P.emit("vector", lambda v, o=xnf[:, kk, :], x=psT[:, kk % 2, 0:128]:
                   v.tensor_copy(o, x), inc=(P.sem("c_b"), 1))
        P.fence("vector")
        mv = P.mark("vector")
        P.wait("sync", *mv)
        P.wait("tensor", *mv)
        P.dma("sync", io["xn1_fm"].rearrange("(k p) n -> p k n", p=128)[:, :, tt * 128:(tt + 1) * 128],
              xnf)
        for wn, ps in (("Wq", psQ), ("Wk", psK)):
            for ng in range(2):
                mmgroup(P, ps[ng * 512:(ng + 1) * 512] if False else ps[:, ng * 512:(ng + 1) * 512],
                        [(xnf[:, k, :], wts[wn][:, k, ng * 512:(ng + 1) * 512]) for k in range(8)],
                        inc=(P.sem("c_a"), 1) if ng == 1 else None)
        P.wait("vector", P.sem("c_a"))
        P.emit("vector", lambda v, o=qt, x=psQ, b=bb["bq"]:
               v.scalar_tensor_tensor(out=o, in0=x, scalar=0.125, in1=b, op0=ALU.mult, op1=ALU.add))
        P.emit("vector", lambda v, o=kt, x=psK, b=bb["bk"]:
               v.scalar_tensor_tensor(out=o, in0=x, scalar=1.0, in1=b, op0=ALU.mult, op1=ALU.add),
               inc=(P.sem("c_b"), 1))
        P.wait("tensor", P.sem("c_b"))
        for ng in range(2):
            mmgroup(P, psQ[:, ng * 512:(ng + 1) * 512],
                    [(xnf[:, k, :], wts["Wv"][:, k, ng * 512:(ng + 1) * 512]) for k in range(8)],
                    inc=(P.sem("c_a"), 1) if ng == 1 else None)
        P.wait("vector", P.sem("c_a"))
        P.emit("vector", lambda v, o=vbf, x=psQ, b=bb["bv"]: v.tensor_add(o, x, b), inc=(P.sem("c_b"), 1))
        P.fence("vector")
        mv = P.mark("vector")
        P.wait("sync", *mv)
        P.dma("sync", io["v1"][tt * 128:(tt + 1) * 128, :].rearrange("p (h c) -> p h c", c=65)[:, :, 0:64],
              vbf.rearrange("p (h c) -> p h c", c=64))
        for src_f32, dstio in ((qt, "qf1"), (kt, "kf1")):
            rope(src_f32, tt)
            mv = P.mark("vector")
            P.wait("tensor", *mv)
            for kk in range(8):
                P.wait("tensor", P.sem("c_b"))
                P.emit("tensor", lambda t, o=psT[:, kk % 2, 0:128], x=rot[:, kk * 128:(kk + 1) * 128],
                       i=g["ident_b"]: t.transpose(o, x, i), inc=(P.sem("c_a"), 1))
                P.wait("vector", P.sem("c_a"))
                P.emit("vector", lambda v, o=qkf[:, kk, :], x=psT[:, kk % 2, 0:128]:
                       v.tensor_copy(o, x), inc=(P.sem("c_b"), 1))
            P.fence("vector")
            mv = P.mark("vector")
            P.wait("sync", *mv)
            P.dma("sync", io[dstio].rearrange("(k p) n -> p k n", p=128)[:, :, tt * 128:(tt + 1) * 128],
                  qkf)
            P.wait("vector", P.sem("d_sync"))
    P.barrier()


def emit_p9(P, g, io, es):
    nc = P.nc
    import contextlib as _ctx
    es2 = es.enter_context(_ctx.ExitStack())
    sb = lambda n, s, d=F32: es.enter_context(nc.sbuf_tensor(n, s, d))[:]
    pst2 = lambda n, s, d=F32: es2.enter_context(nc.psum_tensor(n, s, d))[:]
    v_res = sb("p9_v", [128, NT, 1040], BF16)
    att = sb("p9_att", [128, NT, 1024], BF16)
    q2h = sb("p9_q2h", [128, 2, T], BF16)
    k2h = sb("p9_k2h", [128, 2, T], BF16)
    ET = sb("p9_ET", [128, 4, 512], BF16)
    s65 = sb("p9_s65", [65, 512], F32)
    rc = sb("p9_rc", [128, 4], F32)
    wpb = sb("p9_wpb", [128, 8, 1024], BF16)
    wstage = sb("p9_wstage", [128, 4, 1024], F32)
    s1bf = sb("p9_s1bf", [128, 1024], BF16)
    afm = sb("p9_afm", [128, 8, 128], BF16)
    psc = pst2("p9_psc", [128, 4, 512], F32)
    ps65 = pst2("p9_ps65", [65, 512], F32)
    psT4 = pst2("p9_psT4", [128, 2, 512], F32)

    for half in range(2):
        ld = P.dma("sync", wstage, io["Wp"].rearrange("(k p) n -> p k n", p=128)[:, half * 4:(half + 1) * 4, :])
        P.wait("vector", P.sem("d_sync"), ld)
        P.emit("vector", lambda v, o=wpb[:, half * 4:(half + 1) * 4, :], x=wstage: v.tensor_copy(o, x))
        mv = P.mark("vector")
        P.wait("sync", *mv)
    P.dma("sync", v_res, io["v1"].rearrange("(j p) d -> p j d", p=128))
    P.wait("gpsimd", P.sem("d_sync"))
    P.emit("gpsimd", lambda gp, o=v_res.rearrange("p j (h c) -> p j h c", c=65)[:, :, :, 64:65]:
           gp.memset(o, 1.0), inc=(P.sem("c_d"), 1))
    P.wait("tensor", P.sem("c_d"))
    P.barrier()

    for hg in range(8):
        buf = hg % 2
        if hg >= 2:
            P.wait("sync", P.sem("c_a"))
        P.dma("sync", q2h[:, buf, :], io["qf1"][hg * 128:(hg + 1) * 128, :])
        ld = P.dma("sync", k2h[:, buf, :], io["kf1"][hg * 128:(hg + 1) * 128, :])
        P.wait("tensor", P.sem("d_sync"), ld)
        for h2 in range(2):
            hb = 64 * h2
            hh = hg * 2 + h2
            for ib in range(8):
                i0 = ib * 512
                njt = min(NT, 4 * ib + 5)
                pairs = [tuple(x for x in (a, a + 1) if x < njt) for a in range(0, njt, 2)]
                cb_base = P.sem("c_b").n
                cc_base = P.sem("c_c").n
                npv = 0
                for jp, jts in enumerate(pairs):
                    bp = jp % 2
                    P.wait("tensor", P.sem("c_b"), cb_base + jp - 1 if jp >= 2 else cb_base)
                    for ji, jt in enumerate(jts):
                        P.emit("tensor", lambda t, o=psc[:, 2 * bp + ji, :],
                               l=k2h[:, buf, :][hb:hb + 64, jt * 128:(jt + 1) * 128],
                               r=q2h[:, buf, :][hb:hb + 64, i0:i0 + 512]:
                               t.matmul(o, l, r, start=True, stop=True),
                               inc=(P.sem("c_a"), 1) if ji == len(jts) - 1 else None)
                    P.wait("scalar", P.sem("c_a"))
                    P.wait("scalar", P.sem("c_c"),
                           cc_base + sum(len(p) for p in pairs[:jp - 1]) if jp >= 2 else cc_base)
                    nj = len(jts)
                    P.emit("scalar", lambda s, o=ET[:, 2 * bp:2 * bp + nj, :].rearrange("p a b -> p (a b)"),
                           x=psc[:, 2 * bp:2 * bp + nj, :].rearrange("p a b -> p (a b)"):
                           s.activation(o, x, AF.Exp), inc=(P.sem("c_b"), 1))
                    for ji, jt in enumerate(jts):
                        masked = jt >= 4 * ib
                        if masked:
                            P.wait("gpsimd", P.sem("c_b"))
                            P.emit("gpsimd", lambda gp, o=ET[:, 2 * bp + ji, :], b=i0 + 64 - 128 * jt:
                                   gp.affine_select(out=o, in_=o, compare_op=ALU.is_ge, fill=0.0,
                                                    base=b, pattern=[[1, 512]], channel_multiplier=-1),
                                   inc=(P.sem("c_d"), 1))
                            P.wait("tensor", P.sem("c_d"))
                        else:
                            P.wait("tensor", P.sem("c_b"))
                        P.emit("tensor", lambda t, o=ps65, l=v_res[:, jt, 65 * hh:65 * hh + 65],
                               r=ET[:, 2 * bp + ji, :], st=(jt == 0), sp=(jt == njt - 1):
                               t.matmul(o, l, r, start=st, stop=sp),
                               inc=(P.sem("c_c"), 1))
                        npv += 1
                P.wait("vector", P.sem("c_c"))
                P.emit("vector", lambda v, o=s65, x=ps65: v.tensor_copy(o, x), inc=(P.sem("c_e"), 1))
                P.wait("tensor", P.sem("c_e"))
                cf_base = P.sem("c_f").n
                for qq in range(4):
                    P.wait("tensor", P.sem("c_f"), cf_base + qq - 1 if qq >= 2 else cf_base)
                    P.emit("tensor", lambda t, o=psT4[:, qq % 2, 0:65], x=s65[:, qq * 128:(qq + 1) * 128],
                           i=g["ident_f"][0:65, 0:65]: t.transpose(o, x, i), inc=(P.sem("c_a"), 1))
                    P.wait("vector", P.sem("c_a"))
                    P.emit("vector", lambda v, o=rc[:, qq:qq + 1], x=psT4[:, qq % 2, 64:65]:
                           v.reciprocal(o, x))
                    P.fence("vector")
                    tt = ib * 4 + qq
                    P.emit("vector", lambda v, o=att[:, tt, 64 * hh:64 * hh + 64], x=psT4[:, qq % 2, 0:64],
                           r=rc[:, qq:qq + 1]: v.tensor_scalar(out=o, in0=x, scalar1=r, scalar2=None,
                                                               op0=ALU.mult),
                           inc=(P.sem("c_f"), 1))
    P.barrier()
    es2.close()
    pst = lambda n, s, d=F32: es.enter_context(nc.psum_tensor(n, s, d))[:]
    psP = pst("p9_psP", [128, 2, 1024], F32)
    psTb = pst("p9_psTb", [128, 2, 1024], BF16)
    for tt in range(NT):
        cb0 = P.sem("c_b").n
        for kk in range(8):
            P.wait("tensor", P.sem("c_b"), cb0 + kk - 1 if kk >= 2 else cb0)
            P.emit("tensor", lambda t, o=psTb[:, kk % 2, 0:128], x=att[:, tt, kk * 128:(kk + 1) * 128],
                   i=g["ident_b"]: t.transpose(o, x, i), inc=(P.sem("c_a"), 1))
            P.wait("vector", P.sem("c_a"))
            P.emit("vector", lambda v, o=afm[:, kk, :], x=psTb[:, kk % 2, 0:128]:
                   v.tensor_copy(o, x), inc=(P.sem("c_b"), 1))
        P.fence("vector")
        mv = P.mark("vector")
        P.wait("tensor", *mv)
        if tt >= 2:
            P.wait("tensor", P.sem("c_f"))
        for ng in range(2):
            mmgroup(P, psP[:, tt % 2, ng * 512:(ng + 1) * 512],
                    [(afm[:, k, :], wpb[:, k, ng * 512:(ng + 1) * 512]) for k in range(8)],
                    inc=(P.sem("c_a"), 1) if ng == 1 else None)
        P.wait("vector", P.sem("c_a"))
        P.wait("vector", P.sem("d_sync"))
        P.emit("vector", lambda v, o=s1bf, x=psP[:, tt % 2, :]: v.tensor_copy(o, x), inc=(P.sem("c_f"), 1))
        mv = P.mark("vector")
        P.wait("sync", *mv)
        P.dma("sync", io["s1"][tt * 128:(tt + 1) * 128, :], s1bf)
    P.barrier()


def emit_p11(P, g, io, es):
    nc = P.nc
    import contextlib as _ctx
    es2 = es.enter_context(_ctx.ExitStack())
    sb = lambda n, s, d=F32: es2.enter_context(nc.sbuf_tensor(n, s, d))[:]
    pst = lambda n, s, d=F32: es2.enter_context(nc.psum_tensor(n, s, d))[:]
    xnf = sb("p11_xnf", [128, 8, T], BF16)
    gwb = sb("p11_gwb", [128, 8, 2048], BF16)
    uwb = sb("p11_uwb", [128, 8, 2048], BF16)
    dwb = sb("p11_dwb", [128, 16, 1024], BF16)
    wstage = sb("p11_wstage", [128, 4096], F32)
    hfm = sb("p11_hfm", [128, 16, 512], BF16)
    gsil = sb("p11_gsil", [128, 2, 512], BF16)
    dsb = sb("p11_dsb", [128, 2, 1024], F32)
    rw_s = sb("p11_rws", [128, 8, 12], BF16)
    rw_f = sb("p11_rwf", [128, 8, 12], F32)
    s12 = sb("p11_s12", [128, 12], F32)
    oh4 = sb("p11_oh4", [128, 4], F32)
    u4 = sb("p11_u4", [128, 4], F32)
    w8 = sb("p11_w8", [128, NT, 8], F32)
    st1 = sb("p11_st1", [128, 2], F32)
    psA = pst("p11_psA", [128, 4, 512], F32)
    psD = pst("p11_psD", [128, 2, 1024], F32)
    psR = psA[:, 0, :]

    P.dma("sync", xnf, io["xn1_fm"].rearrange("(k p) n -> p k n", p=128))
    P.dma("sync", rw_f, io["rw"].rearrange("(k p) n -> p k n", p=128))
    P.wait("vector", P.sem("d_sync"))
    P.emit("vector", lambda v, o=rw_s, x=rw_f: v.tensor_copy(o, x))
    P.barrier()
    for tt in range(NT):
        P.wait("tensor", P.sem("c_b"))
        mmgroup(P, psR[:, 0:12],
                [(xnf[:, k, tt * 128:(tt + 1) * 128], rw_s[:, k, :]) for k in range(8)],
                inc=(P.sem("c_a"), 1))
        P.wait("scalar", P.sem("c_a"))
        P.emit("scalar", lambda s, o=s12, x=psR[:, 0:12]: s.activation(o, x, AF.Sigmoid),
               inc=(P.sem("c_b"), 1))
        P.wait("vector", P.sem("c_b"))
        P.emit("vector", lambda v, o=st1, x=s12: v.tensor_reduce(o[:, 0:1], x[:, 0:4], AX.X, ALU.max))
        P.fence("vector")
        P.emit("vector", lambda v, o=oh4, x=s12, r=st1: v.tensor_scalar(out=o, in0=x[:, 0:4],
                                                                        scalar1=r[:, 0:1], scalar2=None,
                                                                        op0=ALU.is_ge))
        P.emit("vector", lambda v, o=u4, x=s12: v.tensor_add(o, x.rearrange("p (g e) -> p g e", e=2)[:, 4 // 2:, 0]
                                                             if False else x[:, 4:12:2], x[:, 5:12:2]))
        P.fence("vector")
        P.emit("vector", lambda v, o=u4: v.tensor_scalar(out=o, in0=o, scalar1=1e-9, scalar2=None,
                                                         op0=ALU.add))
        P.fence("vector")
        P.emit("vector", lambda v, o=u4: v.reciprocal(o, o))
        P.fence("vector")
        P.emit("vector", lambda v, o=u4, a=oh4: v.tensor_mul(o, o, a))
        P.fence("vector")
        P.emit("vector", lambda v, o=w8.rearrange("p t (g e) -> p t g e", e=2)[:, tt, :, :],
               x=s12.rearrange("p (g e) -> p g e", e=2)[:, 2:6, :], f=u4[:, :, None].broadcast_to([128, 4, 2]):
               v.tensor_mul(o, x, f))
        P.fence("vector")
    P.barrier()

    NTB = 8
    for e in range(8):
        for wn, dst in (("gate_w", gwb), ("up_w", uwb)):
            for qtr in range(4):
                P.fence("vector")
                mv = P.mark("vector")
                P.wait("sync", *mv)
                ld = P.dma("sync", wstage.rearrange("p (a n) -> p a n", n=2048),
                           io[wn][e].rearrange("(k p) n -> p k n", p=128)[:, qtr * 2:(qtr + 1) * 2, :])
                P.wait("vector", P.sem("d_sync"), ld)
                P.emit("vector", lambda v, o=dst[:, qtr * 2:(qtr + 1) * 2, :],
                       x=wstage.rearrange("p (a n) -> p a n", n=2048): v.tensor_copy(o, x))
        for qtr in range(4):
            P.fence("vector")
            mv = P.mark("vector")
            P.wait("sync", *mv)
            ld = P.dma("sync", wstage.rearrange("p (a n) -> p a n", n=1024),
                       io["down_w"][e].rearrange("(k p) n -> p k n", p=128)[:, qtr * 4:(qtr + 1) * 4, :])
            P.wait("vector", P.sem("d_sync"), ld)
            P.emit("vector", lambda v, o=dwb[:, qtr * 4:(qtr + 1) * 4, :],
                   x=wstage.rearrange("p (a n) -> p a n", n=1024): v.tensor_copy(o, x))
        P.fence("vector")
        mv = P.mark("vector")
        P.wait("tensor", *mv)
        for tb in range(NTB):
            t0 = tb * 512
            cb0 = P.sem("c_b").n
            cc0 = P.sem("c_c").n
            for mc in range(16):
                pb = mc % 2
                P.wait("tensor", P.sem("c_b"), cb0 + mc - 1 if mc >= 2 else cb0)
                mmgroup(P, psA[:, pb, :],
                        [(gwb[:, k, mc * 128:(mc + 1) * 128], xnf[:, k, t0:t0 + 512]) for k in range(8)],
                        inc=(P.sem("c_a"), 1))
                P.wait("scalar", P.sem("c_a"))
                P.wait("scalar", P.sem("c_c"), cc0 + mc - 1 if mc >= 2 else cc0)
                P.emit("scalar", lambda s, o=gsil[:, pb, :], x=psA[:, pb, :]: s.activation(o, x, AF.Silu),
                       inc=(P.sem("c_b"), 1))
                P.wait("tensor", P.sem("c_c"), cc0 + mc - 1 if mc >= 2 else cc0)
                mmgroup(P, psA[:, 2 + pb, :],
                        [(uwb[:, k, mc * 128:(mc + 1) * 128], xnf[:, k, t0:t0 + 512]) for k in range(8)],
                        inc=(P.sem("c_a"), 1))
                P.wait("vector", P.sem("c_a"))
                P.wait("vector", P.sem("c_b"))
                P.emit("vector", lambda v, o=hfm[:, mc, :], a=gsil[:, pb, :], b=psA[:, 2 + pb, :]:
                       v.tensor_mul(o, a, b), inc=(P.sem("c_c"), 1))
            ce0 = P.sem("c_e").n
            P.wait("tensor", P.sem("c_c"))
            for ts8 in range(4):
                pb = ts8 % 2
                tt = tb * 4 + ts8
                P.wait("tensor", P.sem("c_e"), ce0 + ts8 - 1 if ts8 >= 2 else ce0)
                for ngd in range(2):
                    mmgroup(P, psD[:, pb, ngd * 512:(ngd + 1) * 512],
                            [(hfm[:, kf, ts8 * 128:(ts8 + 1) * 128], dwb[:, kf, ngd * 512:(ngd + 1) * 512])
                             for kf in range(16)],
                            inc=(P.sem("c_a"), 1) if ngd == 1 else None)
                P.wait("vector", P.sem("c_a"))
                P.wait("vector", P.sem("d_gp"))
                P.emit("vector", lambda v, o=dsb[:, pb, :], x=psD[:, pb, :], w=w8[:, tt, e:e + 1]:
                       v.tensor_scalar(out=o, in0=x, scalar1=w, scalar2=None, op0=ALU.mult),
                       inc=(P.sem("c_e"), 1))
                mv = P.mark("vector")
                P.wait("gpsimd", *mv)
                if e == 0:
                    P.dma("gpsimd", io["moe_acc"][tt * 128:(tt + 1) * 128, :], dsb[:, pb, :])
                else:
                    P.dma("gpsimd", io["moe_acc"][tt * 128:(tt + 1) * 128, :], dsb[:, pb, :],
                          accum_op=ALU.add)
    P.barrier()
    es2.close()
    sb = lambda n, s, d=F32: es.enter_context(nc.sbuf_tensor(n, s, d))[:]
    macc = sb("p11_macc", [128, 2, 1024], F32)
    x1t = sb("p11_x1t", [128, 2, 1024], F32)
    s1t = sb("p11_s1t", [128, 2, 1024], BF16)
    g1 = sb("p11_g1", [128, 1024], F32)
    tmpo = sb("p11_tmpo", [128, 1024], F32)
    for tt in range(NT):
        buf = tt % 2
        if tt >= 2:
            P.fence("vector")
            mv = P.mark("vector")
            P.wait("sync", *mv)
        P.dma("sync", macc[:, buf, :], io["moe_acc"][tt * 128:(tt + 1) * 128, :])
        P.dma("sync", x1t[:, buf, :], io["x1"][tt * 128:(tt + 1) * 128, :])
        ld = P.dma("sync", s1t[:, buf, :], io["s1"][tt * 128:(tt + 1) * 128, :])
        P.wait("scalar", P.sem("d_sync"), ld)
        P.emit("scalar", lambda s, o=g1, x=macc[:, buf, :]: s.activation(o, x, AF.Silu),
               inc=(P.sem("c_b"), 1))
        P.wait("vector", P.sem("c_b"))
        P.emit("vector", lambda v, o=tmpo, a=s1t[:, buf, :], b=g1: v.tensor_mul(o, a, b))
        P.fence("vector")
        P.emit("vector", lambda v, o=tmpo, a=x1t[:, buf, :], b=tmpo: v.tensor_add(o, a, b))
        P.fence("vector")
        mv = P.mark("vector")
        P.wait("sync", *mv)
        P.dma("sync", io["out"][tt * 128:(tt + 1) * 128, :], tmpo)
        P.wait("vector", P.sem("d_sync"))
        P.wait("scalar", P.sem("d_sync"))
    P.barrier()



SCRATCH = {
    "qf0": ([D, T], BF16), "kf0": ([D, T], BF16), "v0": ([T, D], BF16),
    "xnr_tm": ([T, D], BF16), "xnr_fm": ([D, T], BF16),
    "s0pre_fm": ([D, T], BF16),
    "s0": ([T, D], BF16),
    "se_fm": ([D, T], BF16),
    "lT": ([T, T], F32),
    "maxC": ([1, 1], F32),
    "slots": ([T, D], BF16),
    "processed": ([T, D], BF16),
    "x1": ([T, D], F32),
    "xn1_fm": ([D, T], BF16),
    "qf1": ([D, T], BF16), "kf1": ([D, T], BF16), "v1": ([T, 1040], BF16),
    "s1": ([T, D], BF16),
    "moe_acc": ([T, D], F32),
    "out": ([T, D], F32),
}

INPUTS = {
    "x0": ([T, D], F32),
    "W1q": ([D, D], F32), "W2q": ([D, D], F32), "W1k": ([D, D], F32), "W2k": ([D, D], F32),
    "W1v": ([D, D], F32), "W2v": ([D, D], F32),
    "b1q": ([D], F32), "b2q": ([D], F32), "b1k": ([D], F32), "b2k": ([D], F32),
    "b1v": ([D], F32), "b2v": ([D], F32),
    "ln_w": ([D], F32), "ln_b": ([D], F32),
    "gamma_x": ([D], F32), "gamma_slot": ([D], F32),
    "slot_embeds": ([T, D], F32),
    "w1": ([D, 4096], F32), "w2": ([4096, D], F32),
    "Wq": ([D, D], F32), "Wk": ([D, D], F32), "Wv": ([D, D], F32), "Wp": ([D, D], F32),
    "bq": ([D], F32), "bk": ([D], F32), "bv": ([D], F32),
    "ctab": ([T, 32], F32), "stab": ([T, 32], F32),
    "rw": ([D, 12], F32),
    "gate_w": ([8, D, 2048], F32), "up_w": ([8, D, 2048], F32), "down_w": ([8, 2048, D], F32),
}

PHASES = {
    "p1": (emit_p1, ["x0", "W1q", "W2q", "W1k", "W2k", "W1v", "W2v", "b1q", "b2q", "b1k",
                     "b2k", "b1v", "b2v", "gamma_x"],
           ["qf0", "kf0", "v0", "xnr_tm", "xnr_fm"]),
    "p2": (emit_p2, ["qf0", "kf0", "v0"], ["s0pre_fm"]),
    "p2b": (emit_p2b, ["s0pre_fm", "ln_w", "ln_b"], ["s0"]),
    "p3": (emit_p3, ["slot_embeds", "gamma_slot"], ["se_fm"]),
    "p5": (emit_p5, ["se_fm", "xnr_fm", "xnr_tm"], ["lT", "slots", "maxC"]),
    "p6": (emit_p6, ["slots", "w1", "w2"], ["processed"]),
    "p7": (emit_p7, ["lT", "processed", "x0", "s0"], ["x1"]),
    "p8": (emit_p8, ["x1", "Wq", "Wk", "Wv", "bq", "bk", "bv", "ctab", "stab"],
           ["qf1", "kf1", "v1", "xn1_fm"]),
    "p9": (emit_p9, ["qf1", "kf1", "v1", "Wp"], ["s1"]),
    "p11": (emit_p11, ["xn1_fm", "rw", "gate_w", "up_w", "down_w", "x1", "s1"],
            ["moe_acc", "out"]),
}


def build_launch(phase_names, needed_later=None):
    nc = bass.Bass("TRN2", target_bir_lowering=False)
    P = Prog(nc)
    needed_in, needed_out = set(), set()
    for pn in phase_names:
        _, ins, outs = PHASES[pn]
        for n in ins:
            if n not in needed_out:
                needed_in.add(n)
        needed_out.update(outs)
    if needed_later is None:
        ext_out = set(needed_out)
    else:
        ext_out = needed_out & needed_later
    io = {}
    for n in sorted(needed_in):
        shape, dt = INPUTS[n] if n in INPUTS else SCRATCH[n]
        io[n] = nc.dram_tensor("in_" + n, list(shape), dt, kind="ExternalInput").ap()
    for n in sorted(needed_out):
        shape, dt = SCRATCH[n]
        kind = "ExternalOutput" if n in ext_out else "Internal"
        io[n] = nc.dram_tensor(("out_" if kind == "ExternalOutput" else "scr_") + n,
                               list(shape), dt, kind=kind).ap()
    with contextlib.ExitStack() as es:
        for s in P.sems.values():
            s.h = es.enter_context(nc.semaphore(s.name))
        g = alloc_common(P, es)
        for pn in phase_names:
            emitter, _, _ = PHASES[pn]
            with contextlib.ExitStack() as pes:
                emitter(P, g, io, pes)
            P.barrier()
        block = es.enter_context(nc.Block())
        P.run(block)
    in_names = {n: ("in_" + n) for n in needed_in}
    out_names = {n: ("out_" + n) for n in sorted(ext_out)}
    return nc, in_names, out_names


def prep_core_inputs(inputs, b):
    ctab, stab = host_constants()
    micro = np.ascontiguousarray(np.transpose(np.asarray(inputs["L1_ffn_micro_w"]), (1, 0, 2))
                                 ).reshape(D, 8)
    rw = np.concatenate([np.asarray(inputs["L1_ffn_macro_w"]), micro], axis=1)
    d = {
        "x0": np.ascontiguousarray(np.asarray(inputs["L0_attn_x"])[b]),
        "W1q": inputs["L0_attn_W1q"], "W2q": inputs["L0_attn_W2q"],
        "W1k": inputs["L0_attn_W1k"], "W2k": inputs["L0_attn_W2k"],
        "W1v": inputs["L0_attn_W1v"], "W2v": inputs["L0_attn_W2v"],
        "b1q": inputs["L0_attn_b1q"], "b2q": inputs["L0_attn_b2q"],
        "b1k": inputs["L0_attn_b1k"], "b2k": inputs["L0_attn_b2k"],
        "b1v": inputs["L0_attn_b1v"], "b2v": inputs["L0_attn_b2v"],
        "ln_w": inputs["L0_attn_ln_w"], "ln_b": inputs["L0_attn_ln_b"],
        "gamma_x": inputs["L0_ffn_gamma_x"], "gamma_slot": inputs["L0_ffn_gamma_slot"],
        "slot_embeds": np.asarray(inputs["L0_ffn_slot_embeds"]).reshape(T, D),
        "w1": inputs["L0_ffn_w1"], "w2": inputs["L0_ffn_w2"],
        "Wq": inputs["L1_attn_Wq"], "Wk": inputs["L1_attn_Wk"],
        "Wv": inputs["L1_attn_Wv"], "Wp": inputs["L1_attn_Wp"],
        "bq": inputs["L1_attn_bq"], "bk": inputs["L1_attn_bk"], "bv": inputs["L1_attn_bv"],
        "ctab": ctab, "stab": stab, "rw": rw,
        "gate_w": inputs["L1_ffn_gate_w"], "up_w": inputs["L1_ffn_up_w"],
        "down_w": inputs["L1_ffn_down_w"],
    }
    return {k: np.ascontiguousarray(np.asarray(v, dtype=np.float32)) for k, v in d.items()}


LAUNCH_PLAN = [["p1", "p2", "p2b", "p3", "p5", "p6", "p7", "p8", "p9", "p11"]]


def run_launches(per_core, launch_plan=None):
    plan = launch_plan if launch_plan is not None else LAUNCH_PLAN
    for li, phase_names in enumerate(plan):
        later = set(["out"])
        for pns in plan[li + 1:]:
            for pn in pns:
                later.update(PHASES[pn][1])
        nc, in_names, out_names = build_launch(phase_names, needed_later=later)
        in_maps = []
        for b in range(8):
            m = {}
            for n, tn in in_names.items():
                m[tn] = per_core[b][n]
            in_maps.append(m)
        res = run_bass_kernel_spmd(nc, in_maps, core_ids=list(range(8)))
        for b in range(8):
            for n, tn in out_names.items():
                per_core[b][n] = np.asarray(res.results[b][tn])
    return per_core


def kernel(**inputs):
    per_core = [prep_core_inputs(inputs, b) for b in range(8)]
    run_launches(per_core)
    out = np.stack([per_core[b]["out"] for b in range(8)], axis=0)
    return out.astype(np.float32)


# version 07 (final)
# speedup vs baseline: 1.2616x; 1.0408x over previous
import numpy as np
import contextlib
import ml_dtypes
import concourse.bass as bass
import concourse.mybir as mybir
from concourse.bass_utils import run_bass_kernel_spmd

F32 = mybir.dt.float32
BF16 = mybir.dt.bfloat16
AF = mybir.ActivationFunctionType
ALU = mybir.AluOpType
AX = mybir.AxisListType

T = 4096
D = 1024
NT = T // 128
L0_SCALE = 1.0 / float(np.sqrt(np.float32(D * 2048), dtype=np.float32))
LN_EPS = 1e-5
BF = np.dtype(ml_dtypes.bfloat16)


class Sem:
    def __init__(self, name):
        self.name = name
        self.n = 0
        self.h = None


class Prog:
    ENGINES = ("sync", "tensor", "vector", "scalar", "gpsimd")

    def __init__(self, nc):
        self.nc = nc
        self.q = {e: [] for e in self.ENGINES}
        self.last = {e: None for e in self.ENGINES}
        self.sems = {}
        for nm in ("d_sync", "d_gp", "d_sc", "d_ve",
                   "x_ve", "x_sc", "x_gp", "x_te", "c_a", "c_b", "c_c", "c_d", "c_e", "c_f"):
            self.sems[nm] = Sem(nm)

    def sem(self, nm):
        return self.sems[nm]

    def emit(self, e, fn, inc=None, real=True):
        item = {"fn": fn, "inc": None}
        if inc is not None:
            sem, k = inc
            sem.n += k
            item["fn"] = (lambda eng, f=fn, s=sem, k=k: f(eng).then_inc(s.h, k))
            item["inc"] = (sem, sem.n)
        self.q[e].append(item)
        if real:
            self.last[e] = item
        return item

    def wait(self, e, sem, val=None):
        v = sem.n if val is None else val
        if v <= 0:
            return
        self.q[e].append({"fn": (lambda eng, s=sem, v=v: eng.wait_ge(s.h, v)), "inc": None})

    def dma(self, queue, dst, src, sem=None, **kw):
        s = sem if sem is not None else self.sems["d_" + ("gp" if queue == "gpsimd" else
                                                          "sc" if queue == "scalar" else
                                                          "ve" if queue == "vector" else "sync")]
        self.emit(queue, (lambda eng, d=dst, x=src, kw=kw: eng.dma_start(out=d, in_=x, **kw)),
                  inc=(s, 16))
        return s.n

    def fence(self, e):
        li = self.last[e]
        if li is None:
            return
        if li["inc"] is None:
            sname = {"vector": "x_ve", "scalar": "x_sc", "gpsimd": "x_gp", "tensor": "x_te"}.get(e)
            if sname is None:
                return
            sem = self.sems[sname]
            sem.n += 1
            f_old = li["fn"]
            li["fn"] = (lambda eng, f=f_old, s=sem: f(eng).then_inc(s.h, 1))
            li["inc"] = (sem, sem.n)
        sem, val = li["inc"]
        self.wait(e, sem, val)

    def mark(self, e):
        li = self.last[e]
        if li is None:
            return None
        if li["inc"] is None:
            sname = {"vector": "x_ve", "scalar": "x_sc", "gpsimd": "x_gp", "tensor": "x_te"}[e]
            sem = self.sems[sname]
            sem.n += 1
            f_old = li["fn"]
            li["fn"] = (lambda eng, f=f_old, s=sem: f(eng).then_inc(s.h, 1))
            li["inc"] = (sem, sem.n)
        return li["inc"]

    def barrier(self):
        marks = []
        for e in ("tensor", "vector", "scalar", "gpsimd"):
            m = self.mark(e)
            if m is not None:
                marks.append(m)
        for nm in ("d_sync", "d_gp", "d_sc", "d_ve"):
            s = self.sems[nm]
            if s.n > 0:
                marks.append((s, s.n))
        for e in self.ENGINES:
            for sem, val in marks:
                li = self.last[e]
                if li is not None and li["inc"] == (sem, val):
                    continue
                self.wait(e, sem, val)

    def run(self, block):
        for e in self.ENGINES:
            items = self.q[e]

            def mk(engine_items):
                def body(eng):
                    for it in engine_items:
                        it["fn"](eng)
                return body
            getattr(block, e)(mk(items))


def mmgroup(P, psum_ap, pairs, inc=None):
    n = len(pairs)
    for i, (lt, rh) in enumerate(pairs):
        fn = (lambda t, o=psum_ap, l=lt, r=rh, st=(i == 0), sp=(i == n - 1):
              t.matmul(o, l, r, start=st, stop=sp))
        P.emit("tensor", fn, inc=(inc if i == n - 1 else None))



def host_constants():
    HEAD_DIM = 64
    THETA = 10000.0
    inv = 1.0 / THETA ** (np.arange(0, HEAD_DIM, 2, dtype=np.float32) / HEAD_DIM)
    ang = np.arange(T, dtype=np.float32)[:, None] * inv[None, :]
    return np.cos(ang).astype(np.float32), np.sin(ang).astype(np.float32)



def alloc_common(P, es):
    nc = P.nc
    g = {}
    g["ident_b"] = es.enter_context(nc.sbuf_tensor("g_ident_b", [128, 128], BF16))[:]
    g["ident_f"] = es.enter_context(nc.sbuf_tensor("g_ident_f", [128, 128], F32))[:]
    g["ones1"] = es.enter_context(nc.sbuf_tensor("g_ones1", [1, 128], F32))[:]
    g["consts"] = es.enter_context(nc.sbuf_tensor("g_consts", [128, 8], F32))[:]
    def setup(gp, g=g):
        gp.memset(g["ident_b"][:], 0.0)
        gp.affine_select(out=g["ident_b"][:], in_=g["ident_b"][:], compare_op=ALU.not_equal,
                         fill=1.0, base=0, pattern=[[-1, 128]], channel_multiplier=1)
        gp.memset(g["ident_f"][:], 0.0)
        gp.affine_select(out=g["ident_f"][:], in_=g["ident_f"][:], compare_op=ALU.not_equal,
                         fill=1.0, base=0, pattern=[[-1, 128]], channel_multiplier=1)
        gp.memset(g["ones1"][:], 1.0)
        gp.memset(g["consts"][:, 0:1], LN_EPS)
        gp.memset(g["consts"][:, 1:2], 1e-12)
        gp.memset(g["consts"][:, 2:3], 1e-9)
        gp.memset(g["consts"][:, 3:4], 1.0)
        return gp.memset(g["consts"][:, 4:5], 1.0 / 1024.0)
    P.emit("gpsimd", setup)
    P.mark("gpsimd")
    P.barrier()
    return g


def emit_ln(P, xt, xn_out, tmp, stats, sq=None, out2=None, rho_out=None, g=None):
    nm = stats[:, 0:1]
    ssq = stats[:, 1:2]
    rstd = stats[:, 2:3]
    t3 = stats[:, 3:4]
    P.emit("vector", lambda v, o=nm, x=xt: v.tensor_reduce(o, x, AX.X, ALU.add, negate=True))
    P.fence("vector")
    P.emit("vector", lambda v, o=nm: v.tensor_scalar(out=o, in0=o, scalar1=1.0 / 1024.0,
                                                     scalar2=None, op0=ALU.mult))
    P.fence("vector")
    P.emit("vector", lambda v, o=tmp, x=xt, s=nm: v.tensor_scalar(out=o, in0=x, scalar1=s,
                                                                  scalar2=None, op0=ALU.add))
    P.fence("vector")
    mv = P.mark("vector")
    P.wait("scalar", *mv)
    P.emit("scalar", lambda s, o=ssq, x=tmp, d=sq: s.activation(d, x, AF.Square, accum_out=o))
    P.fence("scalar")
    P.emit("scalar", lambda s, o=rstd, x=ssq, c=g["consts"]: s.activation(o, x, AF.Sqrt,
                                                                          bias=c[:, 0:1], scale=1.0 / 1024.0))
    ms = P.mark("scalar")
    P.wait("vector", *ms)
    if rho_out is not None:
        P.emit("scalar", lambda s, o=t3, x=ssq: s.activation(o, x, AF.Sqrt))
        P.fence("scalar")
    ms2 = P.mark("scalar")
    P.wait("vector", *ms2)
    P.emit("vector", lambda v, o=rstd: v.reciprocal(o, o))
    P.fence("vector")
    if rho_out is not None:
        P.emit("vector", lambda v, o=t3, r=rstd: v.tensor_tensor(o, o, r, ALU.mult))
        P.fence("vector")
        P.emit("vector", lambda v, o=t3, c=g["consts"]: v.tensor_scalar(out=o, in0=o, scalar1=c[:, 1:2],
                                                                        scalar2=None, op0=ALU.max))
        P.fence("vector")
        P.emit("vector", lambda v, o=rho_out, x=t3: v.reciprocal(o, x))
        P.fence("vector")
        P.emit("vector", lambda v, o=rho_out: v.tensor_scalar(out=o, in0=o, scalar1=32.0,
                                                              scalar2=None, op0=ALU.mult))
        P.fence("vector")
    P.emit("vector", lambda v, o=xn_out, x=tmp, r=rstd: v.tensor_scalar(out=o, in0=x, scalar1=r,
                                                                        scalar2=None, op0=ALU.mult))
    if out2 is not None:
        P.emit("vector", lambda v, o=out2, x=tmp, r=rstd: v.tensor_scalar(out=o, in0=x, scalar1=r,
                                                                          scalar2=None, op0=ALU.mult))


def transpose_tile(P, psT, src_bf, dst, ident, w=128):
    P.emit("tensor", lambda t, o=psT, x=src_bf, i=ident: t.transpose(o, x, i), inc=(P.sem("c_a"), 1))
    P.wait("vector", P.sem("c_a"))
    P.emit("vector", lambda v, o=dst, x=psT: v.tensor_copy(o, x))


def emit_p1(P, g, io, es):
    nc = P.nc
    sb = lambda n, s, d=F32: es.enter_context(nc.sbuf_tensor(n, s, d))[:]
    pst = lambda n, s, d=F32: es.enter_context(nc.psum_tensor(n, s, d))[:]

    wnames = ["W1q", "W2q", "W1k", "W2k", "W1v", "W2v"]
    wb = {n: sb("p1_" + n, [128, 8, 1024], BF16) for n in wnames}
    wstage = sb("p1_wstage", [128, 8, 1024], F32)
    bia = sb("p1_bias", [128, 8, 8], F32)
    bvb = sb("p1_bvb", [128, 1024], F32)
    gxb = sb("p1_gxb", [128, 1024], F32)
    rowbuf = sb("p1_rowbuf", [1, 1024], F32)

    for i, n in enumerate(wnames):
        ld = P.dma("sync", wstage[:], io[n].rearrange("(k p) n -> p k n", p=128))
        P.wait("vector", P.sem("d_sync"), ld)
        if i > 0:
            P.fence("vector")
        P.emit("vector", lambda v, o=wb[n], x=wstage: v.tensor_copy(o[:], x[:]))
        mv = P.mark("vector")
        P.wait("sync", *mv)
    for i, bn in enumerate(["b1q", "b2q", "b1k", "b2k", "b1v", "b2v"]):
        for mc in range(8):
            P.dma("gpsimd", bia[:, i, mc:mc + 1], io[bn][mc * 128:(mc + 1) * 128].unsqueeze(1))
    P.wait("vector", P.sem("d_gp"))
    P.emit("vector", lambda v, o=bia: v.tensor_scalar(out=o[:, 1, :], in0=o[:, 1, :],
                                                      scalar1=L0_SCALE, scalar2=None, op0=ALU.mult))
    P.fence("vector")
    P.dma("gpsimd", rowbuf[0:1, :], io["b2v"][None, :])
    psV = pst("p1_psV", [128, 2, 1024], F32)
    ps_b = psV[:, 0, :]
    P.wait("tensor", P.sem("d_gp"))
    mmgroup(P, ps_b[:, 0:512], [(g["ones1"][:], rowbuf[0:1, 0:512])])
    mmgroup(P, ps_b[:, 512:1024], [(g["ones1"][:], rowbuf[0:1, 512:1024])], inc=(P.sem("c_a"), 1))
    P.wait("vector", P.sem("c_a"))
    P.emit("vector", lambda v, o=bvb, x=ps_b: v.tensor_copy(o[:], x[:]), inc=(P.sem("c_b"), 1))
    P.wait("gpsimd", P.sem("c_a"))
    ld = P.dma("gpsimd", rowbuf[0:1, :], io["gamma_x"][None, :])
    P.wait("tensor", P.sem("d_gp"), ld)
    P.wait("tensor", P.sem("c_b"))
    mmgroup(P, ps_b[:, 0:512], [(g["ones1"][:], rowbuf[0:1, 0:512])])
    mmgroup(P, ps_b[:, 512:1024], [(g["ones1"][:], rowbuf[0:1, 512:1024])], inc=(P.sem("c_a"), 1))
    P.wait("vector", P.sem("c_a"))
    P.emit("vector", lambda v, o=gxb, x=ps_b: v.tensor_copy(o[:], x[:]))

    xt = sb("p1_xt", [128, 2, 1024], F32)
    tmp = sb("p1_tmp", [128, 1024], F32)
    sqscr = sb("p1_sqscr", [128, 1024], F32)
    stats = sb("p1_stats", [128, 4], F32)
    rho = sb("p1_rho", [128, 1], F32)
    xn_bf = sb("p1_xnbf", [128, 1024], BF16)
    xnr_bf = sb("p1_xnrbf", [128, 1024], BF16)
    xnf = sb("p1_xnf", [128, 8, 512], BF16)
    xnrf = sb("p1_xnrf", [128, 8, 512], BF16)
    hq = sb("p1_hq", [128, 8, 512], BF16)
    qf = sb("p1_qf", [128, 8, 512], BF16)
    vtm = sb("p1_vtm", [128, 2, 1024], BF16)
    psT = pst("p1_psT", [128, 2, 1024], BF16)
    psH = pst("p1_psH", [128, 2, 512], F32)

    P.barrier()

    for ch in range(8):
        for tt4 in range(4):
            tt = ch * 4 + tt4
            buf = tt % 2
            if tt >= 2:
                P.fence("vector")
                mv = P.mark("vector")
                P.wait("sync", *mv)
            ld = P.dma("sync", xt[:, buf, :], io["x0"][tt * 128:(tt + 1) * 128, :])
            P.wait("vector", P.sem("d_sync"), ld)
            emit_ln(P, xt[:, buf, :], xn_bf, tmp, stats, sq=sqscr, rho_out=rho[:, 0:1], g=g)
            P.fence("vector")
            P.emit("vector", lambda v, o=xnr_bf, x=xn_bf, r=rho, w=gxb:
                   v.scalar_tensor_tensor(out=o, in0=x, scalar=r[:, 0:1], in1=w, op0=ALU.mult, op1=ALU.mult))
            P.fence("vector")
            mv = P.mark("vector")
            P.wait("tensor", *mv)
            P.wait("sync", *mv)
            P.dma("sync", io["xnr_tm"][tt * 128:(tt + 1) * 128, :], xnr_bf[:])
            P.wait("vector", P.sem("d_sync"))
            cb0 = P.sem("c_b").n
            for ki in range(16):
                k = ki % 8
                srcb = xn_bf if ki < 8 else xnr_bf
                dstb = xnf if ki < 8 else xnrf
                P.wait("tensor", P.sem("c_b"), cb0 + ki - 1 if ki >= 2 else cb0)
                P.emit("tensor", lambda t, o=psT[:, ki % 2, 0:128], x=srcb[:, k * 128:(k + 1) * 128],
                       i=g["ident_b"][:]: t.transpose(o, x, i), inc=(P.sem("c_a"), 1))
                P.wait("vector", P.sem("c_a"))
                P.emit("vector", lambda v, o=dstb[:, k, tt4 * 128:(tt4 + 1) * 128], x=psT[:, ki % 2, 0:128]:
                       v.tensor_copy(o, x), inc=(P.sem("c_b"), 1))
        mv = P.mark("vector")
        P.wait("sync", *mv)
        P.dma("sync", io["xnr_fm"].rearrange("(k p) n -> p k n", p=128)[:, :, ch * 512:(ch + 1) * 512],
              xnrf[:])
        i0 = ch * 512
        for proj, (w1n, w2n, b1c, b2c) in enumerate([("W1q", "W2q", 0, 1), ("W1k", "W2k", 2, 3),
                                                     ("W1v", "W2v", 4, 5)]):
            cc0 = P.sem("c_c").n
            for mc in range(8):
                pb = mc % 2
                P.wait("tensor", P.sem("c_c"), cc0 + mc - 1 if mc >= 2 else cc0)
                mmgroup(P, psH[:, pb, :],
                        [(wb[w1n][:, k, mc * 128:(mc + 1) * 128], xnf[:, k, :]) for k in range(8)],
                        inc=(P.sem("c_a"), 1))
                P.wait("scalar", P.sem("c_a"))
                P.emit("scalar", lambda s, o=hq[:, mc, :], x=psH[:, pb, :], b=bia[:, b1c, mc:mc + 1]:
                       s.activation(o, x, AF.Gelu, bias=b), inc=(P.sem("c_c"), 1))
            if proj < 2:
                scl = L0_SCALE if proj == 0 else 1.0
                P.wait("tensor", P.sem("c_c"))
                cd0 = P.sem("c_d").n
                for mc in range(8):
                    pb = mc % 2
                    P.wait("tensor", P.sem("c_d"), cd0 + mc - 1 if mc >= 2 else cd0)
                    mmgroup(P, psH[:, pb, :],
                            [(wb[w2n][:, k, mc * 128:(mc + 1) * 128], hq[:, k, :]) for k in range(8)],
                            inc=(P.sem("c_a"), 1))
                    P.wait("scalar", P.sem("c_a"))
                    P.emit("scalar", lambda s, o=qf[:, mc, :], x=psH[:, pb, :], b=bia[:, b2c, mc:mc + 1], sc=scl:
                           s.activation(o, x, AF.Identity, bias=b, scale=sc), inc=(P.sem("c_d"), 1))
                ms = P.mark("scalar")
                P.wait("sync", *ms)
                dst = io["qf0"] if proj == 0 else io["kf0"]
                P.dma("sync", dst.rearrange("(k p) n -> p k n", p=128)[:, :, i0:i0 + 512], qf[:])
                P.wait("scalar", P.sem("d_sync"))
            else:
                P.wait("tensor", P.sem("c_c"))
                cb0 = P.sem("c_b").n
                ds0 = P.sem("d_sync").n
                for tt4 in range(4):
                    P.wait("tensor", P.sem("c_b"), cb0 + tt4 - 1 if tt4 >= 2 else cb0)
                    if tt4 >= 2:
                        P.wait("vector", P.sem("d_sync"), ds0 + 16 * (tt4 - 1))
                    for half in range(2):
                        mmgroup(P, psV[:, tt4 % 2, half * 512:(half + 1) * 512],
                                [(hq[:, k, tt4 * 128:(tt4 + 1) * 128],
                                  wb[w2n][:, k, half * 512:(half + 1) * 512]) for k in range(8)],
                                inc=(P.sem("c_a"), 1) if half == 1 else None)
                    P.wait("vector", P.sem("c_a"))
                    P.emit("vector", lambda v, o=vtm[:, tt4 % 2, :], x=psV[:, tt4 % 2, :], b=bvb:
                           v.tensor_add(o, x, b), inc=(P.sem("c_b"), 1))
                    mv = P.mark("vector")
                    P.wait("sync", *mv)
                    tt = ch * 4 + tt4
                    P.dma("sync", io["v0"][tt * 128:(tt + 1) * 128, :], vtm[:, tt4 % 2, :])
    P.barrier()




def emit_p2(P, g, io, es):
    nc = P.nc
    sb = lambda n, s, d=F32: es.enter_context(nc.sbuf_tensor(n, s, d))[:]
    pst = lambda n, s, d=F32: es.enter_context(nc.psum_tensor(n, s, d))[:]

    k_res = sb("p2_k", [128, 8, T], BF16)
    v_res = sb("p2_v", [128, NT, D], BF16)
    qc = sb("p2_q", [128, 2, 8, 512], BF16)
    sc_sb = sb("p2_sc", [128, NT, 512], BF16)
    s0pre = sb("p2_s0pre", [128, 8, 512], BF16)
    psc = pst("p2_psc", [128, 2, 512], F32)
    pso = pst("p2_pso", [128, 2, 512], F32)

    P.dma("sync", k_res, io["kf0"].rearrange("(k p) n -> p k n", p=128))
    P.dma("sync", v_res, io["v0"].rearrange("(j p) d -> p j d", p=128))

    for ib in range(8):
        i0 = ib * 512
        njt = 4 * ib + 4
        buf = ib % 2
        if ib >= 2:
            P.fence("tensor")
            mt = P.mark("tensor")
            P.wait("sync", *mt)
        P.dma("sync", qc[:, buf, :, :], io["qf0"].rearrange("(k p) n -> p k n", p=128)[:, :, i0:i0 + 512])
        P.wait("tensor", P.sem("d_sync"))
        cc0 = P.sem("c_c").n
        for jt in range(njt):
            pb = jt % 2
            P.wait("tensor", P.sem("c_c"), cc0 + jt - 1 if jt >= 2 else cc0)
            mmgroup(P, psc[:, pb, :],
                    [(k_res[:, k, jt * 128:(jt + 1) * 128], qc[:, buf, k, :]) for k in range(8)],
                    inc=(P.sem("c_a"), 1))
            P.wait("scalar", P.sem("c_a"))
            P.wait("scalar", P.sem("c_d"))
            P.emit("scalar", lambda s, o=sc_sb[:, jt, :], x=psc[:, pb, :]:
                   s.activation(o, x, AF.Identity), inc=(P.sem("c_c"), 1))
            if jt >= 4 * ib:
                P.wait("gpsimd", P.sem("c_c"))
                P.emit("gpsimd", lambda gp, o=sc_sb[:, jt, :], b=i0 - 128 * jt:
                       gp.affine_select(out=o, in_=o, compare_op=ALU.is_ge, fill=0.0,
                                        base=b, pattern=[[1, 512]], channel_multiplier=-1),
                       inc=(P.sem("c_d"), 1))
        P.wait("tensor", P.sem("c_c"))
        P.wait("tensor", P.sem("c_d"))
        cb0 = P.sem("c_b").n
        for dv in range(8):
            pb = dv % 2
            P.wait("tensor", P.sem("c_b"), cb0 + dv - 1 if dv >= 2 else cb0)
            mmgroup(P, pso[:, pb, :],
                    [(v_res[:, 4 * ib0 + r, dv * 128:(dv + 1) * 128], sc_sb[:, 4 * ib0 + r, :])
                     for ib0 in range(ib + 1) for r in range(4)][:njt],
                    inc=(P.sem("c_a"), 1))
            P.wait("vector", P.sem("c_a"))
            P.emit("vector", lambda v, o=s0pre[:, dv, :], x=pso[:, pb, :]:
                   v.tensor_copy(o, x), inc=(P.sem("c_b"), 1))
        P.fence("vector")
        mv = P.mark("vector")
        P.wait("sync", *mv)
        P.dma("sync", io["s0pre_fm"].rearrange("(k p) n -> p k n", p=128)[:, :, i0:i0 + 512], s0pre)
        P.wait("vector", P.sem("d_sync"))
    P.barrier()


def emit_p2b(P, g, io, es):
    nc = P.nc
    sb = lambda n, s, d=F32: es.enter_context(nc.sbuf_tensor(n, s, d))[:]
    pst = lambda n, s, d=F32: es.enter_context(nc.psum_tensor(n, s, d))[:]
    sfm = sb("p2b_sfm", [128, 2, 8, 128], BF16)
    s0tm = sb("p2b_s0tm", [128, 1024], F32)
    tmp = sb("p2b_tmp", [128, 1024], F32)
    sq = sb("p2b_sq", [128, 1024], F32)
    stats = sb("p2b_stats", [128, 4], F32)
    s0bf = sb("p2b_s0bf", [128, 1024], BF16)
    lnw_b = sb("p2b_lnwb", [128, 1024], F32)
    lnb_b = sb("p2b_lnbb", [128, 1024], F32)
    rowbuf = sb("p2b_rowbuf", [1, 1024], F32)
    psT = pst("p2b_psT", [128, 2, 1024], BF16)
    psB = pst("p2b_psB", [128, 1024], F32)

    for nm, dst in (("ln_w", lnw_b), ("ln_b", lnb_b)):
        P.wait("gpsimd", P.sem("c_a"))
        ld = P.dma("gpsimd", rowbuf[0:1, :], io[nm][None, :])
        P.wait("tensor", P.sem("d_gp"), ld)
        P.wait("tensor", P.sem("c_b"))
        mmgroup(P, psB[:, 0:512], [(g["ones1"][:], rowbuf[0:1, 0:512])])
        mmgroup(P, psB[:, 512:1024], [(g["ones1"][:], rowbuf[0:1, 512:1024])], inc=(P.sem("c_a"), 1))
        P.wait("vector", P.sem("c_a"))
        P.emit("vector", lambda v, o=dst, x=psB: v.tensor_copy(o[:], x[:]), inc=(P.sem("c_b"), 1))

    for tt in range(NT):
        buf = tt % 2
        if tt >= 2:
            P.fence("tensor")
            mt = P.mark("tensor")
            P.wait("sync", *mt)
        ld = P.dma("sync", sfm[:, buf, :, :],
                   io["s0pre_fm"].rearrange("(k p) n -> p k n", p=128)[:, :, tt * 128:(tt + 1) * 128])
        P.wait("tensor", P.sem("d_sync"), ld)
        for kk in range(8):
            P.wait("tensor", P.sem("c_b"))
            P.emit("tensor", lambda t, o=psT[:, kk % 2, 0:128], x=sfm[:, buf, kk, :],
                   i=g["ident_b"]: t.transpose(o, x, i), inc=(P.sem("c_a"), 1))
            P.wait("vector", P.sem("c_a"))
            P.emit("vector", lambda v, o=s0tm[:, kk * 128:(kk + 1) * 128], x=psT[:, kk % 2, 0:128]:
                   v.tensor_copy(o, x), inc=(P.sem("c_b"), 1))
        emit_ln(P, s0tm, tmp, tmp, stats, sq=sq, g=g)
        P.fence("vector")
        P.emit("vector", lambda v, o=tmp, w=lnw_b: v.tensor_tensor(o, o, w, ALU.mult))
        P.fence("vector")
        P.emit("vector", lambda v, o=s0bf, x=tmp, b=lnb_b: v.tensor_tensor(o, x, b, ALU.add))
        P.fence("vector")
        mv = P.mark("vector")
        P.wait("sync", *mv)
        P.dma("sync", io["s0"][tt * 128:(tt + 1) * 128, :], s0bf)
        P.wait("vector", P.sem("d_sync"))
    P.barrier()


def emit_p3(P, g, io, es):
    nc = P.nc
    sb = lambda n, s, d=F32: es.enter_context(nc.sbuf_tensor(n, s, d))[:]
    pst = lambda n, s, d=F32: es.enter_context(nc.psum_tensor(n, s, d))[:]
    set_ = sb("p3_se", [128, 2, 1024], F32)
    sqd = sb("p3_sqd", [128, 1024], F32)
    ssq = sb("p3_ssq", [128, 4], F32)
    senb = sb("p3_senb", [128, 1024], BF16)
    sef = sb("p3_sef", [128, 8, 128], BF16)
    gsb = sb("p3_gsb", [128, 1024], F32)
    rowbuf = sb("p3_rowbuf", [1, 1024], F32)
    c32 = sb("p3_c32", [1, 128], F32)
    psT = pst("p3_psT", [128, 2, 1024], BF16)
    psB = pst("p3_psB", [128, 1024], F32)

    P.emit("gpsimd", lambda gp, o=c32: gp.memset(o, 32.0))
    P.wait("gpsimd", P.sem("c_a"))
    ld = P.dma("gpsimd", rowbuf[0:1, :], io["gamma_slot"][None, :])
    mg = P.mark("gpsimd")
    P.wait("tensor", *mg)
    P.wait("tensor", P.sem("d_gp"), ld)
    P.wait("tensor", P.sem("c_b"))
    mmgroup(P, psB[:, 0:512], [(c32, rowbuf[0:1, 0:512])])
    mmgroup(P, psB[:, 512:1024], [(c32, rowbuf[0:1, 512:1024])], inc=(P.sem("c_a"), 1))
    P.wait("vector", P.sem("c_a"))
    P.emit("vector", lambda v, o=gsb, x=psB: v.tensor_copy(o[:], x[:]), inc=(P.sem("c_b"), 1))

    for tt in range(NT):
        buf = tt % 2
        if tt >= 2:
            P.fence("vector")
            mv = P.mark("vector")
            P.wait("sync", *mv)
        ld = P.dma("sync", set_[:, buf, :], io["slot_embeds"][tt * 128:(tt + 1) * 128, :])
        P.wait("vector", P.sem("d_sync"), ld)
        P.emit("vector", lambda v, o=sqd, x=set_[:, buf, :]: v.tensor_mul(o, x, x))
        P.fence("vector")
        P.emit("vector", lambda v, o=ssq, x=sqd: v.tensor_reduce(o[:, 0:1], x, AX.X, ALU.add))
        P.fence("vector")
        mv = P.mark("vector")
        P.wait("scalar", *mv)
        P.emit("scalar", lambda s, o=ssq: s.activation(o[:, 1:2], o[:, 0:1], AF.Sqrt))
        ms = P.mark("scalar")
        P.wait("vector", *ms)
        P.emit("vector", lambda v, o=ssq, c=g["consts"]: v.tensor_scalar(out=o[:, 1:2], in0=o[:, 1:2],
                                                                         scalar1=c[:, 1:2], scalar2=None,
                                                                         op0=ALU.max))
        P.fence("vector")
        P.emit("vector", lambda v, o=ssq: v.reciprocal(o[:, 2:3], o[:, 1:2]))
        P.fence("vector")
        P.emit("vector", lambda v, o=senb, x=set_[:, buf, :], r=ssq, w=gsb:
               v.scalar_tensor_tensor(out=o, in0=x, scalar=r[:, 2:3], in1=w, op0=ALU.mult, op1=ALU.mult))
        P.fence("vector")
        mv = P.mark("vector")
        P.wait("tensor", *mv)
        for kk in range(8):
            P.wait("tensor", P.sem("c_b"))
            P.emit("tensor", lambda t, o=psT[:, kk % 2, 0:128], x=senb[:, kk * 128:(kk + 1) * 128],
                   i=g["ident_b"]: t.transpose(o, x, i), inc=(P.sem("c_a"), 1))
            P.wait("vector", P.sem("c_a"))
            P.emit("vector", lambda v, o=sef[:, kk, :], x=psT[:, kk % 2, 0:128]:
                   v.tensor_copy(o, x), inc=(P.sem("c_b"), 1))
        P.fence("vector")
        mv = P.mark("vector")
        P.wait("sync", *mv)
        P.dma("sync", io["se_fm"].rearrange("(k p) n -> p k n", p=128)[:, :, tt * 128:(tt + 1) * 128], sef)
        P.wait("vector", P.sem("d_sync"))
    P.barrier()




def emit_p5(P, g, io, es):
    nc = P.nc
    sb = lambda n, s, d=F32: es.enter_context(nc.sbuf_tensor(n, s, d))[:]
    pst = lambda n, s, d=F32: es.enter_context(nc.psum_tensor(n, s, d))[:]
    xnrf = sb("p5_xnrf", [128, 8, T], BF16)
    xnrt = sb("p5_xnrt", [128, NT, D], BF16)
    se_sl = sb("p5_sesl", [128, 2, 8, 128], BF16)
    lT_sb = sb("p5_lt", [128, T], F32)
    E = sb("p5_E", [128, T], BF16)
    disp = sb("p5_disp", [128, NT, 128], BF16)
    slots_bf = sb("p5_slots", [128, 1024], BF16)
    st4 = sb("p5_st4", [128, 6], F32)
    runC = sb("p5_runC", [128, 1], F32)
    cfin = sb("p5_cfin", [1, 128], F32)
    psL = pst("p5_psL", [128, 2, 512], F32)
    psS = pst("p5_psS", [128, 2, 512], F32)
    psT = pst("p5_psT", [128, 2, 1024], BF16)
    psC = pst("p5_psC", [1, 128], F32)

    P.emit("gpsimd", lambda gp, o=runC: gp.memset(o, -3e38))
    mg = P.mark("gpsimd")
    P.wait("vector", *mg)
    P.dma("sync", xnrf, io["xnr_fm"].rearrange("(k p) n -> p k n", p=128))
    P.dma("sync", xnrt, io["xnr_tm"].rearrange("(s p) d -> p s d", p=128))

    for sc in range(NT):
        buf = sc % 2
        if sc >= 2:
            P.wait("sync", P.sem("c_a"))
        ld = P.dma("sync", se_sl[:, buf, :, :],
                   io["se_fm"].rearrange("(k p) n -> p k n", p=128)[:, :, sc * 128:(sc + 1) * 128])
        P.wait("tensor", P.sem("d_sync"), ld)
        if sc >= 1:
            P.wait("vector", P.sem("d_sync"))
        cb0 = P.sem("c_b").n
        for ng in range(8):
            pb = ng % 2
            P.wait("tensor", P.sem("c_b"), cb0 + ng - 1 if ng >= 2 else cb0)
            mmgroup(P, psL[:, pb, :],
                    [(se_sl[:, buf, k, :], xnrf[:, k, ng * 512:(ng + 1) * 512]) for k in range(8)],
                    inc=(P.sem("c_a"), 1))
            P.wait("vector", P.sem("c_a"))
            P.emit("vector", lambda v, o=lT_sb[:, ng * 512:(ng + 1) * 512], x=psL[:, pb, :]:
                   v.tensor_copy(o, x), inc=(P.sem("c_b"), 1))
        P.fence("vector")
        mv = P.mark("vector")
        P.wait("sync", *mv)
        P.dma("sync", io["lT"][sc * 128:(sc + 1) * 128, :], lT_sb)
        P.emit("vector", lambda v, o=st4, x=lT_sb: v.tensor_reduce(o[:, 0:1], x, AX.X, ALU.max, negate=True))
        P.fence("vector")
        P.emit("vector", lambda v, o=st4: v.tensor_scalar(out=o[:, 3:4], in0=o[:, 0:1], scalar1=-1.0,
                                                          scalar2=None, op0=ALU.mult))
        P.fence("vector")
        P.emit("vector", lambda v, o=runC, x=st4: v.tensor_max(o, o, x[:, 3:4]))
        mv = P.mark("vector")
        P.wait("scalar", *mv)
        P.emit("scalar", lambda s, o=E, x=lT_sb, st=st4: s.activation(o, x, AF.Exp, bias=st[:, 0:1]))
        P.fence("scalar")
        ms = P.mark("scalar")
        P.wait("vector", *ms)
        P.wait("tensor", *ms)
        P.emit("vector", lambda v, o=st4, x=E: v.tensor_reduce(o[:, 1:2], x, AX.X, ALU.add))
        P.fence("vector")
        P.emit("vector", lambda v, o=st4: v.reciprocal(o[:, 2:3], o[:, 1:2]))
        P.fence("vector")
        P.emit("vector", lambda v, o=st4: v.tensor_copy(o[:, 4:5], o[:, 2:3]))
        P.fence("vector")
        mv2 = P.mark("vector")
        P.wait("scalar", *mv2)
        cb1 = P.sem("c_b").n
        for st_ in range(NT):
            P.wait("tensor", P.sem("c_b"), cb1 + st_ - 1 if st_ >= 2 else cb1)
            P.emit("tensor", lambda t, o=psT[:, st_ % 2, 0:128], x=E[:, st_ * 128:(st_ + 1) * 128],
                   i=g["ident_b"]: t.transpose(o, x, i), inc=(P.sem("c_a"), 1))
            P.wait("vector", P.sem("c_a"))
            P.emit("vector", lambda v, o=disp[:, st_, :], x=psT[:, st_ % 2, 0:128]:
                   v.tensor_copy(o, x), inc=(P.sem("c_b"), 1))
        for ng2 in range(2):
            pb = ng2 % 2
            P.wait("tensor", P.sem("c_b"))
            P.wait("tensor", P.sem("c_c"))
            mmgroup(P, psS[:, pb, :],
                    [(disp[:, st_, :], xnrt[:, st_, ng2 * 512:(ng2 + 1) * 512]) for st_ in range(NT)],
                    inc=(P.sem("c_a"), 1))
            P.wait("scalar", P.sem("c_a"))
            P.emit("scalar", lambda s, o=slots_bf[:, ng2 * 512:(ng2 + 1) * 512], x=psS[:, pb, :],
                   st=st4: s.activation(o, x, AF.Identity, scale=st[:, 2:3]), inc=(P.sem("c_c"), 1))
        ms = P.mark("scalar")
        P.wait("sync", *ms)
        P.dma("sync", io["slots"][sc * 128:(sc + 1) * 128, :], slots_bf)
        P.wait("scalar", P.sem("d_sync"))
    P.fence("vector")
    mv = P.mark("vector")
    P.wait("tensor", *mv)
    P.emit("tensor", lambda t, o=psC, x=runC, i=g["ident_f"]: t.transpose(o, x, i),
           inc=(P.sem("c_a"), 1))
    P.wait("vector", P.sem("c_a"))
    P.emit("vector", lambda v, o=cfin, x=psC: v.tensor_copy(o, x))
    P.fence("vector")
    P.emit("vector", lambda v, o=cfin: v.tensor_reduce(o[0:1, 0:1], o[0:1, :], AX.X, ALU.max))
    P.fence("vector")
    mv = P.mark("vector")
    P.wait("sync", *mv)
    P.dma("sync", io["maxC"][0:1, 0:1], cfin[0:1, 0:1])
    P.barrier()


P6_STAGE = 99


def emit_p6(P, g, io, es):
    nc = P.nc
    sb = lambda n, s, d=F32: es.enter_context(nc.sbuf_tensor(n, s, d))[:]
    pst = lambda n, s, d=F32: es.enter_context(nc.psum_tensor(n, s, d))[:]
    w1b = sb("p6_w1b", [128, 8, 4096], BF16)
    w2b = sb("p6_w2b", [128, 32, 1024], BF16)
    wstage = sb("p6_wstage", [128, 2048], F32)
    slo = sb("p6_slo", [128, 2, 4, 1024], BF16)
    sfm = sb("p6_sfm", [128, 8, 512], BF16)
    hfm = sb("p6_hfm", [128, 32, 512], BF16)
    pro = sb("p6_pro", [128, 4, 1024], BF16)
    psH = pst("p6_psH", [128, 2, 512], F32)
    psP = pst("p6_psP", [128, 2, 512], F32)
    psT = pst("p6_psT", [128, 2, 1024], BF16)

    for qi in range(8):
        for half in range(2):
            ld = P.dma("sync", wstage,
                       io["w1"].rearrange("(k p) n -> p k n", p=128)[:, qi, half * 2048:(half + 1) * 2048])
            P.wait("vector", P.sem("d_sync"), ld)
            P.emit("vector", lambda v, o=w1b[:, qi, half * 2048:(half + 1) * 2048], x=wstage:
                   v.tensor_copy(o, x))
            mv = P.mark("vector")
            P.wait("sync", *mv)
    for qi in range(16):
        ld = P.dma("sync", wstage.rearrange("p (a n) -> p a n", n=1024)[:, 0:2, :],
                   io["w2"].rearrange("(k p) n -> p k n", p=128)[:, qi * 2:(qi + 1) * 2, :])
        P.wait("vector", P.sem("d_sync"), ld)
        P.emit("vector", lambda v, o=w2b[:, qi * 2:(qi + 1) * 2, :],
               x=wstage.rearrange("p (a n) -> p a n", n=1024)[:, 0:2, :]: v.tensor_copy(o, x))
        mv = P.mark("vector")
        P.wait("sync", *mv)
    P.barrier()

    for pr in range(8):
        buf = pr % 2
        if pr >= 2:
            P.fence("tensor")
            mt = P.mark("tensor")
            P.wait("sync", *mt)
        ld = P.dma("sync", slo[:, buf, :, :],
                   io["slots"].rearrange("(n p) d -> p n d", p=128)[:, pr * 4:pr * 4 + 4, :])
        P.wait("tensor", P.sem("d_sync"), ld)
        cb0 = P.sem("c_b").n
        ti = 0
        for half in range(4):
            for kk in range(8):
                P.wait("tensor", P.sem("c_b"), cb0 + ti - 1 if ti >= 2 else cb0)
                P.emit("tensor", lambda t, o=psT[:, ti % 2, 0:128], x=slo[:, buf, half, kk * 128:(kk + 1) * 128],
                       i=g["ident_b"]: t.transpose(o, x, i), inc=(P.sem("c_a"), 1))
                P.wait("vector", P.sem("c_a"))
                P.emit("vector", lambda v, o=sfm[:, kk, half * 128:(half + 1) * 128], x=psT[:, ti % 2, 0:128]:
                       v.tensor_copy(o, x), inc=(P.sem("c_b"), 1))
                ti += 1
        P.fence("vector")
        mv = P.mark("vector")
        P.wait("tensor", *mv)
        cc0 = P.sem("c_c").n
        for mc in range(32):
            pb = mc % 2
            P.wait("tensor", P.sem("c_c"), cc0 + mc - 1 if mc >= 2 else cc0)
            mmgroup(P, psH[:, pb, :],
                    [(w1b[:, k, mc * 128:(mc + 1) * 128], sfm[:, k, :]) for k in range(8)],
                    inc=(P.sem("c_a"), 1))
            P.wait("scalar", P.sem("c_a"))
            P.emit("scalar", lambda s, o=hfm[:, mc, :], x=psH[:, pb, :]:
                   s.activation(o, x, AF.Gelu), inc=(P.sem("c_c"), 1))
        P.wait("tensor", P.sem("c_c"))
        cd0 = P.sem("c_d").n
        di = 0
        for mh in range(4):
            for ng in range(2):
                pb = di % 2
                P.wait("tensor", P.sem("c_d"), cd0 + di - 1 if di >= 2 else cd0)
                mmgroup(P, psP[:, pb, :],
                        [(hfm[:, kf, mh * 128:(mh + 1) * 128], w2b[:, kf, ng * 512:(ng + 1) * 512])
                         for kf in range(32)],
                        inc=(P.sem("c_a"), 1))
                P.wait("vector", P.sem("c_a"))
                P.emit("vector", lambda v, o=pro[:, mh, ng * 512:(ng + 1) * 512], x=psP[:, pb, :]:
                       v.tensor_copy(o, x), inc=(P.sem("c_d"), 1))
                di += 1
        P.fence("vector")
        mv = P.mark("vector")
        P.wait("sync", *mv)
        P.dma("sync", io["processed"].rearrange("(n p) d -> p n d", p=128)[:, pr * 4:pr * 4 + 4, :], pro)
        P.wait("vector", P.sem("d_sync"))
    P.barrier()


P7_DEBUG = False
P11_SERIAL = False


def emit_p7(P, g, io, es):
    nc = P.nc
    sb = lambda n, s, d=F32: es.enter_context(nc.sbuf_tensor(n, s, d))[:]
    pst = lambda n, s, d=F32: es.enter_context(nc.psum_tensor(n, s, d))[:]
    pro = sb("p7_pro", [128, NT, 1024], BF16)
    lbig = sb("p7_lbig", [128, 2, NT, 128], F32)
    ltm = sb("p7_ltm", [128, T], F32)
    Etm = sb("p7_Etm", [128, T], BF16)
    Ecol = sb("p7_Ecol", [128, NT, 128], BF16)
    st2 = sb("p7_st2", [128, 4], F32)
    g0bf = sb("p7_g0", [128, 1024], BF16)
    x0t = sb("p7_x0t", [128, 2, 1024], F32)
    s0t = sb("p7_s0t", [128, 2, 1024], BF16)
    tmp = sb("p7_tmp", [128, 1024], F32)
    x1t = sb("p7_x1t", [128, 1024], F32)
    psTf = pst("p7_psTf", [128, 2, 512], F32)
    psTb = pst("p7_psTb", [128, 2, 1024], BF16)
    psG = pst("p7_psG", [128, 2, 512], F32)

    P.dma("sync", pro, io["processed"].rearrange("(n p) d -> p n d", p=128))
    P.barrier()

    tile_tr_mark = {}
    for tt in range(NT):
        ca0 = P.sem("c_a").n
        cb0 = P.sem("c_b").n
        buf = tt % 2
        if tt >= 2:
            P.wait("sync", P.sem("c_a"), tile_tr_mark[tt - 2])
        ld = P.dma("sync", lbig[:, buf, :, :],
                   io["lT"].rearrange("(n p) t -> p n t", p=128)[:, :, tt * 128:(tt + 1) * 128])
        P.wait("tensor", P.sem("d_sync"), ld)
        for nc_ in range(NT):
            P.wait("tensor", P.sem("c_b"), cb0 + nc_ - 1 if nc_ >= 2 else cb0)
            P.emit("tensor", lambda t, o=psTf[:, nc_ % 2, 0:128], x=lbig[:, buf, nc_, :], i=g["ident_f"]:
                   t.transpose(o, x, i), inc=(P.sem("c_a"), 1))
            P.wait("vector", P.sem("c_a"))
            P.emit("vector", lambda v, o=ltm[:, nc_ * 128:(nc_ + 1) * 128], x=psTf[:, nc_ % 2, 0:128]:
                   v.tensor_copy(o, x), inc=(P.sem("c_b"), 1))
        tile_tr_mark[tt] = P.sem("c_a").n
        if P7_DEBUG:
            P.fence("vector")
            mvd = P.mark("vector")
            P.wait("sync", *mvd)
            P.dma("sync", io["dbg_ltm"][tt * 128:(tt + 1) * 128, :], ltm)
        P.emit("vector", lambda v, o=st2, x=ltm: v.tensor_reduce(o[:, 0:1], x, AX.X, ALU.max, negate=True))
        P.fence("vector")
        mv = P.mark("vector")
        P.wait("scalar", *mv)
        P.emit("scalar", lambda s, o=Etm, x=ltm, st=st2: s.activation(o, x, AF.Exp, bias=st[:, 0:1]))
        P.fence("scalar")
        ms = P.mark("scalar")
        P.wait("vector", *ms)
        P.wait("tensor", *ms)
        P.emit("vector", lambda v, o=st2, x=Etm: v.tensor_reduce(o[:, 1:2], x, AX.X, ALU.add))
        P.fence("vector")
        P.emit("vector", lambda v, o=st2: v.reciprocal(o[:, 2:3], o[:, 1:2]))
        P.fence("vector")
        P.emit("vector", lambda v, o=st2: v.tensor_copy(o[:, 3:4], o[:, 2:3]))
        P.fence("vector")
        mv2 = P.mark("vector")
        P.wait("scalar", *mv2)
        cb1 = P.sem("c_b").n
        for nc_ in range(NT):
            P.wait("tensor", P.sem("c_b"), cb1 + nc_ if nc_ >= 1 else cb1)
            P.emit("tensor", lambda t, o=psTb[:, nc_ % 2, 0:128], x=Etm[:, nc_ * 128:(nc_ + 1) * 128],
                   i=g["ident_b"]: t.transpose(o, x, i), inc=(P.sem("c_a"), 1))
            P.wait("vector", P.sem("c_a"))
            P.emit("vector", lambda v, o=Ecol[:, nc_, :], x=psTb[:, nc_ % 2, 0:128]:
                   v.tensor_copy(o, x), inc=(P.sem("c_b"), 1))
        for ng in range(2):
            if ng == 0:
                P.wait("tensor", P.sem("c_b"))
                P.wait("tensor", P.sem("c_c"))
            mmgroup(P, psG[:, ng, :],
                    [(Ecol[:, nc_, :], pro[:, nc_, ng * 512:(ng + 1) * 512]) for nc_ in range(NT)],
                    inc=(P.sem("c_a"), 1) if ng == 1 else None)
        P.wait("scalar", P.sem("c_a"))
        P.emit("scalar", lambda s, o=g0bf, x=psG.rearrange("p a b -> p (a b)"), st=st2:
               s.activation(o, x, AF.Silu, scale=st[:, 2:3]), inc=(P.sem("c_c"), 1))
        if P7_DEBUG:
            msd = P.mark("scalar")
            P.wait("sync", *msd)
            P.dma("sync", io["dbg_g0"][tt * 128:(tt + 1) * 128, :], g0bf)
            P.wait("scalar", P.sem("d_sync"))
            mvd2 = P.mark("vector")
            P.wait("sync", *mvd2)
            P.dma("sync", io["dbg_st"][tt * 128:(tt + 1) * 128, :], st2)
            P.dma("sync", io["dbg_E"][tt * 128:(tt + 1) * 128, :], Etm)
            P.wait("vector", P.sem("d_sync"))
        buf = tt % 2
        if tt >= 2:
            P.fence("vector")
            mv = P.mark("vector")
            P.wait("sync", *mv)
        P.dma("sync", x0t[:, buf, :], io["x0"][tt * 128:(tt + 1) * 128, :])
        ld2 = P.dma("sync", s0t[:, buf, :], io["s0"][tt * 128:(tt + 1) * 128, :])
        P.wait("vector", P.sem("d_sync"), ld2)
        P.wait("vector", P.sem("c_c"))
        P.emit("vector", lambda v, o=tmp, a=s0t[:, buf, :], b=g0bf: v.tensor_mul(o, a, b))
        P.fence("vector")
        P.emit("vector", lambda v, o=x1t, a=x0t[:, buf, :], b=tmp: v.tensor_add(o, a, b))
        P.fence("vector")
        mv = P.mark("vector")
        P.wait("sync", *mv)
        P.dma("sync", io["x1"][tt * 128:(tt + 1) * 128, :], x1t)
        P.wait("vector", P.sem("d_sync"))
        P.wait("scalar", P.sem("d_sync"))
    P.barrier()


def emit_p8(P, g, io, es):
    nc = P.nc
    sb = lambda n, s, d=F32: es.enter_context(nc.sbuf_tensor(n, s, d))[:]
    pst = lambda n, s, d=F32: es.enter_context(nc.psum_tensor(n, s, d))[:]
    wts = {}
    wstage = sb("p8_wstage", [128, 8, 1024], F32)
    for wn in ("Wq", "Wk", "Wv"):
        wts[wn] = sb("p8_" + wn, [128, 8, 1024], BF16)
        ld = P.dma("sync", wstage, io[wn].rearrange("(k p) n -> p k n", p=128))
        P.wait("vector", P.sem("d_sync"), ld)
        P.emit("vector", lambda v, o=wts[wn], x=wstage: v.tensor_copy(o, x))
        mv = P.mark("vector")
        P.wait("sync", *mv)
    bb = {}
    rowbuf = sb("p8_rowbuf", [1, 1024], F32)
    c125 = sb("p8_c125", [1, 128], F32)
    psQKV = pst("p8_psQKV", [128, 2, 1024], F32)
    psB = psQKV[:, 0, :]
    P.emit("gpsimd", lambda gp, o=c125: gp.memset(o, 0.125))
    mg = P.mark("gpsimd")
    P.wait("tensor", *mg)
    for bn, lh in (("bq", c125), ("bk", g["ones1"]), ("bv", g["ones1"])):
        bb[bn] = sb("p8_b_" + bn, [128, 1024], F32)
        P.wait("gpsimd", P.sem("c_a"))
        ld = P.dma("gpsimd", rowbuf[0:1, :], io[bn][None, :])
        P.wait("tensor", P.sem("d_gp"), ld)
        P.wait("tensor", P.sem("c_b"))
        mmgroup(P, psB[:, 0:512], [(lh, rowbuf[0:1, 0:512])])
        mmgroup(P, psB[:, 512:1024], [(lh, rowbuf[0:1, 512:1024])], inc=(P.sem("c_a"), 1))
        P.wait("vector", P.sem("c_a"))
        P.emit("vector", lambda v, o=bb[bn], x=psB: v.tensor_copy(o[:], x[:]), inc=(P.sem("c_b"), 1))
    ct = sb("p8_ct", [128, NT, 32], F32)
    stb = sb("p8_st", [128, NT, 32], F32)
    P.dma("sync", ct, io["ctab"].rearrange("(tt p) i -> p tt i", p=128))
    P.dma("sync", stb, io["stab"].rearrange("(tt p) i -> p tt i", p=128))

    x1t = sb("p8_x1t", [128, 2, 1024], F32)
    tmp = sb("p8_tmp", [128, 1024], F32)
    sq = sb("p8_sq", [128, 1024], F32)
    stats = sb("p8_stats", [128, 4], F32)
    xnb = sb("p8_xnb", [128, 1024], BF16)
    xnf = sb("p8_xnf", [128, 8, 128], BF16)
    qt = sb("p8_qt", [128, 1024], F32)
    kt = sb("p8_kt", [128, 1024], F32)
    rp1 = sb("p8_rp1", [128, 512], F32)
    rp2 = sb("p8_rp2", [128, 512], F32)
    rot = sb("p8_rot", [128, 1024], BF16)
    vbf = sb("p8_vbf", [128, 1024], BF16)
    qkf = sb("p8_qkf", [128, 8, 128], BF16)
    psT = pst("p8_psT", [128, 2, 1024], BF16)
    psQ = psQKV[:, 0, :]
    psK = psQKV[:, 1, :]
    P.barrier()

    def rope(src_f32, tt):
        s4 = src_f32.rearrange("p (h i two) -> p h i two", i=32, two=2)
        r4 = rot.rearrange("p (h i two) -> p h i two", i=32, two=2)
        cr = ct[:, tt, None, :].broadcast_to([128, 16, 32])
        sr = stb[:, tt, None, :].broadcast_to([128, 16, 32])
        qe, qo = s4[:, :, :, 0], s4[:, :, :, 1]
        a1 = rp1.rearrange("p (h i) -> p h i", i=32)
        a2 = rp2.rearrange("p (h i) -> p h i", i=32)
        P.emit("vector", lambda v, o=a1, x=qe, c=cr: v.tensor_mul(o, x, c))
        P.emit("vector", lambda v, o=a2, x=qo, c=sr: v.tensor_mul(o, x, c))
        P.fence("vector")
        P.emit("vector", lambda v, o=r4[:, :, :, 0], x=a1, y=a2: v.tensor_sub(o, x, y))
        P.emit("vector", lambda v, o=a1, x=qe, c=sr: v.tensor_mul(o, x, c))
        P.emit("vector", lambda v, o=a2, x=qo, c=cr: v.tensor_mul(o, x, c))
        P.fence("vector")
        P.emit("vector", lambda v, o=r4[:, :, :, 1], x=a1, y=a2: v.tensor_add(o, x, y))
        P.fence("vector")

    for tt in range(NT):
        buf = tt % 2
        if tt >= 2:
            P.fence("vector")
            mv = P.mark("vector")
            P.wait("sync", *mv)
        ld = P.dma("sync", x1t[:, buf, :], io["x1"][tt * 128:(tt + 1) * 128, :])
        P.wait("vector", P.sem("d_sync"), ld)
        emit_ln(P, x1t[:, buf, :], xnb, tmp, stats, sq=sq, g=g)
        P.fence("vector")
        mv = P.mark("vector")
        P.wait("tensor", *mv)
        for kk in range(8):
            P.wait("tensor", P.sem("c_b"))
            P.emit("tensor", lambda t, o=psT[:, kk % 2, 0:128], x=xnb[:, kk * 128:(kk + 1) * 128],
                   i=g["ident_b"]: t.transpose(o, x, i), inc=(P.sem("c_a"), 1))
            P.wait("vector", P.sem("c_a"))
            P.emit("vector", lambda v, o=xnf[:, kk, :], x=psT[:, kk % 2, 0:128]:
                   v.tensor_copy(o, x), inc=(P.sem("c_b"), 1))
        P.fence("vector")
        mv = P.mark("vector")
        P.wait("sync", *mv)
        P.wait("tensor", *mv)
        P.dma("sync", io["xn1_fm"].rearrange("(k p) n -> p k n", p=128)[:, :, tt * 128:(tt + 1) * 128],
              xnf)
        for wn, ps in (("Wq", psQ), ("Wk", psK)):
            for ng in range(2):
                mmgroup(P, ps[ng * 512:(ng + 1) * 512] if False else ps[:, ng * 512:(ng + 1) * 512],
                        [(xnf[:, k, :], wts[wn][:, k, ng * 512:(ng + 1) * 512]) for k in range(8)],
                        inc=(P.sem("c_a"), 1) if ng == 1 else None)
        P.wait("vector", P.sem("c_a"))
        P.emit("vector", lambda v, o=qt, x=psQ, b=bb["bq"]:
               v.scalar_tensor_tensor(out=o, in0=x, scalar=0.125, in1=b, op0=ALU.mult, op1=ALU.add))
        P.emit("vector", lambda v, o=kt, x=psK, b=bb["bk"]:
               v.scalar_tensor_tensor(out=o, in0=x, scalar=1.0, in1=b, op0=ALU.mult, op1=ALU.add),
               inc=(P.sem("c_b"), 1))
        P.wait("tensor", P.sem("c_b"))
        for ng in range(2):
            mmgroup(P, psQ[:, ng * 512:(ng + 1) * 512],
                    [(xnf[:, k, :], wts["Wv"][:, k, ng * 512:(ng + 1) * 512]) for k in range(8)],
                    inc=(P.sem("c_a"), 1) if ng == 1 else None)
        P.wait("vector", P.sem("c_a"))
        P.emit("vector", lambda v, o=vbf, x=psQ, b=bb["bv"]: v.tensor_add(o, x, b), inc=(P.sem("c_b"), 1))
        P.fence("vector")
        mv = P.mark("vector")
        P.wait("sync", *mv)
        P.dma("sync", io["v1"][tt * 128:(tt + 1) * 128, :].rearrange("p (h c) -> p h c", c=65)[:, :, 0:64],
              vbf.rearrange("p (h c) -> p h c", c=64))
        for src_f32, dstio in ((qt, "qf1"), (kt, "kf1")):
            rope(src_f32, tt)
            mv = P.mark("vector")
            P.wait("tensor", *mv)
            for kk in range(8):
                P.wait("tensor", P.sem("c_b"))
                P.emit("tensor", lambda t, o=psT[:, kk % 2, 0:128], x=rot[:, kk * 128:(kk + 1) * 128],
                       i=g["ident_b"]: t.transpose(o, x, i), inc=(P.sem("c_a"), 1))
                P.wait("vector", P.sem("c_a"))
                P.emit("vector", lambda v, o=qkf[:, kk, :], x=psT[:, kk % 2, 0:128]:
                       v.tensor_copy(o, x), inc=(P.sem("c_b"), 1))
            P.fence("vector")
            mv = P.mark("vector")
            P.wait("sync", *mv)
            P.dma("sync", io[dstio].rearrange("(k p) n -> p k n", p=128)[:, :, tt * 128:(tt + 1) * 128],
                  qkf)
            P.wait("vector", P.sem("d_sync"))
    P.barrier()


def emit_p9(P, g, io, es):
    nc = P.nc
    import contextlib as _ctx
    es2 = es.enter_context(_ctx.ExitStack())
    sb = lambda n, s, d=F32: es.enter_context(nc.sbuf_tensor(n, s, d))[:]
    pst2 = lambda n, s, d=F32: es2.enter_context(nc.psum_tensor(n, s, d))[:]
    v_res = sb("p9_v", [128, NT, 1040], BF16)
    att = sb("p9_att", [128, NT, 1024], BF16)
    q2h = sb("p9_q2h", [128, 2, T], BF16)
    k2h = sb("p9_k2h", [128, 2, T], BF16)
    ET = sb("p9_ET", [128, 4, 512], BF16)
    s65 = sb("p9_s65", [65, 512], F32)
    rc = sb("p9_rc", [128, 4], F32)
    wpb = sb("p9_wpb", [128, 8, 1024], BF16)
    wstage = sb("p9_wstage", [128, 4, 1024], F32)
    s1bf = sb("p9_s1bf", [128, 1024], BF16)
    afm = sb("p9_afm", [128, 8, 128], BF16)
    psc = pst2("p9_psc", [128, 4, 512], F32)
    ps65 = pst2("p9_ps65", [65, 512], F32)
    psT4 = pst2("p9_psT4", [128, 2, 512], F32)

    for half in range(2):
        ld = P.dma("sync", wstage, io["Wp"].rearrange("(k p) n -> p k n", p=128)[:, half * 4:(half + 1) * 4, :])
        P.wait("vector", P.sem("d_sync"), ld)
        P.emit("vector", lambda v, o=wpb[:, half * 4:(half + 1) * 4, :], x=wstage: v.tensor_copy(o, x))
        mv = P.mark("vector")
        P.wait("sync", *mv)
    P.dma("sync", v_res, io["v1"].rearrange("(j p) d -> p j d", p=128))
    P.wait("gpsimd", P.sem("d_sync"))
    P.emit("gpsimd", lambda gp, o=v_res.rearrange("p j (h c) -> p j h c", c=65)[:, :, :, 64:65]:
           gp.memset(o, 1.0), inc=(P.sem("c_d"), 1))
    P.wait("tensor", P.sem("c_d"))
    P.barrier()

    for hg in range(8):
        buf = hg % 2
        if hg >= 2:
            P.wait("sync", P.sem("c_a"))
        P.dma("sync", q2h[:, buf, :], io["qf1"][hg * 128:(hg + 1) * 128, :])
        ld = P.dma("sync", k2h[:, buf, :], io["kf1"][hg * 128:(hg + 1) * 128, :])
        P.wait("tensor", P.sem("d_sync"), ld)
        for h2 in range(2):
            hb = 64 * h2
            hh = hg * 2 + h2
            for ib in range(8):
                i0 = ib * 512
                njt = min(NT, 4 * ib + 5)
                pairs = [tuple(x for x in (a, a + 1) if x < njt) for a in range(0, njt, 2)]
                cb_base = P.sem("c_b").n
                cc_base = P.sem("c_c").n
                npv = 0
                for jp, jts in enumerate(pairs):
                    bp = jp % 2
                    P.wait("tensor", P.sem("c_b"), cb_base + jp - 1 if jp >= 2 else cb_base)
                    for ji, jt in enumerate(jts):
                        P.emit("tensor", lambda t, o=psc[:, 2 * bp + ji, :],
                               l=k2h[:, buf, :][hb:hb + 64, jt * 128:(jt + 1) * 128],
                               r=q2h[:, buf, :][hb:hb + 64, i0:i0 + 512]:
                               t.matmul(o, l, r, start=True, stop=True),
                               inc=(P.sem("c_a"), 1) if ji == len(jts) - 1 else None)
                    P.wait("scalar", P.sem("c_a"))
                    P.wait("scalar", P.sem("c_c"),
                           cc_base + sum(len(p) for p in pairs[:jp - 1]) if jp >= 2 else cc_base)
                    nj = len(jts)
                    P.emit("scalar", lambda s, o=ET[:, 2 * bp:2 * bp + nj, :].rearrange("p a b -> p (a b)"),
                           x=psc[:, 2 * bp:2 * bp + nj, :].rearrange("p a b -> p (a b)"):
                           s.activation(o, x, AF.Exp), inc=(P.sem("c_b"), 1))
                    for ji, jt in enumerate(jts):
                        masked = jt >= 4 * ib
                        if masked:
                            P.wait("gpsimd", P.sem("c_b"))
                            P.emit("gpsimd", lambda gp, o=ET[:, 2 * bp + ji, :], b=i0 + 64 - 128 * jt:
                                   gp.affine_select(out=o, in_=o, compare_op=ALU.is_ge, fill=0.0,
                                                    base=b, pattern=[[1, 512]], channel_multiplier=-1),
                                   inc=(P.sem("c_d"), 1))
                            P.wait("tensor", P.sem("c_d"))
                        else:
                            P.wait("tensor", P.sem("c_b"))
                        P.emit("tensor", lambda t, o=ps65, l=v_res[:, jt, 65 * hh:65 * hh + 65],
                               r=ET[:, 2 * bp + ji, :], st=(jt == 0), sp=(jt == njt - 1):
                               t.matmul(o, l, r, start=st, stop=sp),
                               inc=(P.sem("c_c"), 1))
                        npv += 1
                P.wait("vector", P.sem("c_c"))
                P.emit("vector", lambda v, o=s65, x=ps65: v.tensor_copy(o, x), inc=(P.sem("c_e"), 1))
                P.wait("tensor", P.sem("c_e"))
                cf_base = P.sem("c_f").n
                for qq in range(4):
                    P.wait("tensor", P.sem("c_f"), cf_base + qq - 1 if qq >= 2 else cf_base)
                    P.emit("tensor", lambda t, o=psT4[:, qq % 2, 0:65], x=s65[:, qq * 128:(qq + 1) * 128],
                           i=g["ident_f"][0:65, 0:65]: t.transpose(o, x, i), inc=(P.sem("c_a"), 1))
                    P.wait("vector", P.sem("c_a"))
                    P.emit("vector", lambda v, o=rc[:, qq:qq + 1], x=psT4[:, qq % 2, 64:65]:
                           v.reciprocal(o, x))
                    P.fence("vector")
                    tt = ib * 4 + qq
                    P.emit("vector", lambda v, o=att[:, tt, 64 * hh:64 * hh + 64], x=psT4[:, qq % 2, 0:64],
                           r=rc[:, qq:qq + 1]: v.tensor_scalar(out=o, in0=x, scalar1=r, scalar2=None,
                                                               op0=ALU.mult),
                           inc=(P.sem("c_f"), 1))
    P.barrier()
    es2.close()
    pst = lambda n, s, d=F32: es.enter_context(nc.psum_tensor(n, s, d))[:]
    psP = pst("p9_psP", [128, 2, 1024], F32)
    psTb = pst("p9_psTb", [128, 2, 1024], BF16)
    for tt in range(NT):
        cb0 = P.sem("c_b").n
        for kk in range(8):
            P.wait("tensor", P.sem("c_b"), cb0 + kk - 1 if kk >= 2 else cb0)
            P.emit("tensor", lambda t, o=psTb[:, kk % 2, 0:128], x=att[:, tt, kk * 128:(kk + 1) * 128],
                   i=g["ident_b"]: t.transpose(o, x, i), inc=(P.sem("c_a"), 1))
            P.wait("vector", P.sem("c_a"))
            P.emit("vector", lambda v, o=afm[:, kk, :], x=psTb[:, kk % 2, 0:128]:
                   v.tensor_copy(o, x), inc=(P.sem("c_b"), 1))
        P.fence("vector")
        mv = P.mark("vector")
        P.wait("tensor", *mv)
        if tt >= 2:
            P.wait("tensor", P.sem("c_f"))
        for ng in range(2):
            mmgroup(P, psP[:, tt % 2, ng * 512:(ng + 1) * 512],
                    [(afm[:, k, :], wpb[:, k, ng * 512:(ng + 1) * 512]) for k in range(8)],
                    inc=(P.sem("c_a"), 1) if ng == 1 else None)
        P.wait("vector", P.sem("c_a"))
        P.wait("vector", P.sem("d_sync"))
        P.emit("vector", lambda v, o=s1bf, x=psP[:, tt % 2, :]: v.tensor_copy(o, x), inc=(P.sem("c_f"), 1))
        mv = P.mark("vector")
        P.wait("sync", *mv)
        P.dma("sync", io["s1"][tt * 128:(tt + 1) * 128, :], s1bf)
    P.barrier()


def emit_p11(P, g, io, es):
    nc = P.nc
    import contextlib as _ctx
    es2 = es.enter_context(_ctx.ExitStack())
    sb = lambda n, s, d=F32: es2.enter_context(nc.sbuf_tensor(n, s, d))[:]
    pst = lambda n, s, d=F32: es2.enter_context(nc.psum_tensor(n, s, d))[:]
    xnf = sb("p11_xnf", [128, 8, T], BF16)
    gwb = sb("p11_gwb", [128, 8, 2048], BF16)
    uwb = sb("p11_uwb", [128, 8, 2048], BF16)
    dwb = sb("p11_dwb", [128, 16, 1024], BF16)
    wstage = sb("p11_wstage", [128, 4096], F32)
    hfm = sb("p11_hfm", [128, 16, 512], BF16)
    gsil = sb("p11_gsil", [128, 2, 512], BF16)
    dsb = sb("p11_dsb", [128, 2, 1024], F32)
    rw_s = sb("p11_rws", [128, 8, 12], BF16)
    rw_f = sb("p11_rwf", [128, 8, 12], F32)
    s12 = sb("p11_s12", [128, 12], F32)
    oh4 = sb("p11_oh4", [128, 4], F32)
    u4 = sb("p11_u4", [128, 4], F32)
    w8 = sb("p11_w8", [128, NT, 8], F32)
    st1 = sb("p11_st1", [128, 2], F32)
    psA = pst("p11_psA", [128, 4, 512], F32)
    psD = pst("p11_psD", [128, 2, 1024], F32)
    psR = psA[:, 0, :]

    P.dma("sync", xnf, io["xn1_fm"].rearrange("(k p) n -> p k n", p=128))
    P.dma("sync", rw_f, io["rw"].rearrange("(k p) n -> p k n", p=128))
    P.wait("vector", P.sem("d_sync"))
    P.emit("vector", lambda v, o=rw_s, x=rw_f: v.tensor_copy(o, x))
    P.barrier()
    for tt in range(NT):
        P.wait("tensor", P.sem("c_b"))
        mmgroup(P, psR[:, 0:12],
                [(xnf[:, k, tt * 128:(tt + 1) * 128], rw_s[:, k, :]) for k in range(8)],
                inc=(P.sem("c_a"), 1))
        P.wait("scalar", P.sem("c_a"))
        P.emit("scalar", lambda s, o=s12, x=psR[:, 0:12]: s.activation(o, x, AF.Sigmoid),
               inc=(P.sem("c_b"), 1))
        P.wait("vector", P.sem("c_b"))
        P.emit("vector", lambda v, o=st1, x=s12: v.tensor_reduce(o[:, 0:1], x[:, 0:4], AX.X, ALU.max))
        P.fence("vector")
        P.emit("vector", lambda v, o=oh4, x=s12, r=st1: v.tensor_scalar(out=o, in0=x[:, 0:4],
                                                                        scalar1=r[:, 0:1], scalar2=None,
                                                                        op0=ALU.is_ge))
        P.emit("vector", lambda v, o=u4, x=s12: v.tensor_add(o, x.rearrange("p (g e) -> p g e", e=2)[:, 4 // 2:, 0]
                                                             if False else x[:, 4:12:2], x[:, 5:12:2]))
        P.fence("vector")
        P.emit("vector", lambda v, o=u4: v.tensor_scalar(out=o, in0=o, scalar1=1e-9, scalar2=None,
                                                         op0=ALU.add))
        P.fence("vector")
        P.emit("vector", lambda v, o=u4: v.reciprocal(o, o))
        P.fence("vector")
        P.emit("vector", lambda v, o=u4, a=oh4: v.tensor_mul(o, o, a))
        P.fence("vector")
        P.emit("vector", lambda v, o=w8.rearrange("p t (g e) -> p t g e", e=2)[:, tt, :, :],
               x=s12.rearrange("p (g e) -> p g e", e=2)[:, 2:6, :], f=u4[:, :, None].broadcast_to([128, 4, 2]):
               v.tensor_mul(o, x, f))
        P.fence("vector")
    P.barrier()

    NTB = 8
    dgp_tile = {}
    for e in range(8):
        for wn, dst in (("gate_w", gwb), ("up_w", uwb)):
            for qtr in range(4):
                P.fence("vector")
                mv = P.mark("vector")
                P.wait("sync", *mv)
                ld = P.dma("sync", wstage.rearrange("p (a n) -> p a n", n=2048),
                           io[wn][e].rearrange("(k p) n -> p k n", p=128)[:, qtr * 2:(qtr + 1) * 2, :])
                P.wait("vector", P.sem("d_sync"), ld)
                P.emit("vector", lambda v, o=dst[:, qtr * 2:(qtr + 1) * 2, :],
                       x=wstage.rearrange("p (a n) -> p a n", n=2048): v.tensor_copy(o, x))
        for qtr in range(4):
            P.fence("vector")
            mv = P.mark("vector")
            P.wait("sync", *mv)
            ld = P.dma("sync", wstage.rearrange("p (a n) -> p a n", n=1024),
                       io["down_w"][e].rearrange("(k p) n -> p k n", p=128)[:, qtr * 4:(qtr + 1) * 4, :])
            P.wait("vector", P.sem("d_sync"), ld)
            P.emit("vector", lambda v, o=dwb[:, qtr * 4:(qtr + 1) * 4, :],
                   x=wstage.rearrange("p (a n) -> p a n", n=1024): v.tensor_copy(o, x))
        P.fence("vector")
        mv = P.mark("vector")
        P.wait("tensor", *mv)
        for tb in range(NTB):
            t0 = tb * 512
            cb0 = P.sem("c_b").n
            cc0 = P.sem("c_c").n
            for mc in range(16):
                pb = mc % 2
                if P11_SERIAL:
                    P.wait("tensor", P.sem("c_b"))
                    P.wait("tensor", P.sem("c_c"))
                P.wait("tensor", P.sem("c_b"), cb0 + mc - 1 if mc >= 2 else cb0)
                mmgroup(P, psA[:, pb, :],
                        [(gwb[:, k, mc * 128:(mc + 1) * 128], xnf[:, k, t0:t0 + 512]) for k in range(8)],
                        inc=(P.sem("c_a"), 1))
                P.wait("scalar", P.sem("c_a"))
                P.wait("scalar", P.sem("c_c"), cc0 + mc - 1 if mc >= 2 else cc0)
                P.emit("scalar", lambda s, o=gsil[:, pb, :], x=psA[:, pb, :]: s.activation(o, x, AF.Silu),
                       inc=(P.sem("c_b"), 1))
                P.wait("tensor", P.sem("c_c"), cc0 + mc - 1 if mc >= 2 else cc0)
                mmgroup(P, psA[:, 2 + pb, :],
                        [(uwb[:, k, mc * 128:(mc + 1) * 128], xnf[:, k, t0:t0 + 512]) for k in range(8)],
                        inc=(P.sem("c_a"), 1))
                P.wait("vector", P.sem("c_a"))
                P.wait("vector", P.sem("c_b"))
                P.emit("vector", lambda v, o=hfm[:, mc, :], a=gsil[:, pb, :], b=psA[:, 2 + pb, :]:
                       v.tensor_mul(o, a, b), inc=(P.sem("c_c"), 1))
            ce0 = P.sem("c_e").n
            P.wait("tensor", P.sem("c_c"))
            for ts8 in range(4):
                pb = ts8 % 2
                tt = tb * 4 + ts8
                P.wait("tensor", P.sem("c_e"), ce0 + ts8 - 1 if ts8 >= 2 else ce0)
                for ngd in range(2):
                    mmgroup(P, psD[:, pb, ngd * 512:(ngd + 1) * 512],
                            [(hfm[:, kf, ts8 * 128:(ts8 + 1) * 128], dwb[:, kf, ngd * 512:(ngd + 1) * 512])
                             for kf in range(16)],
                            inc=(P.sem("c_a"), 1) if ngd == 1 else None)
                P.wait("vector", P.sem("c_a"))
                P.wait("vector", P.sem("d_gp"))
                P.emit("vector", lambda v, o=dsb[:, pb, :], x=psD[:, pb, :], w=w8[:, tt, e:e + 1]:
                       v.tensor_scalar(out=o, in0=x, scalar1=w, scalar2=None, op0=ALU.mult),
                       inc=(P.sem("c_e"), 1))
                mv = P.mark("vector")
                P.wait("gpsimd", *mv)
                if e > 0:
                    P.wait("gpsimd", P.sem("d_gp"), dgp_tile[tt])
                if e == 0:
                    P.dma("gpsimd", io["moe_acc"][tt * 128:(tt + 1) * 128, :], dsb[:, pb, :])
                else:
                    P.dma("gpsimd", io["moe_acc"][tt * 128:(tt + 1) * 128, :], dsb[:, pb, :],
                          accum_op=ALU.add)
                dgp_tile[tt] = P.sem("d_gp").n
    P.barrier()
    es2.close()
    sb = lambda n, s, d=F32: es.enter_context(nc.sbuf_tensor(n, s, d))[:]
    macc = sb("p11_macc", [128, 2, 1024], F32)
    x1t = sb("p11_x1t", [128, 2, 1024], F32)
    s1t = sb("p11_s1t", [128, 2, 1024], BF16)
    g1 = sb("p11_g1", [128, 1024], F32)
    tmpo = sb("p11_tmpo", [128, 1024], F32)
    for tt in range(NT):
        buf = tt % 2
        if tt >= 2:
            P.fence("vector")
            mv = P.mark("vector")
            P.wait("sync", *mv)
        P.dma("sync", macc[:, buf, :], io["moe_acc"][tt * 128:(tt + 1) * 128, :])
        P.dma("sync", x1t[:, buf, :], io["x1"][tt * 128:(tt + 1) * 128, :])
        ld = P.dma("sync", s1t[:, buf, :], io["s1"][tt * 128:(tt + 1) * 128, :])
        P.wait("scalar", P.sem("d_sync"), ld)
        P.emit("scalar", lambda s, o=g1, x=macc[:, buf, :]: s.activation(o, x, AF.Silu),
               inc=(P.sem("c_b"), 1))
        P.wait("vector", P.sem("c_b"))
        P.emit("vector", lambda v, o=tmpo, a=s1t[:, buf, :], b=g1: v.tensor_mul(o, a, b))
        P.fence("vector")
        P.emit("vector", lambda v, o=tmpo, a=x1t[:, buf, :], b=tmpo: v.tensor_add(o, a, b))
        P.fence("vector")
        mv = P.mark("vector")
        P.wait("sync", *mv)
        P.dma("sync", io["out"][tt * 128:(tt + 1) * 128, :], tmpo)
        P.wait("vector", P.sem("d_sync"))
        P.wait("scalar", P.sem("d_sync"))
    P.barrier()



SCRATCH = {
    "qf0": ([D, T], BF16), "kf0": ([D, T], BF16), "v0": ([T, D], BF16),
    "xnr_tm": ([T, D], BF16), "xnr_fm": ([D, T], BF16),
    "s0pre_fm": ([D, T], BF16),
    "s0": ([T, D], BF16),
    "se_fm": ([D, T], BF16),
    "lT": ([T, T], F32),
    "dbg_ltm": ([T, T], F32), "dbg_g0": ([T, D], BF16), "dbg_st": ([T, 4], F32), "dbg_E": ([T, T], BF16),
    "maxC": ([1, 1], F32),
    "slots": ([T, D], BF16),
    "processed": ([T, D], BF16),
    "x1": ([T, D], F32),
    "xn1_fm": ([D, T], BF16),
    "qf1": ([D, T], BF16), "kf1": ([D, T], BF16), "v1": ([T, 1040], BF16),
    "s1": ([T, D], BF16),
    "moe_acc": ([T, D], F32),
    "out": ([T, D], F32),
}

INPUTS = {
    "x0": ([T, D], F32),
    "W1q": ([D, D], F32), "W2q": ([D, D], F32), "W1k": ([D, D], F32), "W2k": ([D, D], F32),
    "W1v": ([D, D], F32), "W2v": ([D, D], F32),
    "b1q": ([D], F32), "b2q": ([D], F32), "b1k": ([D], F32), "b2k": ([D], F32),
    "b1v": ([D], F32), "b2v": ([D], F32),
    "ln_w": ([D], F32), "ln_b": ([D], F32),
    "gamma_x": ([D], F32), "gamma_slot": ([D], F32),
    "slot_embeds": ([T, D], F32),
    "w1": ([D, 4096], F32), "w2": ([4096, D], F32),
    "Wq": ([D, D], F32), "Wk": ([D, D], F32), "Wv": ([D, D], F32), "Wp": ([D, D], F32),
    "bq": ([D], F32), "bk": ([D], F32), "bv": ([D], F32),
    "ctab": ([T, 32], F32), "stab": ([T, 32], F32),
    "rw": ([D, 12], F32),
    "gate_w": ([8, D, 2048], F32), "up_w": ([8, D, 2048], F32), "down_w": ([8, 2048, D], F32),
}

PHASES = {
    "p1": (emit_p1, ["x0", "W1q", "W2q", "W1k", "W2k", "W1v", "W2v", "b1q", "b2q", "b1k",
                     "b2k", "b1v", "b2v", "gamma_x"],
           ["qf0", "kf0", "v0", "xnr_tm", "xnr_fm"]),
    "p2": (emit_p2, ["qf0", "kf0", "v0"], ["s0pre_fm"]),
    "p2b": (emit_p2b, ["s0pre_fm", "ln_w", "ln_b"], ["s0"]),
    "p3": (emit_p3, ["slot_embeds", "gamma_slot"], ["se_fm"]),
    "p5": (emit_p5, ["se_fm", "xnr_fm", "xnr_tm"], ["lT", "slots", "maxC"]),
    "p6": (emit_p6, ["slots", "w1", "w2"], ["processed"]),
    "p7": (emit_p7, ["lT", "processed", "x0", "s0"], ["x1"]),
    "p8": (emit_p8, ["x1", "Wq", "Wk", "Wv", "bq", "bk", "bv", "ctab", "stab"],
           ["qf1", "kf1", "v1", "xn1_fm"]),
    "p9": (emit_p9, ["qf1", "kf1", "v1", "Wp"], ["s1"]),
    "p11": (emit_p11, ["xn1_fm", "rw", "gate_w", "up_w", "down_w", "x1", "s1"],
            ["moe_acc", "out"]),
}


def build_launch(phase_names, needed_later=None):
    nc = bass.Bass("TRN2", target_bir_lowering=False)
    P = Prog(nc)
    needed_in, needed_out = set(), set()
    for pn in phase_names:
        _, ins, outs = PHASES[pn]
        for n in ins:
            if n not in needed_out:
                needed_in.add(n)
        needed_out.update(outs)
    if needed_later is None:
        ext_out = set(needed_out)
    else:
        ext_out = needed_out & needed_later
    io = {}
    for n in sorted(needed_in):
        shape, dt = INPUTS[n] if n in INPUTS else SCRATCH[n]
        io[n] = nc.dram_tensor("in_" + n, list(shape), dt, kind="ExternalInput").ap()
    for n in sorted(needed_out):
        shape, dt = SCRATCH[n]
        kind = "ExternalOutput" if n in ext_out else "Internal"
        io[n] = nc.dram_tensor(("out_" if kind == "ExternalOutput" else "scr_") + n,
                               list(shape), dt, kind=kind).ap()
    with contextlib.ExitStack() as es:
        for s in P.sems.values():
            s.h = es.enter_context(nc.semaphore(s.name))
        g = alloc_common(P, es)
        for pn in phase_names:
            emitter, _, _ = PHASES[pn]
            with contextlib.ExitStack() as pes:
                emitter(P, g, io, pes)
            P.barrier()
        block = es.enter_context(nc.Block())
        P.run(block)
    in_names = {n: ("in_" + n) for n in needed_in}
    out_names = {n: ("out_" + n) for n in sorted(ext_out)}
    return nc, in_names, out_names


def prep_core_inputs(inputs, b):
    ctab, stab = host_constants()
    micro = np.ascontiguousarray(np.transpose(np.asarray(inputs["L1_ffn_micro_w"]), (1, 0, 2))
                                 ).reshape(D, 8)
    rw = np.concatenate([np.asarray(inputs["L1_ffn_macro_w"]), micro], axis=1)
    d = {
        "x0": np.ascontiguousarray(np.asarray(inputs["L0_attn_x"])[b]),
        "W1q": inputs["L0_attn_W1q"], "W2q": inputs["L0_attn_W2q"],
        "W1k": inputs["L0_attn_W1k"], "W2k": inputs["L0_attn_W2k"],
        "W1v": inputs["L0_attn_W1v"], "W2v": inputs["L0_attn_W2v"],
        "b1q": inputs["L0_attn_b1q"], "b2q": inputs["L0_attn_b2q"],
        "b1k": inputs["L0_attn_b1k"], "b2k": inputs["L0_attn_b2k"],
        "b1v": inputs["L0_attn_b1v"], "b2v": inputs["L0_attn_b2v"],
        "ln_w": inputs["L0_attn_ln_w"], "ln_b": inputs["L0_attn_ln_b"],
        "gamma_x": inputs["L0_ffn_gamma_x"], "gamma_slot": inputs["L0_ffn_gamma_slot"],
        "slot_embeds": np.asarray(inputs["L0_ffn_slot_embeds"]).reshape(T, D),
        "w1": inputs["L0_ffn_w1"], "w2": inputs["L0_ffn_w2"],
        "Wq": inputs["L1_attn_Wq"], "Wk": inputs["L1_attn_Wk"],
        "Wv": inputs["L1_attn_Wv"], "Wp": inputs["L1_attn_Wp"],
        "bq": inputs["L1_attn_bq"], "bk": inputs["L1_attn_bk"], "bv": inputs["L1_attn_bv"],
        "ctab": ctab, "stab": stab, "rw": rw,
        "gate_w": inputs["L1_ffn_gate_w"], "up_w": inputs["L1_ffn_up_w"],
        "down_w": inputs["L1_ffn_down_w"],
    }
    return {k: np.ascontiguousarray(np.asarray(v, dtype=np.float32)) for k, v in d.items()}


LAUNCH_PLAN = [["p1", "p2", "p2b", "p3", "p5", "p6", "p7", "p8", "p9", "p11"]]


def run_launches(per_core, launch_plan=None):
    plan = launch_plan if launch_plan is not None else LAUNCH_PLAN
    for li, phase_names in enumerate(plan):
        later = set(["out"])
        for pns in plan[li + 1:]:
            for pn in pns:
                later.update(PHASES[pn][1])
        nc, in_names, out_names = build_launch(phase_names, needed_later=later)
        in_maps = []
        for b in range(8):
            m = {}
            for n, tn in in_names.items():
                m[tn] = per_core[b][n]
            in_maps.append(m)
        res = run_bass_kernel_spmd(nc, in_maps, core_ids=list(range(8)))
        for b in range(8):
            for n, tn in out_names.items():
                per_core[b][n] = np.asarray(res.results[b][tn])
    return per_core


def kernel(**inputs):
    per_core = [prep_core_inputs(inputs, b) for b in range(8)]
    run_launches(per_core)
    out = np.stack([per_core[b]["out"] for b in range(8)], axis=0)
    return out.astype(np.float32)
